# version 21
# baseline (speedup 1.0000x reference)
import contextlib
import numpy as np

import concourse.bass as bass
import concourse.mybir as mybir
from concourse.bass_utils import run_bass_kernel_spmd

F32 = mybir.dt.float32
F32R = mybir.dt.float32r
BF16 = mybir.dt.bfloat16
AF = mybir.ActivationFunctionType
ALU = mybir.AluOpType

L = 2048
DM = 1024
DI = 2048
NST = 16
RNK = 64
LC = 512
NCH = L // LC
NB = DI // 128
NMB = DM // 128
KCD = DM // 128

ENGS = ("sync", "tensor", "scalar", "vector", "gpsimd")

STOP = None


class Prog:
    def __init__(self):
        self.ops = {e: [] for e in ENGS}

    def add(self, eng, fn):
        self.ops[eng].append(fn)


class Sem:
    def __init__(self, stack, nc, name):
        self.h = stack.enter_context(nc.semaphore(name))
        self.n = 0


def build_nc():
    nc = bass.Bass("TRN2", target_bir_lowering=False)
    P = Prog()
    stack = contextlib.ExitStack()
    dram = {}

    def din(name, shape, dt=F32):
        dram[name] = nc.dram_tensor(name, list(shape), dt,
                                    kind="ExternalInput").ap()
        return dram[name]

    def dscr(name, shape, dt):
        dram[name] = nc.dram_tensor(name, list(shape), dt, kind="Internal").ap()
        return dram[name]

    xin = din("xin", (L, DM))
    out_d = nc.dram_tensor("out", [L, DM], F32, kind="ExternalOutput").ap()

    for d in ("f", "b"):
        din(f"inproj_{d}", (DM, 2 * DI))
        din(f"xproj_{d}", (DI, RNK + 2 * NST))
        din(f"dtw_{d}", (RNK, DI))
        din(f"outproj_{d}", (DI, DM))
        din(f"convw_{d}", (128, NB * 4))
        din(f"convb_{d}", (128, NB))
        din(f"dtb_{d}", (128, NB))
        din(f"dpcol_{d}", (128, NB))
        din(f"alogc_{d}", (128, NB * NST))
    din("gate_w", (2 * DM, DM))
    din("gate_bc", (128, NMB))
    din("proj_w", (DM, DM))
    din("proj_bc", (128, NMB))
    din("wfr", (DM, DM), F32R)
    din("wfi", (DM, DM), F32R)
    din("fbias_c", (128, NMB))
    din("fr", (DM, DM), F32R)
    din("fi", (DM, DM), F32R)
    din("fr2", (DM, DM), F32R)
    din("fi2", (DM, DM), F32R)
    din("ones_col", (128, 1), F32)
    din("ones_row", (1, 128), F32)
    din("esel", (16, 16 * 128), BF16)
    din("eye", (128, 128))

    inpbf = {d: dscr(f"inpbf_{d}", (2 * NB, KCD, 128, 128), BF16)
             for d in ("f", "b")}
    outbf = {d: dscr(f"outbf_{d}", (DI, DM), BF16) for d in ("f", "b")}
    mscr = dscr("mscr", (DM, DM), F32R)
    yscr = {d: dscr(f"yscr_{d}", (NMB, 128, L), BF16) for d in ("f", "b")}

    s_in = Sem(stack, nc, "s_in")
    s_ws = Sem(stack, nc, "s_ws")
    s_st = Sem(stack, nc, "s_st")
    s_te = Sem(stack, nc, "s_te")
    s_act = Sem(stack, nc, "s_act")
    s_dve = Sem(stack, nc, "s_dve")
    s_dmp = Sem(stack, nc, "s_dmp")
    s_wsA = Sem(stack, nc, "s_wsA")
    s_wsB = Sem(stack, nc, "s_wsB")
    s_gp = Sem(stack, nc, "s_gp")
    s_w4 = [Sem(stack, nc, f"s_w4{i}") for i in range(4)]
    s_ldC = Sem(stack, nc, "s_ldC")
    s_ldD = Sem(stack, nc, "s_ldD")
    s_stC = Sem(stack, nc, "s_stC")

    def wait(eng, sem, val=None):
        v = sem.n if val is None else val
        if v > 0:
            P.add(eng, lambda e, h=sem.h, v=v: e.wait_ge(h, v))

    def run(eng, fn, inc=None):
        if inc is None:
            P.add(eng, fn)
            return None
        sem, k = inc
        sem.n += k
        P.add(eng, lambda e, fn=fn, h=sem.h, k=k: fn(e).then_inc(h, k))
        return sem.n

    def dma(eng, out, in_, sem):
        return run(eng, lambda e, o=out, i=in_: e.dma_start(out=o, in_=i),
                   inc=(sem, 16))

    _names = [0]

    def sbt(shape, dt, scope=None):
        _names[0] += 1
        return (scope or stack).enter_context(
            nc.sbuf_tensor(f"t{_names[0]}", list(shape), dt))

    pA = stack.enter_context(nc.psum_tensor("pA", [128, 2048], F32))
    pB = stack.enter_context(nc.psum_tensor("pB", [128, 2048], F32))

    def bank(i):
        p = pA if i < 4 else pB
        return p[:, (i % 4) * 512:(i % 4) * 512 + 512]

    bank_tok = [None] * 8

    def wait_bank(eng, i):
        t = bank_tok[i]
        if t is not None:
            wait(eng, t[0], t[1])

    out_wide = out_d.rearrange("(a p) m -> p a m", p=128)

    def dump(src_ap, width):
        wait("gpsimd", s_dve)
        wait("gpsimd", s_act)
        wait("gpsimd", s_te)
        for i in range(width // 1024):
            dma("gpsimd", out_wide[:, i, :],
                src_ap[:, i * 1024:(i + 1) * 1024], s_dmp)
        wait("sync", s_dmp)

    cols = {}
    for d in ("f", "b"):
        for nm, w in (("convw", NB * 4), ("convb", NB), ("dtb", NB),
                      ("dpcol", NB), ("alogc", NB * NST)):
            cols[f"{nm}_{d}"] = sbt((128, w), F32)
    for nm in ("gate_bc", "proj_bc", "fbias_c"):
        cols[nm] = sbt((128, NMB), F32)
    onec_sb = sbt((128, 1), F32)
    oner_sb = sbt((1, 128), F32)
    esel_sb = sbt((16, 16 * 128), BF16)
    eye_sb = sbt((128, 128), F32)
    acol = sbt((128, NB * NST), F32)
    carry = sbt((128, NB * NST), F32)
    stash = [sbt((128, NB * 4), BF16) for _ in range(2)]
    junk = sbt((1, 8), F32)
    epsc = sbt((128, 1), F32)

    for d in ("f", "b"):
        for nm in ("convw", "convb", "dtb", "dpcol", "alogc"):
            dma("sync", cols[f"{nm}_{d}"][:], dram[f"{nm}_{d}"][:, :], s_in)
    for nm in ("gate_bc", "proj_bc", "fbias_c"):
        dma("sync", cols[nm][:], dram[nm][:, :], s_in)
    dma("sync", onec_sb[:], dram["ones_col"][:, :], s_in)
    dma("sync", esel_sb[:], dram["esel"][:, :], s_in)
    dma("sync", oner_sb[:], dram["ones_row"][:, :], s_in)
    dma("sync", eye_sb[:], dram["eye"][:, :], s_in)
    wait("scalar", s_in)
    run("vector", lambda e: e.memset(epsc[:], 1e-5), inc=(s_dve, 1))

    ph1 = stack.enter_context(contextlib.ExitStack())
    if True:
        stgf = [sbt((128, 2048), F32, ph1) for _ in range(2)]
        stgb = [sbt((128, 2048), BF16, ph1) for _ in range(2)]
        slabs = []

        def cast_stream(src_row_fn, dst_ap_fn, nslab, colw, src3d_fn=None):
            for kc in range(nslab):
                slabs.append((src_row_fn(kc), dst_ap_fn(kc), colw,
                              src3d_fn(kc) if src3d_fn is not None else None))

        for d in ("f", "b"):
            cast_stream(
                lambda kc, d=d: dram[f"inproj_{d}"][
                    (kc // 2) * 128:(kc // 2 + 1) * 128,
                    (kc % 2) * 2048:(kc % 2 + 1) * 2048],
                lambda kc, d=d: inpbf[d][
                    (kc % 2) * 16:(kc % 2) * 16 + 16, kc // 2, :, :].rearrange(
                    "a p m -> p a m"),
                2 * KCD, 2048,
                src3d_fn=lambda kc: True)
            cast_stream(
                lambda kc, d=d: dram[f"outproj_{d}"][kc * 128:(kc + 1) * 128, :],
                lambda kc, d=d: outbf[d][kc * 128:(kc + 1) * 128, :],
                NB, DM)

        ld_tok = {}
        cast_tok = {}
        st_tok = {}
        for i, (src_ap, dst_ap, colw, src3d) in enumerate(slabs):
            par = i % 2
            if i - 2 in cast_tok:
                wait("scalar", s_act, cast_tok[i - 2])
                wait("scalar", s_stC, st_tok[i - 2])
            lsem = s_ldC if par == 0 else s_ldD
            dma("scalar", stgf[par][:, 0:colw], src_ap, lsem)
            ld_tok[i] = (lsem, lsem.n)
            if i - 1 in ld_tok and i >= 1:
                pass
            if i >= 1:
                j = i - 1
                pv = j % 2
                pcolw, psrc3d, pdst = slabs[j][2], slabs[j][3], slabs[j][1]
                wait("scalar", ld_tok[j][0], ld_tok[j][1])
                run("scalar", lambda e, cw=pcolw, b=stgb[pv], f=stgf[pv]:
                    e.activation(out=b[:, 0:cw], in_=f[:, 0:cw],
                                 func=AF.Copy), inc=(s_act, 1))
                cast_tok[j] = s_act.n
                wait("scalar", s_act, cast_tok[j])
                sap = (stgb[pv][:, 0:pcolw].rearrange(
                    "p (a m) -> p a m", m=128) if psrc3d
                    else stgb[pv][:, 0:pcolw])
                dma("scalar", pdst, sap, s_stC)
                st_tok[j] = s_stC.n
        j = len(slabs) - 1
        pv = j % 2
        pcolw, psrc3d, pdst = slabs[j][2], slabs[j][3], slabs[j][1]
        wait("scalar", ld_tok[j][0], ld_tok[j][1])
        run("scalar", lambda e, cw=pcolw, b=stgb[pv], f=stgf[pv]:
            e.activation(out=b[:, 0:cw], in_=f[:, 0:cw],
                         func=AF.Copy), inc=(s_act, 1))
        wait("scalar", s_act, s_act.n)
        sap = (stgb[pv][:, 0:pcolw].rearrange(
            "p (a m) -> p a m", m=128) if psrc3d
            else stgb[pv][:, 0:pcolw])
        dma("scalar", pdst, sap, s_stC)
        ph0_done = s_stC.n
        del slabs, ld_tok, cast_tok, st_tok

    if True:
        wr_sb = sbt((128, KCD * DM), F32R, ph1)
        wi_sb = sbt((128, KCD * DM), F32R, ph1)
        trt = sbt((128, NMB * DM), F32R, ph1)
        tit = sbt((128, NMB * DM), F32R, ph1)
        fstg = [sbt((128, 512), F32R, ph1) for _ in range(4)]
        mstg = [sbt((128, 512), F32R, ph1) for _ in range(2)]
        ntmp = sbt((128, KCD * DM), F32, ph1)

        wait("sync", s_in)
        dma("sync", wr_sb[:].rearrange("p (a o) -> p a o", o=DM),
            dram["wfr"].rearrange("(a p) o -> p a o", p=128), s_in)
        dma("sync", wi_sb[:].rearrange("p (a o) -> p a o", o=DM),
            dram["wfi"].rearrange("(a p) o -> p a o", p=128), s_in)
        w_ready = s_in.n
        wait("sync", s_in, w_ready)

        fsrc = []
        for _ in range(2):
            for dh in range(2):
                for kc in range(KCD):
                    fsrc.append((dram["fr"][kc * 128:(kc + 1) * 128,
                                            dh * 512:(dh + 1) * 512],
                                 dram["fi"][kc * 128:(kc + 1) * 128,
                                            dh * 512:(dh + 1) * 512]))
        for th in range(2):
            for ob in range(KCD):
                fsrc.append((dram["fr2"][ob * 128:(ob + 1) * 128,
                                         th * 512:(th + 1) * 512],
                             dram["fi2"][ob * 128:(ob + 1) * 128,
                                         th * 512:(th + 1) * 512]))
        fdma_tok = {}
        fte_tok = {}
        fstep = [0]

        def f_issue(i):
            if i >= len(fsrc):
                return
            par = i % 2
            if i - 2 >= 0:
                wait("sync", s_te, fte_tok[i - 2])
            dma("sync", fstg[2 * par][:], fsrc[i][0], s_w4[par])
            dma("sync", fstg[2 * par + 1][:], fsrc[i][1], s_w4[par])
            fdma_tok[i] = s_w4[par].n

        f_issue(0)
        f_issue(1)

        def t_pass(tdst, w0, w1):
            for dh in range(2):
                for kc in range(KCD):
                    i = fstep[0]
                    par = i % 2
                    wait("tensor", s_w4[par], fdma_tok[i])
                    if kc == 0:
                        wait("tensor", s_in, w_ready)
                        for ob in range(8):
                            wait_bank("tensor", ob)
                    for ob in range(KCD):
                        run("tensor", lambda e, b=bank(ob),
                            lh=w0[:, kc * DM + ob * 128:
                                  kc * DM + (ob + 1) * 128],
                            rh=fstg[2 * par][:], st=(kc == 0):
                            e.matmul(b, lh, rh, start=st, stop=False))
                        run("tensor", lambda e, b=bank(ob),
                            lh=w1[:, kc * DM + ob * 128:
                                  kc * DM + (ob + 1) * 128],
                            rh=fstg[2 * par + 1][:], sp=(kc == KCD - 1):
                            e.matmul(b, lh, rh, start=False, stop=sp),
                            inc=(s_te, 1) if ob == KCD - 1 else None)
                    fte_tok[i] = s_te.n
                    f_issue(i + 2)
                    fstep[0] += 1
                wait("vector", s_te)
                for ob in range(KCD):
                    run("vector", lambda e,
                        o=tdst[:, ob * DM + dh * 512:
                               ob * DM + (dh + 1) * 512],
                        i=bank(ob):
                        e.tensor_copy(out=o, in_=i),
                        inc=(s_dve, 1) if ob == KCD - 1 else None)
                for i in range(8):
                    bank_tok[i] = (s_dve, s_dve.n)

        t_pass(tit, wi_sb, wr_sb)
        wait("vector", s_te)
        run("vector", lambda e:
            e.tensor_scalar(out=ntmp[:], in0=wi_sb[:], scalar1=-1.0,
                            scalar2=None, op0=ALU.mult))
        run("vector", lambda e: e.drain())
        run("vector", lambda e:
            e.tensor_copy(out=wi_sb[:], in_=ntmp[:]), inc=(s_dve, 1))
        wi_neg_tok = s_dve.n
        wait("tensor", s_dve, wi_neg_tok)
        t_pass(trt, wr_sb, wi_sb)

        for th in range(2):
            for ob in range(KCD):
                i = fstep[0]
                par = i % 2
                wait("tensor", s_w4[par], fdma_tok[i])
                if ob == 0:
                    for j in range(8):
                        wait_bank("tensor", j)
                for db in range(KCD):
                    run("tensor", lambda e, b=bank(db),
                        lh=trt[:, ob * DM + db * 128: ob * DM + (db + 1) * 128],
                        rh=fstg[2 * par][:], st=(ob == 0):
                        e.matmul(b, lh, rh, start=st, stop=False))
                    run("tensor", lambda e, b=bank(db),
                        lh=tit[:, ob * DM + db * 128: ob * DM + (db + 1) * 128],
                        rh=fstg[2 * par + 1][:], sp=(ob == KCD - 1):
                        e.matmul(b, lh, rh, start=False, stop=sp),
                        inc=(s_te, 1) if db == KCD - 1 else None)
                fte_tok[i] = s_te.n
                f_issue(i + 2)
                fstep[0] += 1
            wait("vector", s_te)
            for db in range(KCD):
                wait("vector", s_st)
                run("vector", lambda e, o=mstg[db % 2][:], i=bank(db):
                    e.tensor_copy(out=o, in_=i), inc=(s_dve, 1))
                wait("sync", s_dve)
                dma("sync",
                    mscr[db * 128:(db + 1) * 128, th * 512:(th + 1) * 512],
                    mstg[db % 2][:], s_st)
            for i in range(8):
                bank_tok[i] = (s_dve, s_dve.n)
        wait("sync", s_st)
        wait("sync", s_stC, ph0_done)
        wait("sync", s_act)
        wait("sync", s_te)
        wait("sync", s_dve)
    ph1.close()

    ph2 = stack.enter_context(contextlib.ExitStack())
    ystg = sbt((128, NMB * 512), BF16, ph2)
    xTc = sbt((128, KCD * LC), BF16, ph2)
    scr2k = sbt((128, 2048), F32, ph2)
    xstg = scr2k[:, 0:DM]
    ubq = sbt((128, NB * LC), BF16, ph2)
    zbuf = sbt((128, NB * LC), BF16, ph2)
    dT = sbt((128, NB * LC), F32, ph2)
    dub = sbt((128, NB * LC), BF16, ph2)
    dtbf = sbt((64, LC), BF16, ph2)
    dblB = sbt((16, LC), BF16, ph2)
    dblC = sbt((16, LC), BF16, ph2)
    inpstg = [sbt((128, KCD * 128), BF16, ph2) for _ in range(3)]
    outstg = [sbt((128, DM), BF16, ph2) for _ in range(2)]

    def emit_xpose(c, xstg, dst):
        for lt in range(4):
            wait("sync", s_te)
            dma("sync", xstg,
                xin[c * LC + lt * 128: c * LC + (lt + 1) * 128, :], s_in)
            wait("tensor", s_in)
            wait("tensor", s_dve)
            wait("tensor", s_act)
            for mb in range(NMB):
                run("tensor", lambda e, o=bank(mb)[:, 0:128],
                    i=xstg[:, mb * 128:(mb + 1) * 128]:
                    e.transpose(o, i, eye_sb[:]),
                    inc=(s_te, 1) if mb == NMB - 1 else None)
            wait("vector", s_te)
            for mb in range(NMB):
                run("vector", lambda e,
                    o=dst[:, mb * LC + lt * 128: mb * LC + lt * 128 + 128],
                    i=bank(mb)[:, 0:128]:
                    e.tensor_copy(out=o, in_=i),
                    inc=(s_dve, 1) if mb == NMB - 1 else None)
        for i in range(8):
            bank_tok[i] = (s_dve, s_dve.n)

    last_q = [0]

    def mamba_dir(d, bufs):
        rev = (d == "b")
        xprojbf, dtwbf_, ucbuf, dscope = bufs
        with contextlib.ExitStack() as wp:
            wstg_ = sbt((128, 2048), F32, wp)
            wait("sync", s_dve)
            wait("sync", s_act)
            dma("sync", wstg_[:, 0:NB * 96].rearrange("p (a j) -> p a j", j=96),
                dram[f"xproj_{d}"].rearrange("(a p) j -> p a j", p=128), s_in)
            wait("vector", s_in)
            run("vector", lambda e:
                e.tensor_copy(out=xprojbf[:], in_=wstg_[:, 0:NB * 96]),
                inc=(s_dve, 1))
            wait("sync", s_dve)
            dma("sync", wstg_[0:64, 0:DI], dram[f"dtw_{d}"][:, :], s_in)
            wait("vector", s_in)
            run("vector", lambda e:
                e.tensor_copy(out=dtwbf_[:], in_=wstg_[0:64, 0:DI]),
                inc=(s_dve, 1))
            run("vector", lambda e: e.memset(carry[:], 0.0), inc=(s_dve, 1))
            run("vector", lambda e: e.memset(stash[0][:], 0.0),
                inc=(s_dve, 1))
            run("vector", lambda e: e.memset(stash[1][:], 0.0),
                inc=(s_dve, 1))
            wait("scalar", s_dve)
            run("scalar", lambda e, i=cols[f"alogc_{d}"][:]:
                e.activation(out=acol[:], in_=i, func=AF.Exp), inc=(s_act, 1))
            wait("vector", s_act)
            run("vector", lambda e:
                e.tensor_scalar(out=acol[:], in0=acol[:], scalar1=-1.0,
                                scalar2=None, op0=ALU.mult), inc=(s_dve, 1))
            wait("sync", s_dve)

        chunks = list(range(NCH - 1, -1, -1)) if rev else list(range(NCH))
        NSL = 32 * NCH
        inp_tok = {}
        inp_te = {}

        def inp_issue(k):
            if k >= NSL:
                return
            par = k % 3
            db = k % 32
            if k - 3 >= 0:
                wait("sync", s_te, inp_te[k - 3])
            dma("sync", inpstg[par][:].rearrange("p (a m) -> p a m", m=128),
                inpbf[d][db, :, :, :].rearrange("a p m -> p a m"),
                s_w4[par])
            inp_tok[k] = s_w4[par].n

        wait("sync", s_te)
        for k in range(3):
            inp_issue(k)

        tok = {"xp_te": 0, "xp_done": 0, "xp_in": 0, "uc_done": {},
               "scr_act": 0, "scr_dve": 0, "q_ready": last_q[0]}

        def front_bundles(ci):
            c = chunks[ci]
            bs = []
            for lt in range(4):
                def bxp(lt=lt, c=c):
                    if lt == 0:
                        wait("sync", s_act, tok["scr_act"])
                        wait("sync", s_dve, tok["scr_dve"])
                    else:
                        wait("sync", s_te, tok["xp_te"])
                    dma("sync", xstg,
                        xin[c * LC + lt * 128: c * LC + (lt + 1) * 128, :],
                        s_in)
                    xin_tok = s_in.n
                    wait("tensor", s_in, xin_tok)
                    for mb in range(NMB):
                        wait_bank("tensor", mb)
                    for mb in range(NMB):
                        run("tensor", lambda e, o=bank(mb)[:, 0:128],
                            i=xstg[:, mb * 128:(mb + 1) * 128]:
                            e.transpose(o, i, eye_sb[:]),
                            inc=(s_te, 1) if mb == NMB - 1 else None)
                    tok["xp_te"] = s_te.n
                    wait("scalar", s_te, tok["xp_te"])
                    for mb in range(NMB):
                        run("scalar", lambda e,
                            o=xTc[:, mb * LC + lt * 128:
                                  mb * LC + lt * 128 + 128],
                            i=bank(mb)[:, 0:128]:
                            e.activation(out=o, in_=i, func=AF.Copy),
                            inc=(s_act, 1) if mb == NMB - 1 else None)
                    for mb in range(NMB):
                        bank_tok[mb] = (s_act, s_act.n)
                    if lt == 3:
                        tok["xp_done"] = s_act.n
                        tok["xp_in"] = xin_tok
                bs.append(bxp)
            for db in range(NB):
                def bu(db=db, ci=ci):
                    k = ci * 32 + db
                    par = k % 3
                    bi = db % 8
                    wait("tensor", s_w4[par], inp_tok[k])
                    if db == 0:
                        wait("tensor", s_act, tok["xp_done"])
                    wait_bank("tensor", bi)
                    for kc in range(KCD):
                        run("tensor", lambda e, b=bank(bi),
                            lh=inpstg[par][:, kc * 128:(kc + 1) * 128],
                            rh=xTc[:, kc * LC:(kc + 1) * LC],
                            st=(kc == 0), sp=(kc == KCD - 1):
                            e.matmul(b, lh, rh, start=st, stop=sp),
                            inc=(s_te, 1) if kc == KCD - 1 else None)
                    inp_te[k] = s_te.n
                    inp_issue(k + 3)
                    wait("scalar", s_te, inp_te[k])
                    if db == 0:
                        wait("scalar", s_dve, tok["q_ready"])
                    run("scalar", lambda e,
                        o=ucbuf[:, db * LC:(db + 1) * LC], i=bank(bi):
                        e.activation(out=o, in_=i, func=AF.Copy),
                        inc=(s_act, 1))
                    bank_tok[bi] = (s_act, s_act.n)
                    if db == NB - 1:
                        tok["uc_done"][ci] = s_act.n
                bs.append(bu)
            return bs

        def z_quartet(ci, g):
            for db in range(NB + 4 * g, NB + 4 * g + 4):
                k = ci * 32 + db
                par = k % 3
                bi = db % 8
                wait("tensor", s_w4[par], inp_tok[k])
                wait_bank("tensor", bi)
                for kc in range(KCD):
                    run("tensor", lambda e, b=bank(bi),
                        lh=inpstg[par][:, kc * 128:(kc + 1) * 128],
                        rh=xTc[:, kc * LC:(kc + 1) * LC],
                        st=(kc == 0), sp=(kc == KCD - 1):
                        e.matmul(b, lh, rh, start=st, stop=sp),
                        inc=(s_te, 1) if kc == KCD - 1 else None)
                inp_te[k] = s_te.n
                inp_issue(k + 3)
                zdb = db - NB
                wait("scalar", s_te, inp_te[k])
                run("scalar", lambda e, o=zbuf[:, zdb * LC:(zdb + 1) * LC],
                    i=bank(bi): e.activation(out=o, in_=i, func=AF.Silu),
                    inc=(s_act, 1))
                bank_tok[bi] = (s_act, s_act.n)

        def emit_conv_z(ci):
            scur = stash[ci % 2]
            snxt = stash[(ci + 1) % 2]
            wcol = cols[f"convw_{d}"]
            bcol = cols[f"convb_{d}"]

            def conv_ops(db, j):
                u_sl = ucbuf[:, db * LC:(db + 1) * LC]
                acc = scr2k[:, j * 512:(j + 1) * 512]
                ops = [lambda e, o=acc, i=u_sl,
                       s1=wcol[:, db * 4 + 3: db * 4 + 4],
                       s2=bcol[:, db: db + 1]:
                       e.tensor_scalar(out=o, in0=i, scalar1=s1,
                                       scalar2=s2, op0=ALU.mult,
                                       op1=ALU.add)]
                for k in (2, 1, 0):
                    m = 3 - k
                    if not rev:
                        ops.append(lambda e, o=acc[:, m:512],
                                   i=u_sl[:, 0:512 - m],
                                   s1=wcol[:, db * 4 + k: db * 4 + k + 1],
                                   a=acc[:, m:512]:
                                   e.scalar_tensor_tensor(
                                       out=o, in0=i, scalar=s1, in1=a,
                                       op0=ALU.mult, op1=ALU.add))
                        ops.append(lambda e, o=acc[:, 0:m],
                                   i=scur[:, db * 4 + 3 - m: db * 4 + 3],
                                   s1=wcol[:, db * 4 + k: db * 4 + k + 1],
                                   a=acc[:, 0:m]:
                                   e.scalar_tensor_tensor(
                                       out=o, in0=i, scalar=s1, in1=a,
                                       op0=ALU.mult, op1=ALU.add))
                    else:
                        ops.append(lambda e, o=acc[:, 0:512 - m],
                                   i=u_sl[:, m:512],
                                   s1=wcol[:, db * 4 + k: db * 4 + k + 1],
                                   a=acc[:, 0:512 - m]:
                                   e.scalar_tensor_tensor(
                                       out=o, in0=i, scalar=s1, in1=a,
                                       op0=ALU.mult, op1=ALU.add))
                        ops.append(lambda e, o=acc[:, 512 - m:512],
                                   i=scur[:, db * 4: db * 4 + m],
                                   s1=wcol[:, db * 4 + k: db * 4 + k + 1],
                                   a=acc[:, 512 - m:512]:
                                   e.scalar_tensor_tensor(
                                       out=o, in0=i, scalar=s1, in1=a,
                                       op0=ALU.mult, op1=ALU.add))
                return ops

            wait("scalar", s_dve)
            wait("vector", s_act, tok["uc_done"][ci])
            wait("vector", s_te, tok["xp_te"])
            wait("vector", s_act, tok["scr_act"])
            wait("vector", s_in, tok["xp_in"])
            if not rev:
                run("vector", lambda e,
                    o=snxt[:].rearrange("p (a k) -> p a k", k=4)[:, :, 0:3],
                    i=ucbuf[:].rearrange("p (a l) -> p a l",
                                         l=LC)[:, :, 509:512]:
                    e.tensor_copy(out=o, in_=i))
            else:
                run("vector", lambda e,
                    o=snxt[:].rearrange("p (a k) -> p a k", k=4)[:, :, 0:3],
                    i=ucbuf[:].rearrange("p (a l) -> p a l",
                                         l=LC)[:, :, 0:3]:
                    e.tensor_copy(out=o, in_=i))
            silu_tok = 0
            silu_toks = []
            for grp in range(4):
                if grp > 0:
                    wait("vector", s_act, silu_tok)
                for j0 in (0, 2):
                    opsA = conv_ops(grp * 4 + j0, j0)
                    opsB = conv_ops(grp * 4 + j0 + 1, j0 + 1)
                    run("vector", opsA[0])
                    run("vector", opsB[0])
                    for lev in range(3):
                        run("vector", lambda e: e.drain())
                        run("vector", opsA[1 + 2 * lev])
                        run("vector", opsA[2 + 2 * lev])
                        run("vector", opsB[1 + 2 * lev])
                        run("vector", opsB[2 + 2 * lev])
                run("vector", lambda e: e.drain(), inc=(s_dve, 1))
                wait("scalar", s_dve)
                run("scalar", lambda e,
                    o=ucbuf[:, grp * 2048:(grp + 1) * 2048], i=scr2k[:]:
                    e.activation(out=o, in_=i, func=AF.Silu), inc=(s_act, 1))
                silu_tok = s_act.n
                silu_toks.append(s_act.n)
                z_quartet(ci, grp)
            return silu_toks

        def xproj_D(ci, silu_toks):
            for i in range(3):
                wait_bank("tensor", i)
            for grp in range(4):
                wait("tensor", s_act, silu_toks[grp])
                for kc in range(grp * 4, grp * 4 + 4):
                    for (j0, w_, dst_part) in ((0, 64, pA[0:64, 0:512]),
                                               (64, 16, pA[0:16, 512:1024]),
                                               (80, 16, pA[0:16, 1024:1536])):
                        run("tensor", lambda e, b=dst_part,
                            lh=xprojbf[:, kc * 96 + j0: kc * 96 + j0 + w_],
                            rh=ucbuf[:, kc * LC:(kc + 1) * LC],
                            st=(kc == 0), sp=(kc == NB - 1):
                            e.matmul(b, lh, rh, start=st, stop=sp),
                            inc=(s_te, 1) if kc == NB - 1 and j0 == 80
                            else None)
            wait("vector", s_te)
            run("vector", lambda e: e.tensor_copy(out=dtbf[:],
                                                  in_=pA[0:64, 0:512]))
            run("vector", lambda e: e.tensor_copy(out=dblB[:],
                                                  in_=pA[0:16, 512:1024]))
            run("vector", lambda e: e.tensor_copy(out=dblC[:],
                                                  in_=pA[0:16, 1024:1536]),
                inc=(s_dve, 1))
            for i in range(3):
                bank_tok[i] = (s_dve, s_dve.n)
            return s_dve.n

        def dtproj_E(ci):
            dw = dtwbf_
            dtb = cols[f"dtb_{d}"]
            wait("tensor", s_dve)
            for grp in range(4):
                bb = 4 - 4 * (grp % 2)
                pp = pB if grp % 2 == 0 else pA
                for j in range(4):
                    db = grp * 4 + j
                    wait_bank("tensor", bb + j)
                    run("tensor", lambda e, b=pp[:, j * 512:(j + 1) * 512],
                        lh=dw[:, db * 128:(db + 1) * 128], rh=dtbf[:]:
                        e.matmul(b, lh, rh, start=True, stop=True),
                        inc=(s_te, 1) if j == 3 else None)
                wait("scalar", s_te)
                wait("scalar", s_dve)
                run("scalar", lambda e: e.drain())
                for j in range(4):
                    db = grp * 4 + j
                    bk = pp[:, j * 512:(j + 1) * 512]
                    run("scalar", lambda e, o=scr2k[:, j * 512:(j + 1) * 512],
                        i=bk, b=dtb[:, db:db + 1]:
                        e.activation(out=o, in_=i, func=AF.Abs, bias=b))
                    run("scalar", lambda e, o=dT[:, db * LC:(db + 1) * LC],
                        i=bk, b=dtb[:, db:db + 1]:
                        e.activation(out=o, in_=i, func=AF.Relu, bias=b),
                        inc=(s_act, 1) if j == 3 else None)
                for j in range(4):
                    bank_tok[bb + j] = (s_act, s_act.n)
                run("scalar", lambda e: e.drain())
                run("scalar", lambda e: e.activation(
                    out=scr2k[:], in_=scr2k[:], func=AF.Exp, scale=-1.0))
                run("scalar", lambda e: e.drain())
                run("scalar", lambda e: e.activation(
                    out=scr2k[:], in_=scr2k[:], func=AF.Ln, bias=1.0),
                    inc=(s_act, 1))
                wait("vector", s_act)
                run("vector", lambda e,
                    o=dT[:, grp * 2048:(grp + 1) * 2048], b=scr2k[:]:
                    e.tensor_tensor(out=o, in0=o, in1=b, op=ALU.add),
                    inc=(s_dve, 1))
            tok["scr_act"] = s_act.n
            tok["scr_dve"] = s_dve.n

        def qinit_F(ci):
            run("vector", lambda e: e.drain())
            dpc = cols[f"dpcol_{d}"]
            for db in range(NB):
                run("vector", lambda e, o=ubq[:, db * LC:(db + 1) * LC],
                    i=ucbuf[:, db * LC:(db + 1) * LC], s1=dpc[:, db:db + 1]:
                    e.tensor_scalar(out=o, in0=i, scalar1=s1, scalar2=None,
                                    op0=ALU.mult))
            run("vector", lambda e: e.tensor_tensor(
                out=dub[:], in0=dT[:], in1=ucbuf[:], op=ALU.mult),
                inc=(s_dve, 1))
            tok["q_ready"] = s_dve.n
            last_q[0] = s_dve.n
            return s_dve.n

        def y_evac(c):
            wait("scalar", s_te)
            wait("scalar", s_st)
            for mb in range(NMB):
                run("scalar", lambda e,
                    o=ystg[:, mb * 512:(mb + 1) * 512],
                    i=bank(mb):
                    e.activation(out=o, in_=i, func=AF.Copy),
                    inc=(s_act, 1) if mb == NMB - 1 else None)
            for i in range(8):
                bank_tok[i] = (s_act, s_act.n)
            wait("sync", s_act)
            dma("sync",
                yscr[d].rearrange("a p l -> p a l")[:, :, c * LC:(c + 1) * LC],
                ystg[:].rearrange("p (a l) -> p a l", l=512), s_st)

        for b in front_bundles(0):
            b()
        prev_c = None
        for ci, c in enumerate(chunks):
            if prev_c is not None:
                y_evac(prev_c)
            silu_toks = emit_conv_z(ci)
            if STOP == "uc":
                dump(ucbuf[:], NB * LC)
                return True
            dbl_tok = xproj_D(ci, silu_toks)
            dtproj_E(ci)
            if STOP == "dT":
                dump(dT[:], NB * LC)
                return True
            q_ready = qinit_F(ci)
            fillers = front_bundles(ci + 1) if ci + 1 < NCH else []
            yield_scan(d, rev, c, q_ready, dbl_tok, fillers)
            if STOP == "hq":
                dump(ubq[:], NB * LC)
                return True
            prev_c = c
        y_evac(prev_c)
        return False

    def yield_scan(d, rev, c, q_ready, dbl_tok, fillers):
        ac = acol
        scan_tok = {0: 0, 1: 0}
        act_tok = {0: 0, 1: 0}
        sm_tok = {0: 0}
        te_tok = {}
        run("scalar", lambda e: e.drain())
        grp_dve_end = {}

        def emit_pats(g):
            gp = g % 2
            if g - 2 in grp_dve_end:
                wait("tensor", s_dve, grp_dve_end[g - 2])
            else:
                wait("tensor", s_dve)
            wait("tensor", s_act)
            for i in range(8):
                wait_bank("tensor", i)
            for j in range(4):
                n = g * 4 + j
                run("tensor", lambda e, b=pA[:, j * 512:(j + 1) * 512],
                    lh=esel_sb[:, n * 128:(n + 1) * 128], rh=dblB[:]:
                    e.matmul(b, lh, rh, start=True, stop=True))
                run("tensor", lambda e, b=pB[:, j * 512:(j + 1) * 512],
                    lh=esel_sb[:, n * 128:(n + 1) * 128], rh=dblC[:]:
                    e.matmul(b, lh, rh, start=True, stop=True),
                    inc=(s_te, 1) if j == 3 else None)
            wait("scalar", s_te)
            run("scalar", lambda e, o=bpat[gp][:]: e.activation(
                out=o, in_=pA[:], func=AF.Copy))
            run("scalar", lambda e, o=cpat[gp][:]: e.activation(
                out=o, in_=pB[:], func=AF.Copy), inc=(s_act, 1))
            for i in range(8):
                bank_tok[i] = (s_act, s_act.n)
            return ("act", s_act.n)

        pat_tok_next = emit_pats(0)
        for g in range(4):
            gp = g % 2
            pat_tok = pat_tok_next
            bpat_g = bpat[gp]
            cpat_g = cpat[gp]
            def emit_A(db):
                h = hg4[db % 2]
                sw = swide[db % 2]
                items = []
                smfn = (lambda e,
                        i0=dub[:, db * LC:(db + 1) * LC]
                        .rearrange("p (o l) -> p o l", o=1)
                        .broadcast_to((128, 4, LC)),
                        i1=bpat_g[:].rearrange("p (o l) -> p o l", l=LC),
                        o=sw[:].rearrange("p (o l) -> p o l", l=LC):
                        e.tensor_tensor(out=o, in0=i0, in1=i1, op=ALU.mult))
                items.append((None, smfn,
                              ("sm", db % 2, pat_tok if db == 0 else 0,
                               q_ready if (g == 0 and db == 0) else 0)))
                for j in range(4):
                    n = g * 4 + j
                    buf = j % 2
                    actfn = (lambda e, o=dab[buf][:],
                             i=dT[:, db * LC:(db + 1) * LC],
                             s=ac[:, db * NST + n: db * NST + n + 1]:
                             e.activation(out=o, in_=i, func=AF.Exp, scale=s))
                    items.append((None, actfn, ("act", buf)))
                    if not rev:
                        scfn = (lambda e, o=h[:, j * 512:(j + 1) * 512],
                                d0=dab[buf][:],
                                d1=sw[:, j * 512:(j + 1) * 512],
                                i0=carry[:, db * NST + n: db * NST + n + 1]:
                                e.tensor_tensor_scan(out=o, data0=d0, data1=d1,
                                                     initial=i0, op0=ALU.mult,
                                                     op1=ALU.add))
                    else:
                        scfn = (lambda e,
                                o=h[:, j * 512:(j + 1) * 512][:, ::-1],
                                d0=dab[buf][:, ::-1],
                                d1=sw[:, j * 512:(j + 1) * 512][:, ::-1],
                                i0=carry[:, db * NST + n: db * NST + n + 1]:
                                e.tensor_tensor_scan(out=o, data0=d0, data1=d1,
                                                     initial=i0, op0=ALU.mult,
                                                     op1=ALU.add))
                    items.append((None, scfn, ("scan", buf, db % 2, j == 0)))
                return items

            def emit_T(db):
                h = hg4[db % 2]
                cstart = 0 if rev else LC - 1
                ops = []
                ops.append((("dve", lambda: max(scan_tok[1],
                                                tloc.get("qa", 0))),
                            lambda e, hh=h, cp=cpat_g: e.tensor_tensor(
                                out=tmp2k[:], in0=hh[:], in1=cp[:],
                                op=ALU.mult), ("tok", "cm")))
                ops.append((None, lambda e,
                            o=carry[:, db * NST + g * 4: db * NST + g * 4 + 4],
                            i=h[:].rearrange("p (a l) -> p a l", l=LC)
                            [:, :, cstart:cstart + 1]
                            .rearrange("p a k -> p (a k)"):
                            e.tensor_copy(out=o, in_=i), None))
                ops.append((("dve", lambda: tloc["cm"]),
                            lambda e: e.tensor_tensor(
                                out=tmp2k[:, 0:1024], in0=tmp2k[:, 0:1024],
                                in1=tmp2k[:, 1024:2048], op=ALU.add),
                            ("tok", "a1")))
                ops.append((("dve", lambda: tloc["a1"]),
                            lambda e: e.tensor_tensor(
                                out=tmp2k[:, 0:512], in0=tmp2k[:, 0:512],
                                in1=tmp2k[:, 512:1024], op=ALU.add),
                            ("tok", "a2")))
                ops.append((("dve", lambda: tloc["a2"]),
                            lambda e, o=ubq[:, db * LC:(db + 1) * LC]:
                            e.tensor_tensor(out=o, in0=o, in1=tmp2k[:, 0:512],
                                            op=ALU.add), ("tok", "qa")))
                if g == 3:
                    par = db % 2
                    ops.append((("te2", db - 2 if db >= 2 else None,
                                 "dve", lambda: tloc["qa"]),
                                lambda e, o=y2b[par][:],
                                i0=ubq[:, db * LC:(db + 1) * LC],
                                i1=zbuf[:, db * LC:(db + 1) * LC]:
                                e.tensor_tensor(out=o, in0=i0, in1=i1,
                                                op=ALU.mult), ("y2", db)))
                return ops

            tloc = {}

            def flush(items):
                for item in items:
                    pre, fn, tag = item
                    if tag is not None and tag[0] == "act":
                        buf = tag[1]
                        wait("scalar", s_dve, scan_tok[buf])
                        run("scalar", fn, inc=(s_act, 1))
                        act_tok[buf] = s_act.n
                        continue
                    if tag is not None and tag[0] == "sm":
                        par, ptok, qtok = tag[1], tag[2], tag[3]
                        if ptok:
                            wait("vector", s_act, ptok[1])
                        run("vector", fn, inc=(s_dve, 1))
                        sm_tok[0] = s_dve.n
                        continue
                    if pre is not None:
                        if pre[0] == "dve":
                            wait("vector", s_dve, pre[1]())
                        elif pre[0] == "te2":
                            if pre[1] is not None:
                                wait("vector", s_te, te_tok[pre[1]])
                            wait("vector", s_dve, pre[3]())
                    if tag is not None and tag[0] == "scan":
                        buf, par, first = tag[1], tag[2], tag[3]
                        wait("vector", s_act, act_tok[buf])
                        if first:
                            wait("vector", s_dve, sm_tok[0])
                        run("vector", fn, inc=(s_dve, 1))
                        scan_tok[buf] = s_dve.n
                    elif tag is not None and tag[0] == "tok":
                        run("vector", fn, inc=(s_dve, 1))
                        tloc[tag[1]] = s_dve.n
                    elif tag is not None and tag[0] == "y2":
                        run("vector", fn, inc=(s_dve, 1))
                        emit_outproj(tag[1], s_dve.n)
                    else:
                        run("vector", fn)

            def interleave(a_items, t_items):
                out = []
                na, nt = len(a_items), len(t_items)
                ai = ti = 0
                while ai < na or ti < nt:
                    if ai < na:
                        out.append(a_items[ai]); ai += 1
                        if ai < na:
                            out.append(a_items[ai]); ai += 1
                    if ti < nt:
                        out.append(t_items[ti]); ti += 1
                return out

            def emit_outproj(db, y2_tok):
                par = db % 2
                psem = s_wsA if par == 0 else s_wsB
                if db == 0:
                    wait("sync", s_te)
                    dma("sync", outstg[0][:], outbf[d][0:128, :], s_wsA)
                    for i in range(8):
                        wait_bank("tensor", i)
                wait("tensor", psem)
                wait("tensor", s_dve, y2_tok)
                for mb in range(NMB):
                    run("tensor", lambda e, b=bank(mb),
                        lh=outstg[par][:, mb * 128:(mb + 1) * 128],
                        rh=y2b[par][:], st=(db == 0), sp=(db == NB - 1):
                        e.matmul(b, lh, rh, start=st, stop=sp),
                        inc=(s_te, 1) if mb == NMB - 1 else None)
                te_tok[db] = s_te.n
                if db < NB - 1:
                    if db >= 1:
                        wait("sync", s_te, te_tok[db - 1])
                    dma("sync", outstg[(db + 1) % 2][:],
                        outbf[d][(db + 1) * 128:(db + 2) * 128, :],
                        s_wsB if (db + 1) % 2 else s_wsA)

            prevT = None
            for db in range(NB):
                a_items = emit_A(db)
                if prevT is None:
                    flush(a_items)
                else:
                    flush(interleave(a_items, prevT))
                prevT = emit_T(db)
                if db == 1 and g < 3:
                    pat_tok_next = emit_pats(g + 1)
                if g < 3 and fillers:
                    fillers.pop(0)()
            flush(prevT)
            grp_dve_end[g] = s_dve.n

    bpat = [sbt((128, 4 * LC), BF16, ph2) for _ in range(2)]
    cpat = [sbt((128, 4 * LC), BF16, ph2) for _ in range(2)]
    hg4 = [sbt((128, 4 * LC), BF16, ph2) for _ in range(2)]
    dab = [sbt((128, LC), F32, ph2) for _ in range(2)]
    swide = [sbt((128, 4 * LC), BF16, ph2) for _ in range(2)]
    tmp2k = sbt((128, 2048), BF16, ph2)
    y2b = [sbt((128, LC), BF16, ph2) for _ in range(2)]

    stopped = False
    for d in ("f", "b"):
        with contextlib.ExitStack() as dscope:
            xprojbf_ = sbt((128, NB * 96), BF16, dscope)
            dtwbf_ = sbt((64, NB * 128), BF16, dscope)
            ucbuf = sbt((128, NB * LC), BF16, dscope)
            stopped = mamba_dir(d, (xprojbf_, dtwbf_, ucbuf, dscope))
            if stopped:
                return nc, P, stack
        if STOP == f"y_{d}":
            dump(ystg[:, 0:4096], 4096)
            return nc, P, stack
    ph2.close()

    ph3 = stack.enter_context(contextlib.ExitStack())
    ystg3 = {d: sbt((128, NMB * 512), BF16, ph3) for d in ("f", "b")}
    xstg3 = sbt((128, DM), F32, ph3)
    gbuf = sbt((128, NMB * LC), BF16, ph3)
    comb = sbt((128, NMB * LC), BF16, ph3)
    tbuf = sbt((128, NMB * LC), BF16, ph3)
    xm = sbt((128, NMB * LC), F32, ph3)
    x1 = sbt((128, NMB * LC), F32, ph3)
    x1r = sbt((128, NMB * LC), F32R, ph3)
    owide = sbt((128, NMB * LC), F32, ph3)
    mstg3 = [sbt((128, DM), F32R, ph3) for _ in range(2)]
    xTf = sbt((128, NMB * LC), F32, ph3)
    r1 = sbt((128, NMB * LC), F32, ph3)
    rows = sbt((1, 1024), F32, ph3)
    ostg = sbt((128, DM), F32, ph3)
    wstg = [sbt((128, DM), BF16, ph3) for _ in range(3)]
    wstgf = [sbt((128, DM), F32, ph3) for _ in range(3)]
    wsem = [s_w4[0], s_w4[1], s_ldC]

    def ln_block(src, xm, sqw, dst, src_toks=None):
        if src_toks is None:
            wait("tensor", s_dve)
        wait_bank("tensor", 0)
        for mb in range(NMB):
            if src_toks is not None:
                wait("tensor", s_dve, src_toks[mb])
            run("tensor", lambda e, b=pA[0:1, 0:512],
                lh=onec_sb[:], rh=src[:, mb * LC:(mb + 1) * LC],
                st=(mb == 0), sp=(mb == NMB - 1):
                e.matmul(b, lh, rh, start=st, stop=sp),
                inc=(s_te, 1) if mb == NMB - 1 else None)
        wait("vector", s_te)
        run("vector", lambda e: e.tensor_scalar(
            out=rows[0:1, 0:512], in0=pA[0:1, 0:512], scalar1=1.0 / DM,
            scalar2=None, op0=ALU.mult), inc=(s_dve, 1))
        bank_tok[0] = (s_dve, s_dve.n)
        wait("tensor", s_dve)
        wait_bank("tensor", 4)
        run("tensor", lambda e: e.matmul(pB[:, 0:512], oner_sb[:],
                                         rows[0:1, 0:512], start=True,
                                         stop=True), inc=(s_te, 1))
        wait("vector", s_te)
        for mb in range(NMB):
            run("vector", lambda e, o=xm[:, mb * LC:(mb + 1) * LC],
                a=src[:, mb * LC:(mb + 1) * LC], b=pB[:, 0:512]:
                e.tensor_tensor(out=o, in0=a, in1=b, op=ALU.subtract))
        run("vector", lambda e: e.drain())
        for mb in range(NMB):
            run("vector", lambda e, o=sqw[:, mb * LC:(mb + 1) * LC],
                a=xm[:, mb * LC:(mb + 1) * LC]:
                e.tensor_tensor(out=o, in0=a, in1=a, op=ALU.mult),
                inc=(s_dve, 1) if mb == NMB - 1 else None)
        bank_tok[4] = (s_dve, s_dve.n)
        wait("tensor", s_dve)
        wait_bank("tensor", 1)
        for mb in range(NMB):
            run("tensor", lambda e, b=pA[0:1, 512:1024],
                lh=onec_sb[:], rh=sqw[:, mb * LC:(mb + 1) * LC],
                st=(mb == 0), sp=(mb == NMB - 1):
                e.matmul(b, lh, rh, start=st, stop=sp),
                inc=(s_te, 1) if mb == NMB - 1 else None)
        wait("vector", s_te)
        run("vector", lambda e: e.tensor_scalar(
            out=rows[0:1, 512:1024], in0=pA[0:1, 512:1024], scalar1=1.0 / DM,
            scalar2=None, op0=ALU.mult), inc=(s_dve, 1))
        bank_tok[1] = (s_dve, s_dve.n)
        wait("scalar", s_dve)
        run("scalar", lambda e: e.activation(out=rows[0:1, 512:1024],
                                             in_=rows[0:1, 512:1024],
                                             func=AF.Ln, bias=epsc[0:1, 0:1]))
        run("scalar", lambda e: e.drain())
        run("scalar", lambda e: e.activation(out=rows[0:1, 512:1024],
                                             in_=rows[0:1, 512:1024],
                                             func=AF.Exp, scale=-0.5),
            inc=(s_act, 1))
        wait("tensor", s_act)
        wait_bank("tensor", 5)
        run("tensor", lambda e: e.matmul(pB[:, 512:1024], oner_sb[:],
                                         rows[0:1, 512:1024], start=True,
                                         stop=True), inc=(s_te, 1))
        wait("vector", s_te)
        dst_toks = []
        for mb in range(NMB):
            run("vector", lambda e, o=dst[:, mb * LC:(mb + 1) * LC],
                a=xm[:, mb * LC:(mb + 1) * LC], b=pB[:, 512:1024]:
                e.tensor_tensor(out=o, in0=a, in1=b, op=ALU.mult),
                inc=(s_dve, 1))
            dst_toks.append(s_dve.n)
        bank_tok[5] = (s_dve, s_dve.n)
        return dst_toks

    wait("sync", s_st)
    def issue_ystg(c, dve_tok, te_tok):
        if dve_tok:
            wait("sync", s_dve, dve_tok)
        if te_tok:
            wait("sync", s_te, te_tok)
        for d in ("f", "b"):
            dma("sync", ystg3[d][:].rearrange("p (a l) -> p a l", l=512),
                yscr[d].rearrange("a p l -> p a l")[:, :, c * LC:(c + 1) * LC],
                s_ldD)
        return s_ldD.n

    ystg_tok = {}
    xp3 = {"te": 0, "dve": 0}

    def xpose_t(c):
        for lt in range(4):
            wait("sync", s_te, xp3["te"])
            dma("sync", xstg3[:],
                xin[c * LC + lt * 128: c * LC + (lt + 1) * 128, :], s_in)
            xtok = s_in.n
            wait("tensor", s_in, xtok)
            for mb in range(NMB):
                wait_bank("tensor", mb)
            for mb in range(NMB):
                run("tensor", lambda e, o=bank(mb)[:, 0:128],
                    i=xstg3[:, mb * 128:(mb + 1) * 128]:
                    e.transpose(o, i, eye_sb[:]),
                    inc=(s_te, 1) if mb == NMB - 1 else None)
            xp3["te"] = s_te.n
            wait("scalar", s_te, xp3["te"])
            for mb in range(NMB):
                run("scalar", lambda e,
                    o=xTf[:, mb * LC + lt * 128: mb * LC + lt * 128 + 128],
                    i=bank(mb)[:, 0:128]:
                    e.activation(out=o, in_=i, func=AF.Copy),
                    inc=(s_act, 1) if mb == NMB - 1 else None)
            for mb in range(NMB):
                bank_tok[mb] = (s_act, s_act.n)
            xp3["act"] = s_act.n

    wait("sync", s_te)
    wait("sync", s_dve)
    ystg_tok[0] = issue_ystg(0, 0, 0)
    xpose_t(0)
    for c in range(NCH):
        if True:
            wdma_tok = {}
            wcast_tok = {}
            wte_tok = {}

            def w_dma(kc, src_d):
                par = kc % 3
                if kc - 3 in wcast_tok:
                    wait("sync", s_act, wcast_tok[kc - 3])
                dma("sync", wstgf[par][:],
                    src_d[kc * 128:(kc + 1) * 128, :], wsem[par])
                wdma_tok[kc] = wsem[par].n

            def w_cast(kc):
                par = kc % 3
                wait("scalar", wsem[par], wdma_tok[kc])
                if kc - 3 in wte_tok:
                    wait("scalar", s_te, wte_tok[kc - 3])
                run("scalar", lambda e, o=wstg[par][:], i=wstgf[par][:]:
                    e.activation(out=o, in_=i, func=AF.Copy),
                    inc=(s_act, 1))
                wcast_tok[kc] = s_act.n

            wait("sync", s_te)
            for kc in range(3):
                w_dma(kc, dram["gate_w"])
                w_cast(kc)
            for kc in range(16):
                par = kc % 3
                wait("tensor", s_act, wcast_tok[kc])
                if kc == 0:
                    for i in range(8):
                        wait_bank("tensor", i)
                    wait("tensor", s_ldD, ystg_tok[c])
                    wait("tensor", s_in)
                ysrc = ystg3["f"] if kc < 8 else ystg3["b"]
                ycol = (kc % 8) * 512
                for mb in range(NMB):
                    run("tensor", lambda e, b=bank(mb),
                        lh=wstg[par][:, mb * 128:(mb + 1) * 128],
                        rh=ysrc[:, ycol:ycol + LC],
                        st=(kc == 0), sp=(kc == 15):
                        e.matmul(b, lh, rh, start=st, stop=sp),
                        inc=(s_te, 1) if mb == NMB - 1 else None)
                wte_tok[kc] = s_te.n
                if kc + 3 < 16:
                    w_dma(kc + 3, dram["gate_w"])
                    w_cast(kc + 3)
            gate_fin = wte_tok[15]
            wait("scalar", s_te)
            for mb in range(NMB):
                run("scalar", lambda e, o=gbuf[:, mb * LC:(mb + 1) * LC],
                    i=bank(mb), b=cols["gate_bc"][:, mb:mb + 1]:
                    e.activation(out=o, in_=i, func=AF.Sigmoid, bias=b),
                    inc=(s_act, 1) if mb == NMB - 1 else None)
            for i in range(8):
                bank_tok[i] = (s_act, s_act.n)
            wait("vector", s_act)
            for mb in range(NMB):
                yf = ystg3["f"][:, mb * 512:(mb + 1) * 512]
                yb = ystg3["b"][:, mb * 512:(mb + 1) * 512]
                run("vector", lambda e, a=yf, b=yb,
                    o=tbuf[:, mb * LC:(mb + 1) * LC]:
                    e.tensor_tensor(out=o, in0=a, in1=b, op=ALU.subtract))
            run("vector", lambda e: e.drain())
            for mb in range(NMB):
                run("vector", lambda e, g=gbuf[:, mb * LC:(mb + 1) * LC],
                    o=tbuf[:, mb * LC:(mb + 1) * LC]:
                    e.tensor_tensor(out=o, in0=g, in1=o, op=ALU.mult))
            run("vector", lambda e: e.drain())
            comb_tok = []
            for mb in range(NMB):
                yb = ystg3["b"][:, mb * 512:(mb + 1) * 512]
                run("vector", lambda e, o=comb[:, mb * LC:(mb + 1) * LC],
                    b=yb, t=tbuf[:, mb * LC:(mb + 1) * LC]:
                    e.tensor_tensor(out=o, in0=b, in1=t, op=ALU.add),
                    inc=(s_dve, 1))
                comb_tok.append(s_dve.n)
            if c + 1 < NCH:
                ystg_tok[c + 1] = issue_ystg(c + 1, comb_tok[-1], gate_fin)
            wait("vector", s_act, xp3.get("act", 0))
            for mb in range(NMB):
                run("vector", lambda e, o=r1[:, mb * LC:(mb + 1) * LC],
                    i=xTf[:, mb * LC:(mb + 1) * LC],
                    s1=cols["proj_bc"][:, mb:mb + 1]:
                    e.tensor_scalar(out=o, in0=i, scalar1=s1, scalar2=None,
                                    op0=ALU.add))
            run("vector", lambda e: e.drain(), inc=(s_dve, 1))
            r1h_tok = s_dve.n
            wdma_tok = {}
            wcast_tok = {}
            wte_tok = {}
            wait("sync", s_te)
            for kc in range(3):
                w_dma(kc, dram["proj_w"])
                w_cast(kc)
            for kc in range(8):
                par = kc % 3
                wait("tensor", s_act, wcast_tok[kc])
                wait("tensor", s_dve, comb_tok[kc])
                if kc == 0:
                    for i in range(8):
                        wait_bank("tensor", i)
                for mb in range(NMB):
                    run("tensor", lambda e, b=bank(mb),
                        lh=wstg[par][:, mb * 128:(mb + 1) * 128],
                        rh=comb[:, kc * LC:(kc + 1) * LC],
                        st=(kc == 0), sp=(kc == 7):
                        e.matmul(b, lh, rh, start=st, stop=sp),
                        inc=(s_te, 1) if mb == NMB - 1 else None)
                wte_tok[kc] = s_te.n
                if kc + 3 < 8:
                    w_dma(kc + 3, dram["proj_w"])
                    w_cast(kc + 3)
            proj_fin = wte_tok[7]
            wait("vector", s_te, proj_fin)
            r1_tok = []
            for mb in range(NMB):
                run("vector", lambda e, o=r1[:, mb * LC:(mb + 1) * LC],
                    b=bank(mb):
                    e.tensor_tensor(out=o, in0=o, in1=b, op=ALU.add),
                    inc=(s_dve, 1))
                r1_tok.append(s_dve.n)
            for i in range(8):
                bank_tok[i] = (s_dve, s_dve.n)
            if c + 1 < NCH:
                xpose_t(c + 1)
        if STOP == "r1":
            dump(r1[:], NMB * LC)
            return nc, P, stack
        if True:
            x1_toks = ln_block(r1[:], xm[:], owide[:], x1[:],
                               src_toks=r1_tok)
            x1r_tok = []
            for mb in range(NMB):
                wait("vector", s_dve, x1_toks[mb])
                run("vector", lambda e,
                    o=x1r[:, mb * LC:(mb + 1) * LC],
                    i=x1[:, mb * LC:(mb + 1) * LC]:
                    e.tensor_copy(out=o, in_=i), inc=(s_dve, 1))
                x1r_tok.append(s_dve.n)
            def m_dma(kc, tokd):
                par = kc % 2
                dma("sync", mstg3[par][:],
                    mscr[kc * 128:(kc + 1) * 128, :], s_w4[2 + par])
                tokd[kc] = s_w4[2 + par].n

            wdma_tok = {}
            wte_tok = {}
            wait("sync", s_te)
            m_dma(0, wdma_tok)
            m_dma(1, wdma_tok)
            for kc in range(8):
                par = kc % 2
                wait("tensor", s_w4[2 + par], wdma_tok[kc])
                wait("tensor", s_dve, x1r_tok[kc])
                if kc == 0:
                    for i in range(8):
                        wait_bank("tensor", i)
                for mb in range(NMB):
                    run("tensor", lambda e, b=bank(mb),
                        lh=mstg3[par][:, mb * 128:(mb + 1) * 128],
                        rh=x1r[:, kc * LC:(kc + 1) * LC],
                        st=(kc == 0), sp=(kc == 7):
                        e.matmul(b, lh, rh, start=st, stop=sp),
                        inc=(s_te, 1) if mb == NMB - 1 else None)
                wte_tok[kc] = s_te.n
                if kc + 2 < 8:
                    wait("sync", s_te, wte_tok[kc])
                    m_dma(kc + 2, wdma_tok)
            ffn_fin = wte_tok[7]
            for mb in range(NMB):
                run("vector", lambda e, o=r1[:, mb * LC:(mb + 1) * LC],
                    i=x1[:, mb * LC:(mb + 1) * LC],
                    s1=cols["fbias_c"][:, mb:mb + 1]:
                    e.tensor_scalar(out=o, in0=i, scalar1=s1, scalar2=None,
                                    op0=ALU.add))
            run("vector", lambda e: e.drain())
            wait("vector", s_te, ffn_fin)
            r2_tok = []
            for mb in range(NMB):
                run("vector", lambda e, o=r1[:, mb * LC:(mb + 1) * LC],
                    b=bank(mb):
                    e.tensor_tensor(out=o, in0=o, in1=b, op=ALU.add),
                    inc=(s_dve, 1))
                r2_tok.append(s_dve.n)
            for i in range(8):
                bank_tok[i] = (s_dve, s_dve.n)
            ow_toks = ln_block(r1[:], xm[:], x1[:], owide[:],
                               src_toks=r2_tok)
            wait("tensor", s_dve, ow_toks[-1])
            for lt in range(4):
                if lt > 0:
                    wait("tensor", s_dve)
                for i in range(8):
                    wait_bank("tensor", i)
                for mb in range(NMB):
                    run("tensor", lambda e, o=bank(mb)[:, 0:128],
                        i=owide[:, mb * LC + lt * 128:
                                mb * LC + lt * 128 + 128]:
                        e.transpose(o, i, eye_sb[:]),
                        inc=(s_te, 1) if mb == NMB - 1 else None)
                wait("vector", s_te)
                wait("vector", s_st)
                for mb in range(NMB):
                    run("vector", lambda e,
                        o=ostg[:, mb * 128:(mb + 1) * 128],
                        i=bank(mb)[:, 0:128]:
                        e.tensor_copy(out=o, in_=i),
                        inc=(s_dve, 1) if mb == NMB - 1 else None)
                for i in range(8):
                    bank_tok[i] = (s_dve, s_dve.n)
                wait("sync", s_dve)
                dma("sync",
                    out_d[c * LC + lt * 128: c * LC + (lt + 1) * 128, :],
                    ostg[:], s_st)
    ph3.close()

    P.add("sync", lambda e, h=s_st.h, v=s_st.n: e.wait_ge(h, v))
    return nc, P, stack


def _emit(nc, P):
    with nc.Block() as block:
        for eng in ENGS:
            ops = P.ops[eng]
            if not ops:
                continue

            def body(e, ops=ops):
                for fn in ops:
                    fn(e)
            getattr(block, eng)(body)


_CACHE = {}


def _build():
    if "nc" not in _CACHE:
        nc, P, stack = build_nc()
        _emit(nc, P)
        stack.close()
        _CACHE["nc"] = nc
    return _CACHE["nc"]


def _host_inputs(inputs):
    f32 = np.float32
    import ml_dtypes

    def colpack(v, width):
        return np.ascontiguousarray(np.asarray(v, f32).reshape(width, 128).T)

    D = DM
    jk = np.outer(np.arange(D), np.arange(D)) % D
    ang = (2.0 * np.pi / D) * jk
    fr = np.cos(ang).astype(f32)
    fi = (-np.sin(ang)).astype(f32)

    esel = np.zeros((16, 16 * 128), dtype=np.float32)
    for n in range(16):
        esel[n, n * 128:(n + 1) * 128] = 1.0
    esel = esel.astype(ml_dtypes.bfloat16)
    base = {
        "gate_w": np.ascontiguousarray(np.asarray(inputs["L0_attn_gate_w"], f32)),
        "gate_bc": colpack(inputs["L0_attn_gate_b"], NMB),
        "proj_w": np.ascontiguousarray(np.asarray(inputs["L0_attn_proj_w"], f32)),
        "proj_bc": colpack(inputs["L0_attn_proj_b"], NMB),
        "wfr": np.ascontiguousarray(np.asarray(inputs["L0_ffn_w_freq_real"], f32)),
        "wfi": np.ascontiguousarray(np.asarray(inputs["L0_ffn_w_freq_imag"], f32)),
        "fbias_c": colpack(inputs["L0_ffn_bias"], NMB),
        "fr": fr, "fi": fi,
        "fr2": (fr / 1024.0).astype(f32), "fi2": (fi / 1024.0).astype(f32),
        "esel": esel,
        "eye": np.eye(128, dtype=f32),
        "ones_col": np.ones((128, 1), f32),
        "ones_row": np.ones((1, 128), f32),
    }
    for d in ("f", "b"):
        base[f"inproj_{d}"] = np.ascontiguousarray(
            np.asarray(inputs[f"L0_attn_inproj_{d}"], f32))
        base[f"xproj_{d}"] = np.ascontiguousarray(
            np.asarray(inputs[f"L0_attn_xproj_{d}"], f32))
        base[f"dtw_{d}"] = np.ascontiguousarray(
            np.asarray(inputs[f"L0_attn_dt_w_{d}"], f32))
        base[f"outproj_{d}"] = np.ascontiguousarray(
            np.asarray(inputs[f"L0_attn_outproj_{d}"], f32))
        cw = np.asarray(inputs[f"L0_attn_conv_w_{d}"], f32)
        base[f"convw_{d}"] = np.ascontiguousarray(
            cw.reshape(NB, 128, 4).transpose(1, 0, 2).reshape(128, NB * 4))
        base[f"convb_{d}"] = colpack(inputs[f"L0_attn_conv_b_{d}"], NB)
        base[f"dtb_{d}"] = colpack(inputs[f"L0_attn_dt_b_{d}"], NB)
        base[f"dpcol_{d}"] = colpack(inputs[f"L0_attn_Dp_{d}"], NB)
        al = np.asarray(inputs[f"L0_attn_Alog_{d}"], f32)
        base[f"alogc_{d}"] = np.ascontiguousarray(
            al.reshape(NB, 128, NST).transpose(1, 0, 2).reshape(128, NB * NST))
    x = np.asarray(inputs["L0_attn_x"], f32)
    in_maps = []
    for b in range(8):
        m = dict(base)
        m["xin"] = np.ascontiguousarray(x[b])
        in_maps.append(m)
    return in_maps


def kernel(**inputs):
    nc = _build()
    in_maps = _host_inputs(inputs)
    res = run_bass_kernel_spmd(nc, in_maps, core_ids=list(range(8)))
    return np.stack([np.asarray(r["out"], dtype=np.float32)
                     for r in res.results], axis=0)


# version 22
# speedup vs baseline: 1.0056x; 1.0056x over previous
import contextlib
import numpy as np

import concourse.bass as bass
import concourse.mybir as mybir
from concourse.bass_utils import run_bass_kernel_spmd

F32 = mybir.dt.float32
F32R = mybir.dt.float32r
BF16 = mybir.dt.bfloat16
AF = mybir.ActivationFunctionType
ALU = mybir.AluOpType

L = 2048
DM = 1024
DI = 2048
NST = 16
RNK = 64
LC = 512
NCH = L // LC
NB = DI // 128
NMB = DM // 128
KCD = DM // 128

ENGS = ("sync", "tensor", "scalar", "vector", "gpsimd")

STOP = None


class Prog:
    def __init__(self):
        self.ops = {e: [] for e in ENGS}

    def add(self, eng, fn):
        self.ops[eng].append(fn)


class Sem:
    def __init__(self, stack, nc, name):
        self.h = stack.enter_context(nc.semaphore(name))
        self.n = 0


def build_nc():
    nc = bass.Bass("TRN2", target_bir_lowering=False)
    P = Prog()
    stack = contextlib.ExitStack()
    dram = {}

    def din(name, shape, dt=F32):
        dram[name] = nc.dram_tensor(name, list(shape), dt,
                                    kind="ExternalInput").ap()
        return dram[name]

    def dscr(name, shape, dt):
        dram[name] = nc.dram_tensor(name, list(shape), dt, kind="Internal").ap()
        return dram[name]

    xin = din("xin", (L, DM))
    out_d = nc.dram_tensor("out", [L, DM], F32, kind="ExternalOutput").ap()

    for d in ("f", "b"):
        din(f"inproj_{d}", (DM, 2 * DI))
        din(f"xproj_{d}", (DI, RNK + 2 * NST))
        din(f"dtw_{d}", (RNK, DI))
        din(f"outproj_{d}", (DI, DM))
        din(f"convw_{d}", (128, NB * 4))
        din(f"convb_{d}", (128, NB))
        din(f"dtb_{d}", (128, NB))
        din(f"dpcol_{d}", (128, NB))
        din(f"alogc_{d}", (128, NB * NST))
    din("gate_w", (2 * DM, DM))
    din("gate_bc", (128, NMB))
    din("proj_w", (DM, DM))
    din("proj_bc", (128, NMB))
    din("wfr", (DM, DM), F32R)
    din("wfi", (DM, DM), F32R)
    din("fbias_c", (128, NMB))
    din("fr", (DM, DM), F32R)
    din("fi", (DM, DM), F32R)
    din("fr2", (DM, DM), F32R)
    din("fi2", (DM, DM), F32R)
    din("ones_col", (128, 1), F32)
    din("ones_row", (1, 128), F32)
    din("esel", (16, 16 * 128), BF16)
    din("eye", (128, 128))

    inpbf = {d: dscr(f"inpbf_{d}", (2 * NB, KCD, 128, 128), BF16)
             for d in ("f", "b")}
    outbf = {d: dscr(f"outbf_{d}", (DI, DM), BF16) for d in ("f", "b")}
    mscr = dscr("mscr", (DM, DM), F32R)
    yscr = {d: dscr(f"yscr_{d}", (NMB, 128, L), BF16) for d in ("f", "b")}

    s_in = Sem(stack, nc, "s_in")
    s_ws = Sem(stack, nc, "s_ws")
    s_st = Sem(stack, nc, "s_st")
    s_te = Sem(stack, nc, "s_te")
    s_act = Sem(stack, nc, "s_act")
    s_dve = Sem(stack, nc, "s_dve")
    s_dmp = Sem(stack, nc, "s_dmp")
    s_wsA = Sem(stack, nc, "s_wsA")
    s_wsB = Sem(stack, nc, "s_wsB")
    s_gp = Sem(stack, nc, "s_gp")
    s_w4 = [Sem(stack, nc, f"s_w4{i}") for i in range(4)]
    s_ldC = Sem(stack, nc, "s_ldC")
    s_ldD = Sem(stack, nc, "s_ldD")
    s_stC = Sem(stack, nc, "s_stC")

    def wait(eng, sem, val=None):
        v = sem.n if val is None else val
        if v > 0:
            P.add(eng, lambda e, h=sem.h, v=v: e.wait_ge(h, v))

    def run(eng, fn, inc=None):
        if inc is None:
            P.add(eng, fn)
            return None
        sem, k = inc
        sem.n += k
        P.add(eng, lambda e, fn=fn, h=sem.h, k=k: fn(e).then_inc(h, k))
        return sem.n

    def dma(eng, out, in_, sem):
        return run(eng, lambda e, o=out, i=in_: e.dma_start(out=o, in_=i),
                   inc=(sem, 16))

    _names = [0]

    def sbt(shape, dt, scope=None):
        _names[0] += 1
        return (scope or stack).enter_context(
            nc.sbuf_tensor(f"t{_names[0]}", list(shape), dt))

    pA = stack.enter_context(nc.psum_tensor("pA", [128, 2048], F32))
    pB = stack.enter_context(nc.psum_tensor("pB", [128, 2048], F32))

    def bank(i):
        p = pA if i < 4 else pB
        return p[:, (i % 4) * 512:(i % 4) * 512 + 512]

    bank_tok = [None] * 8

    def wait_bank(eng, i):
        t = bank_tok[i]
        if t is not None:
            wait(eng, t[0], t[1])

    out_wide = out_d.rearrange("(a p) m -> p a m", p=128)

    def dump(src_ap, width):
        wait("gpsimd", s_dve)
        wait("gpsimd", s_act)
        wait("gpsimd", s_te)
        for i in range(width // 1024):
            dma("gpsimd", out_wide[:, i, :],
                src_ap[:, i * 1024:(i + 1) * 1024], s_dmp)
        wait("sync", s_dmp)

    cols = {}
    for d in ("f", "b"):
        for nm, w in (("convw", NB * 4), ("convb", NB), ("dtb", NB),
                      ("dpcol", NB), ("alogc", NB * NST)):
            cols[f"{nm}_{d}"] = sbt((128, w), F32)
    for nm in ("gate_bc", "proj_bc", "fbias_c"):
        cols[nm] = sbt((128, NMB), F32)
    onec_sb = sbt((128, 1), F32)
    oner_sb = sbt((1, 128), F32)
    esel_sb = sbt((16, 16 * 128), BF16)
    eye_sb = sbt((128, 128), F32)
    acol = sbt((128, NB * NST), F32)
    carry = sbt((128, NB * NST), F32)
    stash = [sbt((128, NB * 4), BF16) for _ in range(2)]
    junk = sbt((1, 8), F32)
    epsc = sbt((128, 1), F32)

    for d in ("f", "b"):
        for nm in ("convw", "convb", "dtb", "dpcol", "alogc"):
            dma("sync", cols[f"{nm}_{d}"][:], dram[f"{nm}_{d}"][:, :], s_in)
    for nm in ("gate_bc", "proj_bc", "fbias_c"):
        dma("sync", cols[nm][:], dram[nm][:, :], s_in)
    dma("sync", onec_sb[:], dram["ones_col"][:, :], s_in)
    dma("sync", esel_sb[:], dram["esel"][:, :], s_in)
    dma("sync", oner_sb[:], dram["ones_row"][:, :], s_in)
    dma("sync", eye_sb[:], dram["eye"][:, :], s_in)
    wait("scalar", s_in)
    run("vector", lambda e: e.memset(epsc[:], 1e-5), inc=(s_dve, 1))

    ph1 = stack.enter_context(contextlib.ExitStack())
    if True:
        stgf = [sbt((128, 2048), F32, ph1) for _ in range(2)]
        stgb = [sbt((128, 2048), BF16, ph1) for _ in range(2)]
        slabs = []

        def cast_stream(src_row_fn, dst_ap_fn, nslab, colw, src3d_fn=None):
            for kc in range(nslab):
                slabs.append((src_row_fn(kc), dst_ap_fn(kc), colw,
                              src3d_fn(kc) if src3d_fn is not None else None))

        for d in ("f", "b"):
            cast_stream(
                lambda kc, d=d: dram[f"inproj_{d}"][
                    (kc // 2) * 128:(kc // 2 + 1) * 128,
                    (kc % 2) * 2048:(kc % 2 + 1) * 2048],
                lambda kc, d=d: inpbf[d][
                    (kc % 2) * 16:(kc % 2) * 16 + 16, kc // 2, :, :].rearrange(
                    "a p m -> p a m"),
                2 * KCD, 2048,
                src3d_fn=lambda kc: True)
            cast_stream(
                lambda kc, d=d: dram[f"outproj_{d}"][kc * 128:(kc + 1) * 128, :],
                lambda kc, d=d: outbf[d][kc * 128:(kc + 1) * 128, :],
                NB, DM)

        ld_tok = {}
        cast_tok = {}
        st_tok = {}
        for i, (src_ap, dst_ap, colw, src3d) in enumerate(slabs):
            par = i % 2
            if i - 2 in cast_tok:
                wait("scalar", s_act, cast_tok[i - 2])
                wait("scalar", s_stC, st_tok[i - 2])
            lsem = s_ldC if par == 0 else s_ldD
            dma("scalar", stgf[par][:, 0:colw], src_ap, lsem)
            ld_tok[i] = (lsem, lsem.n)
            if i - 1 in ld_tok and i >= 1:
                pass
            if i >= 1:
                j = i - 1
                pv = j % 2
                pcolw, psrc3d, pdst = slabs[j][2], slabs[j][3], slabs[j][1]
                wait("scalar", ld_tok[j][0], ld_tok[j][1])
                run("scalar", lambda e, cw=pcolw, b=stgb[pv], f=stgf[pv]:
                    e.activation(out=b[:, 0:cw], in_=f[:, 0:cw],
                                 func=AF.Copy), inc=(s_act, 1))
                cast_tok[j] = s_act.n
                wait("scalar", s_act, cast_tok[j])
                sap = (stgb[pv][:, 0:pcolw].rearrange(
                    "p (a m) -> p a m", m=128) if psrc3d
                    else stgb[pv][:, 0:pcolw])
                dma("scalar", pdst, sap, s_stC)
                st_tok[j] = s_stC.n
        j = len(slabs) - 1
        pv = j % 2
        pcolw, psrc3d, pdst = slabs[j][2], slabs[j][3], slabs[j][1]
        wait("scalar", ld_tok[j][0], ld_tok[j][1])
        run("scalar", lambda e, cw=pcolw, b=stgb[pv], f=stgf[pv]:
            e.activation(out=b[:, 0:cw], in_=f[:, 0:cw],
                         func=AF.Copy), inc=(s_act, 1))
        wait("scalar", s_act, s_act.n)
        sap = (stgb[pv][:, 0:pcolw].rearrange(
            "p (a m) -> p a m", m=128) if psrc3d
            else stgb[pv][:, 0:pcolw])
        dma("scalar", pdst, sap, s_stC)
        ph0_done = s_stC.n
        del slabs, ld_tok, cast_tok, st_tok

    if True:
        wr_sb = sbt((128, KCD * DM), F32R, ph1)
        wi_sb = sbt((128, KCD * DM), F32R, ph1)
        trt = sbt((128, NMB * DM), F32R, ph1)
        tit = sbt((128, NMB * DM), F32R, ph1)
        fstg = [sbt((128, 512), F32R, ph1) for _ in range(4)]
        mstg = [sbt((128, 512), F32R, ph1) for _ in range(2)]
        ntmp = sbt((128, KCD * DM), F32, ph1)

        wait("sync", s_in)
        dma("sync", wr_sb[:].rearrange("p (a o) -> p a o", o=DM),
            dram["wfr"].rearrange("(a p) o -> p a o", p=128), s_in)
        dma("sync", wi_sb[:].rearrange("p (a o) -> p a o", o=DM),
            dram["wfi"].rearrange("(a p) o -> p a o", p=128), s_in)
        w_ready = s_in.n
        wait("sync", s_in, w_ready)

        fsrc = []
        for _ in range(2):
            for dh in range(2):
                for kc in range(KCD):
                    fsrc.append((dram["fr"][kc * 128:(kc + 1) * 128,
                                            dh * 512:(dh + 1) * 512],
                                 dram["fi"][kc * 128:(kc + 1) * 128,
                                            dh * 512:(dh + 1) * 512]))
        for th in range(2):
            for ob in range(KCD):
                fsrc.append((dram["fr2"][ob * 128:(ob + 1) * 128,
                                         th * 512:(th + 1) * 512],
                             dram["fi2"][ob * 128:(ob + 1) * 128,
                                         th * 512:(th + 1) * 512]))
        fdma_tok = {}
        fte_tok = {}
        fstep = [0]

        def f_issue(i):
            if i >= len(fsrc):
                return
            par = i % 2
            if i - 2 >= 0:
                wait("sync", s_te, fte_tok[i - 2])
            dma("sync", fstg[2 * par][:], fsrc[i][0], s_w4[par])
            dma("sync", fstg[2 * par + 1][:], fsrc[i][1], s_w4[par])
            fdma_tok[i] = s_w4[par].n

        f_issue(0)
        f_issue(1)

        def t_pass(tdst, w0, w1):
            for dh in range(2):
                for kc in range(KCD):
                    i = fstep[0]
                    par = i % 2
                    wait("tensor", s_w4[par], fdma_tok[i])
                    if kc == 0:
                        wait("tensor", s_in, w_ready)
                        for ob in range(8):
                            wait_bank("tensor", ob)
                    for ob in range(KCD):
                        run("tensor", lambda e, b=bank(ob),
                            lh=w0[:, kc * DM + ob * 128:
                                  kc * DM + (ob + 1) * 128],
                            rh=fstg[2 * par][:], st=(kc == 0):
                            e.matmul(b, lh, rh, start=st, stop=False))
                        run("tensor", lambda e, b=bank(ob),
                            lh=w1[:, kc * DM + ob * 128:
                                  kc * DM + (ob + 1) * 128],
                            rh=fstg[2 * par + 1][:], sp=(kc == KCD - 1):
                            e.matmul(b, lh, rh, start=False, stop=sp),
                            inc=(s_te, 1) if ob == KCD - 1 else None)
                    fte_tok[i] = s_te.n
                    f_issue(i + 2)
                    fstep[0] += 1
                wait("vector", s_te)
                for ob in range(KCD):
                    run("vector", lambda e,
                        o=tdst[:, ob * DM + dh * 512:
                               ob * DM + (dh + 1) * 512],
                        i=bank(ob):
                        e.tensor_copy(out=o, in_=i),
                        inc=(s_dve, 1) if ob == KCD - 1 else None)
                for i in range(8):
                    bank_tok[i] = (s_dve, s_dve.n)

        t_pass(tit, wi_sb, wr_sb)
        wait("vector", s_te)
        run("vector", lambda e:
            e.tensor_scalar(out=ntmp[:], in0=wi_sb[:], scalar1=-1.0,
                            scalar2=None, op0=ALU.mult))
        run("vector", lambda e: e.drain())
        run("vector", lambda e:
            e.tensor_copy(out=wi_sb[:], in_=ntmp[:]), inc=(s_dve, 1))
        wi_neg_tok = s_dve.n
        wait("tensor", s_dve, wi_neg_tok)
        t_pass(trt, wr_sb, wi_sb)

        for th in range(2):
            for ob in range(KCD):
                i = fstep[0]
                par = i % 2
                wait("tensor", s_w4[par], fdma_tok[i])
                if ob == 0:
                    for j in range(8):
                        wait_bank("tensor", j)
                for db in range(KCD):
                    run("tensor", lambda e, b=bank(db),
                        lh=trt[:, ob * DM + db * 128: ob * DM + (db + 1) * 128],
                        rh=fstg[2 * par][:], st=(ob == 0):
                        e.matmul(b, lh, rh, start=st, stop=False))
                    run("tensor", lambda e, b=bank(db),
                        lh=tit[:, ob * DM + db * 128: ob * DM + (db + 1) * 128],
                        rh=fstg[2 * par + 1][:], sp=(ob == KCD - 1):
                        e.matmul(b, lh, rh, start=False, stop=sp),
                        inc=(s_te, 1) if db == KCD - 1 else None)
                fte_tok[i] = s_te.n
                f_issue(i + 2)
                fstep[0] += 1
            wait("vector", s_te)
            for db in range(KCD):
                wait("vector", s_st)
                run("vector", lambda e, o=mstg[db % 2][:], i=bank(db):
                    e.tensor_copy(out=o, in_=i), inc=(s_dve, 1))
                wait("sync", s_dve)
                dma("sync",
                    mscr[db * 128:(db + 1) * 128, th * 512:(th + 1) * 512],
                    mstg[db % 2][:], s_st)
            for i in range(8):
                bank_tok[i] = (s_dve, s_dve.n)
        wait("sync", s_st)
        wait("sync", s_stC, ph0_done)
        wait("sync", s_act)
        wait("sync", s_te)
        wait("sync", s_dve)
    ph1.close()

    ph2 = stack.enter_context(contextlib.ExitStack())
    ystg = sbt((128, NMB * 512), BF16, ph2)
    xTc = sbt((128, KCD * LC), BF16, ph2)
    scr2k = sbt((128, 2048), F32, ph2)
    xstg = scr2k[:, 0:DM]
    ubq = sbt((128, NB * LC), BF16, ph2)
    zbuf = sbt((128, NB * LC), BF16, ph2)
    dT = sbt((128, NB * LC), F32, ph2)
    dub = sbt((128, NB * LC), BF16, ph2)
    dtbf = sbt((64, LC), BF16, ph2)
    dblB = sbt((16, LC), BF16, ph2)
    dblC = sbt((16, LC), BF16, ph2)
    inpstg = [sbt((128, KCD * 128), BF16, ph2) for _ in range(3)]
    outstg = [sbt((128, DM), BF16, ph2) for _ in range(2)]

    def emit_xpose(c, xstg, dst):
        for lt in range(4):
            wait("sync", s_te)
            dma("sync", xstg,
                xin[c * LC + lt * 128: c * LC + (lt + 1) * 128, :], s_in)
            wait("tensor", s_in)
            wait("tensor", s_dve)
            wait("tensor", s_act)
            for mb in range(NMB):
                run("tensor", lambda e, o=bank(mb)[:, 0:128],
                    i=xstg[:, mb * 128:(mb + 1) * 128]:
                    e.transpose(o, i, eye_sb[:]),
                    inc=(s_te, 1) if mb == NMB - 1 else None)
            wait("vector", s_te)
            for mb in range(NMB):
                run("vector", lambda e,
                    o=dst[:, mb * LC + lt * 128: mb * LC + lt * 128 + 128],
                    i=bank(mb)[:, 0:128]:
                    e.tensor_copy(out=o, in_=i),
                    inc=(s_dve, 1) if mb == NMB - 1 else None)
        for i in range(8):
            bank_tok[i] = (s_dve, s_dve.n)

    last_q = [0]

    def mamba_dir(d, bufs):
        rev = (d == "b")
        xprojbf, dtwbf_, ucbuf, dscope = bufs
        with contextlib.ExitStack() as wp:
            wstg_ = sbt((128, 2048), F32, wp)
            wait("sync", s_dve)
            wait("sync", s_act)
            dma("sync", wstg_[:, 0:NB * 96].rearrange("p (a j) -> p a j", j=96),
                dram[f"xproj_{d}"].rearrange("(a p) j -> p a j", p=128), s_in)
            wait("vector", s_in)
            run("vector", lambda e:
                e.tensor_copy(out=xprojbf[:], in_=wstg_[:, 0:NB * 96]),
                inc=(s_dve, 1))
            wait("sync", s_dve)
            dma("sync", wstg_[0:64, 0:DI], dram[f"dtw_{d}"][:, :], s_in)
            wait("vector", s_in)
            run("vector", lambda e:
                e.tensor_copy(out=dtwbf_[:], in_=wstg_[0:64, 0:DI]),
                inc=(s_dve, 1))
            run("vector", lambda e: e.memset(carry[:], 0.0), inc=(s_dve, 1))
            run("vector", lambda e: e.memset(stash[0][:], 0.0),
                inc=(s_dve, 1))
            run("vector", lambda e: e.memset(stash[1][:], 0.0),
                inc=(s_dve, 1))
            wait("scalar", s_dve)
            run("scalar", lambda e, i=cols[f"alogc_{d}"][:]:
                e.activation(out=acol[:], in_=i, func=AF.Exp), inc=(s_act, 1))
            wait("vector", s_act)
            run("vector", lambda e:
                e.tensor_scalar(out=acol[:], in0=acol[:], scalar1=-1.0,
                                scalar2=None, op0=ALU.mult), inc=(s_dve, 1))
            wait("sync", s_dve)

        chunks = list(range(NCH - 1, -1, -1)) if rev else list(range(NCH))
        NSL = 32 * NCH
        inp_tok = {}
        inp_te = {}

        def inp_issue(k):
            if k >= NSL:
                return
            par = k % 3
            db = k % 32
            if k - 3 >= 0:
                wait("sync", s_te, inp_te[k - 3])
            dma("sync", inpstg[par][:].rearrange("p (a m) -> p a m", m=128),
                inpbf[d][db, :, :, :].rearrange("a p m -> p a m"),
                s_w4[par])
            inp_tok[k] = s_w4[par].n

        wait("sync", s_te)
        for k in range(3):
            inp_issue(k)

        tok = {"xp_te": 0, "xp_done": 0, "xp_in": 0, "uc_done": {},
               "scr_act": 0, "scr_dve": 0, "q_ready": last_q[0]}

        def front_bundles(ci):
            c = chunks[ci]
            bs = []
            for lt in range(4):
                def bxp(lt=lt, c=c):
                    if lt == 0:
                        wait("sync", s_act, tok["scr_act"])
                        wait("sync", s_dve, tok["scr_dve"])
                    else:
                        wait("sync", s_te, tok["xp_te"])
                    dma("sync", xstg,
                        xin[c * LC + lt * 128: c * LC + (lt + 1) * 128, :],
                        s_in)
                    xin_tok = s_in.n
                    wait("tensor", s_in, xin_tok)
                    for mb in range(NMB):
                        wait_bank("tensor", mb)
                    for mb in range(NMB):
                        run("tensor", lambda e, o=bank(mb)[:, 0:128],
                            i=xstg[:, mb * 128:(mb + 1) * 128]:
                            e.transpose(o, i, eye_sb[:]),
                            inc=(s_te, 1) if mb == NMB - 1 else None)
                    tok["xp_te"] = s_te.n
                    wait("scalar", s_te, tok["xp_te"])
                    for mb in range(NMB):
                        run("scalar", lambda e,
                            o=xTc[:, mb * LC + lt * 128:
                                  mb * LC + lt * 128 + 128],
                            i=bank(mb)[:, 0:128]:
                            e.activation(out=o, in_=i, func=AF.Copy),
                            inc=(s_act, 1) if mb == NMB - 1 else None)
                    for mb in range(NMB):
                        bank_tok[mb] = (s_act, s_act.n)
                    if lt == 3:
                        tok["xp_done"] = s_act.n
                        tok["xp_in"] = xin_tok
                bs.append(bxp)
            for db in range(NB):
                def bu(db=db, ci=ci):
                    k = ci * 32 + db
                    par = k % 3
                    bi = db % 8
                    wait("tensor", s_w4[par], inp_tok[k])
                    if db == 0:
                        wait("tensor", s_act, tok["xp_done"])
                    wait_bank("tensor", bi)
                    for kc in range(KCD):
                        run("tensor", lambda e, b=bank(bi),
                            lh=inpstg[par][:, kc * 128:(kc + 1) * 128],
                            rh=xTc[:, kc * LC:(kc + 1) * LC],
                            st=(kc == 0), sp=(kc == KCD - 1):
                            e.matmul(b, lh, rh, start=st, stop=sp),
                            inc=(s_te, 1) if kc == KCD - 1 else None)
                    inp_te[k] = s_te.n
                    inp_issue(k + 3)
                    wait("scalar", s_te, inp_te[k])
                    if db == 0:
                        wait("scalar", s_dve, tok["q_ready"])
                    run("scalar", lambda e,
                        o=ucbuf[:, db * LC:(db + 1) * LC], i=bank(bi):
                        e.activation(out=o, in_=i, func=AF.Copy),
                        inc=(s_act, 1))
                    bank_tok[bi] = (s_act, s_act.n)
                    if db == NB - 1:
                        tok["uc_done"][ci] = s_act.n
                bs.append(bu)
            return bs

        def z_quartet(ci, g):
            for db in range(NB + 4 * g, NB + 4 * g + 4):
                k = ci * 32 + db
                par = k % 3
                bi = db % 8
                wait("tensor", s_w4[par], inp_tok[k])
                wait_bank("tensor", bi)
                for kc in range(KCD):
                    run("tensor", lambda e, b=bank(bi),
                        lh=inpstg[par][:, kc * 128:(kc + 1) * 128],
                        rh=xTc[:, kc * LC:(kc + 1) * LC],
                        st=(kc == 0), sp=(kc == KCD - 1):
                        e.matmul(b, lh, rh, start=st, stop=sp),
                        inc=(s_te, 1) if kc == KCD - 1 else None)
                inp_te[k] = s_te.n
                inp_issue(k + 3)
                zdb = db - NB
                wait("scalar", s_te, inp_te[k])
                run("scalar", lambda e, o=zbuf[:, zdb * LC:(zdb + 1) * LC],
                    i=bank(bi): e.activation(out=o, in_=i, func=AF.Silu),
                    inc=(s_act, 1))
                bank_tok[bi] = (s_act, s_act.n)

        def emit_conv_z(ci):
            scur = stash[ci % 2]
            snxt = stash[(ci + 1) % 2]
            wcol = cols[f"convw_{d}"]
            bcol = cols[f"convb_{d}"]

            def conv_ops(db, j):
                u_sl = ucbuf[:, db * LC:(db + 1) * LC]
                acc = scr2k[:, j * 512:(j + 1) * 512]
                ops = [lambda e, o=acc, i=u_sl,
                       s1=wcol[:, db * 4 + 3: db * 4 + 4],
                       s2=bcol[:, db: db + 1]:
                       e.tensor_scalar(out=o, in0=i, scalar1=s1,
                                       scalar2=s2, op0=ALU.mult,
                                       op1=ALU.add)]
                for k in (2, 1, 0):
                    m = 3 - k
                    if not rev:
                        ops.append(lambda e, o=acc[:, m:512],
                                   i=u_sl[:, 0:512 - m],
                                   s1=wcol[:, db * 4 + k: db * 4 + k + 1],
                                   a=acc[:, m:512]:
                                   e.scalar_tensor_tensor(
                                       out=o, in0=i, scalar=s1, in1=a,
                                       op0=ALU.mult, op1=ALU.add))
                        ops.append(lambda e, o=acc[:, 0:m],
                                   i=scur[:, db * 4 + 3 - m: db * 4 + 3],
                                   s1=wcol[:, db * 4 + k: db * 4 + k + 1],
                                   a=acc[:, 0:m]:
                                   e.scalar_tensor_tensor(
                                       out=o, in0=i, scalar=s1, in1=a,
                                       op0=ALU.mult, op1=ALU.add))
                    else:
                        ops.append(lambda e, o=acc[:, 0:512 - m],
                                   i=u_sl[:, m:512],
                                   s1=wcol[:, db * 4 + k: db * 4 + k + 1],
                                   a=acc[:, 0:512 - m]:
                                   e.scalar_tensor_tensor(
                                       out=o, in0=i, scalar=s1, in1=a,
                                       op0=ALU.mult, op1=ALU.add))
                        ops.append(lambda e, o=acc[:, 512 - m:512],
                                   i=scur[:, db * 4: db * 4 + m],
                                   s1=wcol[:, db * 4 + k: db * 4 + k + 1],
                                   a=acc[:, 512 - m:512]:
                                   e.scalar_tensor_tensor(
                                       out=o, in0=i, scalar=s1, in1=a,
                                       op0=ALU.mult, op1=ALU.add))
                return ops

            wait("scalar", s_dve)
            wait("vector", s_act, tok["uc_done"][ci])
            wait("vector", s_te, tok["xp_te"])
            wait("vector", s_act, tok["scr_act"])
            wait("vector", s_in, tok["xp_in"])
            if not rev:
                run("vector", lambda e,
                    o=snxt[:].rearrange("p (a k) -> p a k", k=4)[:, :, 0:3],
                    i=ucbuf[:].rearrange("p (a l) -> p a l",
                                         l=LC)[:, :, 509:512]:
                    e.tensor_copy(out=o, in_=i))
            else:
                run("vector", lambda e,
                    o=snxt[:].rearrange("p (a k) -> p a k", k=4)[:, :, 0:3],
                    i=ucbuf[:].rearrange("p (a l) -> p a l",
                                         l=LC)[:, :, 0:3]:
                    e.tensor_copy(out=o, in_=i))
            silu_tok = 0
            silu_toks = []
            for grp in range(4):
                if grp > 0:
                    wait("vector", s_act, silu_tok)
                for j0 in (0, 2):
                    opsA = conv_ops(grp * 4 + j0, j0)
                    opsB = conv_ops(grp * 4 + j0 + 1, j0 + 1)
                    run("vector", opsA[0])
                    run("vector", opsB[0])
                    for lev in range(3):
                        run("vector", lambda e: e.drain())
                        run("vector", opsA[1 + 2 * lev])
                        run("vector", opsA[2 + 2 * lev])
                        run("vector", opsB[1 + 2 * lev])
                        run("vector", opsB[2 + 2 * lev])
                run("vector", lambda e: e.drain(), inc=(s_dve, 1))
                wait("scalar", s_dve)
                run("scalar", lambda e,
                    o=ucbuf[:, grp * 2048:(grp + 1) * 2048], i=scr2k[:]:
                    e.activation(out=o, in_=i, func=AF.Silu), inc=(s_act, 1))
                silu_tok = s_act.n
                silu_toks.append(s_act.n)
                z_quartet(ci, grp)
            return silu_toks

        def xproj_D(ci, silu_toks):
            for i in range(3):
                wait_bank("tensor", i)
            for grp in range(4):
                wait("tensor", s_act, silu_toks[grp])
                for kc in range(grp * 4, grp * 4 + 4):
                    for (j0, w_, dst_part) in ((0, 64, pA[0:64, 0:512]),
                                               (64, 16, pA[0:16, 512:1024]),
                                               (80, 16, pA[0:16, 1024:1536])):
                        run("tensor", lambda e, b=dst_part,
                            lh=xprojbf[:, kc * 96 + j0: kc * 96 + j0 + w_],
                            rh=ucbuf[:, kc * LC:(kc + 1) * LC],
                            st=(kc == 0), sp=(kc == NB - 1):
                            e.matmul(b, lh, rh, start=st, stop=sp),
                            inc=(s_te, 1) if kc == NB - 1 and j0 == 80
                            else None)
            wait("vector", s_te)
            run("vector", lambda e: e.tensor_copy(out=dtbf[:],
                                                  in_=pA[0:64, 0:512]))
            run("vector", lambda e: e.tensor_copy(out=dblB[:],
                                                  in_=pA[0:16, 512:1024]))
            run("vector", lambda e: e.tensor_copy(out=dblC[:],
                                                  in_=pA[0:16, 1024:1536]),
                inc=(s_dve, 1))
            for i in range(3):
                bank_tok[i] = (s_dve, s_dve.n)
            return s_dve.n

        def dtproj_E(ci):
            dw = dtwbf_
            dtb = cols[f"dtb_{d}"]
            wait("tensor", s_dve)
            for grp in range(4):
                bb = 4 - 4 * (grp % 2)
                pp = pB if grp % 2 == 0 else pA
                for j in range(4):
                    db = grp * 4 + j
                    wait_bank("tensor", bb + j)
                    run("tensor", lambda e, b=pp[:, j * 512:(j + 1) * 512],
                        lh=dw[:, db * 128:(db + 1) * 128], rh=dtbf[:]:
                        e.matmul(b, lh, rh, start=True, stop=True),
                        inc=(s_te, 1) if j == 3 else None)
                wait("scalar", s_te)
                wait("scalar", s_dve)
                run("scalar", lambda e: e.drain())
                for j in range(4):
                    db = grp * 4 + j
                    bk = pp[:, j * 512:(j + 1) * 512]
                    run("scalar", lambda e, o=scr2k[:, j * 512:(j + 1) * 512],
                        i=bk, b=dtb[:, db:db + 1]:
                        e.activation(out=o, in_=i, func=AF.Abs, bias=b))
                    run("scalar", lambda e, o=dT[:, db * LC:(db + 1) * LC],
                        i=bk, b=dtb[:, db:db + 1]:
                        e.activation(out=o, in_=i, func=AF.Relu, bias=b),
                        inc=(s_act, 1) if j == 3 else None)
                for j in range(4):
                    bank_tok[bb + j] = (s_act, s_act.n)
                run("scalar", lambda e: e.drain())
                run("scalar", lambda e: e.activation(
                    out=scr2k[:], in_=scr2k[:], func=AF.Exp, scale=-1.0))
                run("scalar", lambda e: e.drain())
                run("scalar", lambda e: e.activation(
                    out=scr2k[:], in_=scr2k[:], func=AF.Ln, bias=1.0),
                    inc=(s_act, 1))
                wait("vector", s_act)
                run("vector", lambda e,
                    o=dT[:, grp * 2048:(grp + 1) * 2048], b=scr2k[:]:
                    e.tensor_tensor(out=o, in0=o, in1=b, op=ALU.add),
                    inc=(s_dve, 1))
            tok["scr_act"] = s_act.n
            tok["scr_dve"] = s_dve.n

        def qinit_early(ci):
            run("vector", lambda e: e.drain())
            wait("vector", s_act)
            dpc = cols[f"dpcol_{d}"]
            for db in range(NB):
                run("vector", lambda e, o=ubq[:, db * LC:(db + 1) * LC],
                    i=ucbuf[:, db * LC:(db + 1) * LC], s1=dpc[:, db:db + 1]:
                    e.tensor_scalar(out=o, in0=i, scalar1=s1, scalar2=None,
                                    op0=ALU.mult))

        def qinit_F(ci):
            run("vector", lambda e: e.drain())
            run("vector", lambda e: e.tensor_tensor(
                out=dub[:], in0=dT[:], in1=ucbuf[:], op=ALU.mult),
                inc=(s_dve, 1))
            tok["q_ready"] = s_dve.n
            last_q[0] = s_dve.n
            return s_dve.n

        def y_evac(c):
            wait("scalar", s_te)
            wait("scalar", s_st)
            for mb in range(NMB):
                run("scalar", lambda e,
                    o=ystg[:, mb * 512:(mb + 1) * 512],
                    i=bank(mb):
                    e.activation(out=o, in_=i, func=AF.Copy),
                    inc=(s_act, 1) if mb == NMB - 1 else None)
            for i in range(8):
                bank_tok[i] = (s_act, s_act.n)
            wait("sync", s_act)
            dma("sync",
                yscr[d].rearrange("a p l -> p a l")[:, :, c * LC:(c + 1) * LC],
                ystg[:].rearrange("p (a l) -> p a l", l=512), s_st)

        for b in front_bundles(0):
            b()
        prev_c = None
        for ci, c in enumerate(chunks):
            if prev_c is not None:
                y_evac(prev_c)
            silu_toks = emit_conv_z(ci)
            if STOP == "uc":
                dump(ucbuf[:], NB * LC)
                return True
            qinit_early(ci)
            dbl_tok = xproj_D(ci, silu_toks)
            dtproj_E(ci)
            if STOP == "dT":
                dump(dT[:], NB * LC)
                return True
            q_ready = qinit_F(ci)
            fillers = front_bundles(ci + 1) if ci + 1 < NCH else []
            yield_scan(d, rev, c, q_ready, dbl_tok, fillers)
            if STOP == "hq":
                dump(ubq[:], NB * LC)
                return True
            prev_c = c
        y_evac(prev_c)
        return False

    def yield_scan(d, rev, c, q_ready, dbl_tok, fillers):
        ac = acol
        scan_tok = {0: 0, 1: 0}
        act_tok = {0: 0, 1: 0}
        sm_tok = {0: 0}
        te_tok = {}
        run("scalar", lambda e: e.drain())
        grp_dve_end = {}

        def emit_pats(g):
            gp = g % 2
            if g - 2 in grp_dve_end:
                wait("tensor", s_dve, grp_dve_end[g - 2])
            else:
                wait("tensor", s_dve)
            wait("tensor", s_act)
            for i in range(8):
                wait_bank("tensor", i)
            for j in range(4):
                n = g * 4 + j
                run("tensor", lambda e, b=pA[:, j * 512:(j + 1) * 512],
                    lh=esel_sb[:, n * 128:(n + 1) * 128], rh=dblB[:]:
                    e.matmul(b, lh, rh, start=True, stop=True))
                run("tensor", lambda e, b=pB[:, j * 512:(j + 1) * 512],
                    lh=esel_sb[:, n * 128:(n + 1) * 128], rh=dblC[:]:
                    e.matmul(b, lh, rh, start=True, stop=True),
                    inc=(s_te, 1) if j == 3 else None)
            wait("scalar", s_te)
            run("scalar", lambda e, o=bpat[gp][:]: e.activation(
                out=o, in_=pA[:], func=AF.Copy))
            run("scalar", lambda e, o=cpat[gp][:]: e.activation(
                out=o, in_=pB[:], func=AF.Copy), inc=(s_act, 1))
            for i in range(8):
                bank_tok[i] = (s_act, s_act.n)
            return ("act", s_act.n)

        pat_tok_next = emit_pats(0)
        for g in range(4):
            gp = g % 2
            pat_tok = pat_tok_next
            bpat_g = bpat[gp]
            cpat_g = cpat[gp]
            def emit_A(db):
                h = hg4[db % 2]
                sw = swide[db % 2]
                items = []
                smfn = (lambda e,
                        i0=dub[:, db * LC:(db + 1) * LC]
                        .rearrange("p (o l) -> p o l", o=1)
                        .broadcast_to((128, 4, LC)),
                        i1=bpat_g[:].rearrange("p (o l) -> p o l", l=LC),
                        o=sw[:].rearrange("p (o l) -> p o l", l=LC):
                        e.tensor_tensor(out=o, in0=i0, in1=i1, op=ALU.mult))
                items.append((None, smfn,
                              ("sm", db % 2, pat_tok if db == 0 else 0,
                               q_ready if (g == 0 and db == 0) else 0)))
                for j in range(4):
                    n = g * 4 + j
                    buf = j % 2
                    actfn = (lambda e, o=dab[buf][:],
                             i=dT[:, db * LC:(db + 1) * LC],
                             s=ac[:, db * NST + n: db * NST + n + 1]:
                             e.activation(out=o, in_=i, func=AF.Exp, scale=s))
                    items.append((None, actfn, ("act", buf)))
                    if not rev:
                        scfn = (lambda e, o=h[:, j * 512:(j + 1) * 512],
                                d0=dab[buf][:],
                                d1=sw[:, j * 512:(j + 1) * 512],
                                i0=carry[:, db * NST + n: db * NST + n + 1]:
                                e.tensor_tensor_scan(out=o, data0=d0, data1=d1,
                                                     initial=i0, op0=ALU.mult,
                                                     op1=ALU.add))
                    else:
                        scfn = (lambda e,
                                o=h[:, j * 512:(j + 1) * 512][:, ::-1],
                                d0=dab[buf][:, ::-1],
                                d1=sw[:, j * 512:(j + 1) * 512][:, ::-1],
                                i0=carry[:, db * NST + n: db * NST + n + 1]:
                                e.tensor_tensor_scan(out=o, data0=d0, data1=d1,
                                                     initial=i0, op0=ALU.mult,
                                                     op1=ALU.add))
                    items.append((None, scfn, ("scan", buf, db % 2, j == 0)))
                return items

            def emit_T(db):
                h = hg4[db % 2]
                cstart = 0 if rev else LC - 1
                ops = []
                ops.append((("dve", lambda: max(scan_tok[1],
                                                tloc.get("qa", 0))),
                            lambda e, hh=h, cp=cpat_g: e.tensor_tensor(
                                out=tmp2k[:], in0=hh[:], in1=cp[:],
                                op=ALU.mult), ("tok", "cm")))
                ops.append((None, lambda e,
                            o=carry[:, db * NST + g * 4: db * NST + g * 4 + 4],
                            i=h[:].rearrange("p (a l) -> p a l", l=LC)
                            [:, :, cstart:cstart + 1]
                            .rearrange("p a k -> p (a k)"):
                            e.tensor_copy(out=o, in_=i), None))
                ops.append((("dve", lambda: tloc["cm"]),
                            lambda e: e.tensor_tensor(
                                out=tmp2k[:, 0:1024], in0=tmp2k[:, 0:1024],
                                in1=tmp2k[:, 1024:2048], op=ALU.add),
                            ("tok", "a1")))
                ops.append((("dve", lambda: tloc["a1"]),
                            lambda e: e.tensor_tensor(
                                out=tmp2k[:, 0:512], in0=tmp2k[:, 0:512],
                                in1=tmp2k[:, 512:1024], op=ALU.add),
                            ("tok", "a2")))
                ops.append((("dve", lambda: tloc["a2"]),
                            lambda e, o=ubq[:, db * LC:(db + 1) * LC]:
                            e.tensor_tensor(out=o, in0=o, in1=tmp2k[:, 0:512],
                                            op=ALU.add), ("tok", "qa")))
                if g == 3:
                    par = db % 2
                    ops.append((("te2", db - 2 if db >= 2 else None,
                                 "dve", lambda: tloc["qa"]),
                                lambda e, o=y2b[par][:],
                                i0=ubq[:, db * LC:(db + 1) * LC],
                                i1=zbuf[:, db * LC:(db + 1) * LC]:
                                e.tensor_tensor(out=o, in0=i0, in1=i1,
                                                op=ALU.mult), ("y2", db)))
                return ops

            tloc = {}

            def flush(items):
                for item in items:
                    pre, fn, tag = item
                    if tag is not None and tag[0] == "act":
                        buf = tag[1]
                        wait("scalar", s_dve, scan_tok[buf])
                        run("scalar", fn, inc=(s_act, 1))
                        act_tok[buf] = s_act.n
                        continue
                    if tag is not None and tag[0] == "sm":
                        par, ptok, qtok = tag[1], tag[2], tag[3]
                        if ptok:
                            wait("vector", s_act, ptok[1])
                        run("vector", fn, inc=(s_dve, 1))
                        sm_tok[0] = s_dve.n
                        continue
                    if pre is not None:
                        if pre[0] == "dve":
                            wait("vector", s_dve, pre[1]())
                        elif pre[0] == "te2":
                            if pre[1] is not None:
                                wait("vector", s_te, te_tok[pre[1]])
                            wait("vector", s_dve, pre[3]())
                    if tag is not None and tag[0] == "scan":
                        buf, par, first = tag[1], tag[2], tag[3]
                        wait("vector", s_act, act_tok[buf])
                        if first:
                            wait("vector", s_dve, sm_tok[0])
                        run("vector", fn, inc=(s_dve, 1))
                        scan_tok[buf] = s_dve.n
                    elif tag is not None and tag[0] == "tok":
                        run("vector", fn, inc=(s_dve, 1))
                        tloc[tag[1]] = s_dve.n
                    elif tag is not None and tag[0] == "y2":
                        run("vector", fn, inc=(s_dve, 1))
                        emit_outproj(tag[1], s_dve.n)
                    else:
                        run("vector", fn)

            def interleave(a_items, t_items):
                out = []
                na, nt = len(a_items), len(t_items)
                ai = ti = 0
                while ai < na or ti < nt:
                    if ai < na:
                        out.append(a_items[ai]); ai += 1
                        if ai < na:
                            out.append(a_items[ai]); ai += 1
                    if ti < nt:
                        out.append(t_items[ti]); ti += 1
                return out

            def emit_outproj(db, y2_tok):
                par = db % 2
                psem = s_wsA if par == 0 else s_wsB
                if db == 0:
                    wait("sync", s_te)
                    dma("sync", outstg[0][:], outbf[d][0:128, :], s_wsA)
                    for i in range(8):
                        wait_bank("tensor", i)
                wait("tensor", psem)
                wait("tensor", s_dve, y2_tok)
                for mb in range(NMB):
                    run("tensor", lambda e, b=bank(mb),
                        lh=outstg[par][:, mb * 128:(mb + 1) * 128],
                        rh=y2b[par][:], st=(db == 0), sp=(db == NB - 1):
                        e.matmul(b, lh, rh, start=st, stop=sp),
                        inc=(s_te, 1) if mb == NMB - 1 else None)
                te_tok[db] = s_te.n
                if db < NB - 1:
                    if db >= 1:
                        wait("sync", s_te, te_tok[db - 1])
                    dma("sync", outstg[(db + 1) % 2][:],
                        outbf[d][(db + 1) * 128:(db + 2) * 128, :],
                        s_wsB if (db + 1) % 2 else s_wsA)

            prevT = None
            for db in range(NB):
                a_items = emit_A(db)
                if prevT is None:
                    flush(a_items)
                else:
                    flush(interleave(a_items, prevT))
                prevT = emit_T(db)
                if db == 1 and g < 3:
                    pat_tok_next = emit_pats(g + 1)
                if g < 3 and fillers:
                    fillers.pop(0)()
            flush(prevT)
            grp_dve_end[g] = s_dve.n

    bpat = [sbt((128, 4 * LC), BF16, ph2) for _ in range(2)]
    cpat = [sbt((128, 4 * LC), BF16, ph2) for _ in range(2)]
    hg4 = [sbt((128, 4 * LC), BF16, ph2) for _ in range(2)]
    dab = [sbt((128, LC), F32, ph2) for _ in range(2)]
    swide = [sbt((128, 4 * LC), BF16, ph2) for _ in range(2)]
    tmp2k = sbt((128, 2048), BF16, ph2)
    y2b = [sbt((128, LC), BF16, ph2) for _ in range(2)]

    stopped = False
    for d in ("f", "b"):
        with contextlib.ExitStack() as dscope:
            xprojbf_ = sbt((128, NB * 96), BF16, dscope)
            dtwbf_ = sbt((64, NB * 128), BF16, dscope)
            ucbuf = sbt((128, NB * LC), BF16, dscope)
            stopped = mamba_dir(d, (xprojbf_, dtwbf_, ucbuf, dscope))
            if stopped:
                return nc, P, stack
        if STOP == f"y_{d}":
            dump(ystg[:, 0:4096], 4096)
            return nc, P, stack
    ph2.close()

    ph3 = stack.enter_context(contextlib.ExitStack())
    ystg3 = {d: sbt((128, NMB * 512), BF16, ph3) for d in ("f", "b")}
    xstg3 = sbt((128, DM), F32, ph3)
    gbuf = sbt((128, NMB * LC), BF16, ph3)
    comb = sbt((128, NMB * LC), BF16, ph3)
    tbuf = sbt((128, NMB * LC), BF16, ph3)
    xm = sbt((128, NMB * LC), F32, ph3)
    x1 = sbt((128, NMB * LC), F32, ph3)
    x1r = sbt((128, NMB * LC), F32R, ph3)
    owide = sbt((128, NMB * LC), F32, ph3)
    mstg3 = [sbt((128, DM), F32R, ph3) for _ in range(2)]
    xTf = sbt((128, NMB * LC), F32, ph3)
    r1 = sbt((128, NMB * LC), F32, ph3)
    rows = sbt((1, 1024), F32, ph3)
    ostg = sbt((128, DM), F32, ph3)
    wstg = [sbt((128, DM), BF16, ph3) for _ in range(3)]
    wstgf = [sbt((128, DM), F32, ph3) for _ in range(3)]
    wsem = [s_w4[0], s_w4[1], s_ldC]

    def ln_block(src, xm, sqw, dst, src_toks=None):
        if src_toks is None:
            wait("tensor", s_dve)
        wait_bank("tensor", 0)
        for mb in range(NMB):
            if src_toks is not None:
                wait("tensor", s_dve, src_toks[mb])
            run("tensor", lambda e, b=pA[0:1, 0:512],
                lh=onec_sb[:], rh=src[:, mb * LC:(mb + 1) * LC],
                st=(mb == 0), sp=(mb == NMB - 1):
                e.matmul(b, lh, rh, start=st, stop=sp),
                inc=(s_te, 1) if mb == NMB - 1 else None)
        wait("vector", s_te)
        run("vector", lambda e: e.tensor_scalar(
            out=rows[0:1, 0:512], in0=pA[0:1, 0:512], scalar1=1.0 / DM,
            scalar2=None, op0=ALU.mult), inc=(s_dve, 1))
        bank_tok[0] = (s_dve, s_dve.n)
        wait("tensor", s_dve)
        wait_bank("tensor", 4)
        run("tensor", lambda e: e.matmul(pB[:, 0:512], oner_sb[:],
                                         rows[0:1, 0:512], start=True,
                                         stop=True), inc=(s_te, 1))
        wait("vector", s_te)
        for mb in range(NMB):
            run("vector", lambda e, o=xm[:, mb * LC:(mb + 1) * LC],
                a=src[:, mb * LC:(mb + 1) * LC], b=pB[:, 0:512]:
                e.tensor_tensor(out=o, in0=a, in1=b, op=ALU.subtract))
        run("vector", lambda e: e.drain())
        for mb in range(NMB):
            run("vector", lambda e, o=sqw[:, mb * LC:(mb + 1) * LC],
                a=xm[:, mb * LC:(mb + 1) * LC]:
                e.tensor_tensor(out=o, in0=a, in1=a, op=ALU.mult),
                inc=(s_dve, 1) if mb == NMB - 1 else None)
        bank_tok[4] = (s_dve, s_dve.n)
        wait("tensor", s_dve)
        wait_bank("tensor", 1)
        for mb in range(NMB):
            run("tensor", lambda e, b=pA[0:1, 512:1024],
                lh=onec_sb[:], rh=sqw[:, mb * LC:(mb + 1) * LC],
                st=(mb == 0), sp=(mb == NMB - 1):
                e.matmul(b, lh, rh, start=st, stop=sp),
                inc=(s_te, 1) if mb == NMB - 1 else None)
        wait("vector", s_te)
        run("vector", lambda e: e.tensor_scalar(
            out=rows[0:1, 512:1024], in0=pA[0:1, 512:1024], scalar1=1.0 / DM,
            scalar2=None, op0=ALU.mult), inc=(s_dve, 1))
        bank_tok[1] = (s_dve, s_dve.n)
        wait("scalar", s_dve)
        run("scalar", lambda e: e.activation(out=rows[0:1, 512:1024],
                                             in_=rows[0:1, 512:1024],
                                             func=AF.Ln, bias=epsc[0:1, 0:1]))
        run("scalar", lambda e: e.drain())
        run("scalar", lambda e: e.activation(out=rows[0:1, 512:1024],
                                             in_=rows[0:1, 512:1024],
                                             func=AF.Exp, scale=-0.5),
            inc=(s_act, 1))
        wait("tensor", s_act)
        wait_bank("tensor", 5)
        run("tensor", lambda e: e.matmul(pB[:, 512:1024], oner_sb[:],
                                         rows[0:1, 512:1024], start=True,
                                         stop=True), inc=(s_te, 1))
        wait("vector", s_te)
        dst_toks = []
        for mb in range(NMB):
            run("vector", lambda e, o=dst[:, mb * LC:(mb + 1) * LC],
                a=xm[:, mb * LC:(mb + 1) * LC], b=pB[:, 512:1024]:
                e.tensor_tensor(out=o, in0=a, in1=b, op=ALU.mult),
                inc=(s_dve, 1))
            dst_toks.append(s_dve.n)
        bank_tok[5] = (s_dve, s_dve.n)
        return dst_toks

    wait("sync", s_st)
    def issue_ystg(c, dve_tok, te_tok):
        if dve_tok:
            wait("sync", s_dve, dve_tok)
        if te_tok:
            wait("sync", s_te, te_tok)
        for d in ("f", "b"):
            dma("sync", ystg3[d][:].rearrange("p (a l) -> p a l", l=512),
                yscr[d].rearrange("a p l -> p a l")[:, :, c * LC:(c + 1) * LC],
                s_ldD)
        return s_ldD.n

    ystg_tok = {}
    wait("sync", s_te)
    wait("sync", s_dve)
    ystg_tok[0] = issue_ystg(0, 0, 0)
    for c in range(NCH):
        wait("sync", s_te)
        wait("sync", s_dve)
        emit_xpose(c, xstg3[:], xTf)
        if True:
            wdma_tok = {}
            wcast_tok = {}
            wte_tok = {}

            def w_dma(kc, src_d):
                par = kc % 3
                if kc - 3 in wcast_tok:
                    wait("sync", s_act, wcast_tok[kc - 3])
                dma("sync", wstgf[par][:],
                    src_d[kc * 128:(kc + 1) * 128, :], wsem[par])
                wdma_tok[kc] = wsem[par].n

            def w_cast(kc):
                par = kc % 3
                wait("scalar", wsem[par], wdma_tok[kc])
                if kc - 3 in wte_tok:
                    wait("scalar", s_te, wte_tok[kc - 3])
                run("scalar", lambda e, o=wstg[par][:], i=wstgf[par][:]:
                    e.activation(out=o, in_=i, func=AF.Copy),
                    inc=(s_act, 1))
                wcast_tok[kc] = s_act.n

            wait("sync", s_te)
            for kc in range(3):
                w_dma(kc, dram["gate_w"])
                w_cast(kc)
            for kc in range(16):
                par = kc % 3
                wait("tensor", s_act, wcast_tok[kc])
                if kc == 0:
                    for i in range(8):
                        wait_bank("tensor", i)
                    wait("tensor", s_ldD, ystg_tok[c])
                ysrc = ystg3["f"] if kc < 8 else ystg3["b"]
                ycol = (kc % 8) * 512
                for mb in range(NMB):
                    run("tensor", lambda e, b=bank(mb),
                        lh=wstg[par][:, mb * 128:(mb + 1) * 128],
                        rh=ysrc[:, ycol:ycol + LC],
                        st=(kc == 0), sp=(kc == 15):
                        e.matmul(b, lh, rh, start=st, stop=sp),
                        inc=(s_te, 1) if mb == NMB - 1 else None)
                wte_tok[kc] = s_te.n
                if kc + 3 < 16:
                    w_dma(kc + 3, dram["gate_w"])
                    w_cast(kc + 3)
            gate_fin = wte_tok[15]
            wait("scalar", s_te)
            for mb in range(NMB):
                run("scalar", lambda e, o=gbuf[:, mb * LC:(mb + 1) * LC],
                    i=bank(mb), b=cols["gate_bc"][:, mb:mb + 1]:
                    e.activation(out=o, in_=i, func=AF.Sigmoid, bias=b),
                    inc=(s_act, 1) if mb == NMB - 1 else None)
            for i in range(8):
                bank_tok[i] = (s_act, s_act.n)
            wait("vector", s_act)
            for mb in range(NMB):
                yf = ystg3["f"][:, mb * 512:(mb + 1) * 512]
                yb = ystg3["b"][:, mb * 512:(mb + 1) * 512]
                run("vector", lambda e, a=yf, b=yb,
                    o=tbuf[:, mb * LC:(mb + 1) * LC]:
                    e.tensor_tensor(out=o, in0=a, in1=b, op=ALU.subtract))
            run("vector", lambda e: e.drain())
            for mb in range(NMB):
                run("vector", lambda e, g=gbuf[:, mb * LC:(mb + 1) * LC],
                    o=tbuf[:, mb * LC:(mb + 1) * LC]:
                    e.tensor_tensor(out=o, in0=g, in1=o, op=ALU.mult))
            run("vector", lambda e: e.drain())
            comb_tok = []
            for mb in range(NMB):
                yb = ystg3["b"][:, mb * 512:(mb + 1) * 512]
                run("vector", lambda e, o=comb[:, mb * LC:(mb + 1) * LC],
                    b=yb, t=tbuf[:, mb * LC:(mb + 1) * LC]:
                    e.tensor_tensor(out=o, in0=b, in1=t, op=ALU.add),
                    inc=(s_dve, 1))
                comb_tok.append(s_dve.n)
            if c + 1 < NCH:
                ystg_tok[c + 1] = issue_ystg(c + 1, comb_tok[-1], gate_fin)
            for mb in range(NMB):
                run("vector", lambda e, o=r1[:, mb * LC:(mb + 1) * LC],
                    i=xTf[:, mb * LC:(mb + 1) * LC],
                    s1=cols["proj_bc"][:, mb:mb + 1]:
                    e.tensor_scalar(out=o, in0=i, scalar1=s1, scalar2=None,
                                    op0=ALU.add))
            run("vector", lambda e: e.drain(), inc=(s_dve, 1))
            r1h_tok = s_dve.n
            wdma_tok = {}
            wcast_tok = {}
            wte_tok = {}
            wait("sync", s_te)
            for kc in range(3):
                w_dma(kc, dram["proj_w"])
                w_cast(kc)
            for kc in range(8):
                par = kc % 3
                wait("tensor", s_act, wcast_tok[kc])
                wait("tensor", s_dve, comb_tok[kc])
                if kc == 0:
                    for i in range(8):
                        wait_bank("tensor", i)
                for mb in range(NMB):
                    run("tensor", lambda e, b=bank(mb),
                        lh=wstg[par][:, mb * 128:(mb + 1) * 128],
                        rh=comb[:, kc * LC:(kc + 1) * LC],
                        st=(kc == 0), sp=(kc == 7):
                        e.matmul(b, lh, rh, start=st, stop=sp),
                        inc=(s_te, 1) if mb == NMB - 1 else None)
                wte_tok[kc] = s_te.n
                if kc + 3 < 8:
                    w_dma(kc + 3, dram["proj_w"])
                    w_cast(kc + 3)
            proj_fin = wte_tok[7]
            wait("vector", s_te, proj_fin)
            r1_tok = []
            for mb in range(NMB):
                run("vector", lambda e, o=r1[:, mb * LC:(mb + 1) * LC],
                    b=bank(mb):
                    e.tensor_tensor(out=o, in0=o, in1=b, op=ALU.add),
                    inc=(s_dve, 1))
                r1_tok.append(s_dve.n)
            for i in range(8):
                bank_tok[i] = (s_dve, s_dve.n)
        if STOP == "r1":
            dump(r1[:], NMB * LC)
            return nc, P, stack
        if True:
            x1_toks = ln_block(r1[:], xm[:], owide[:], x1[:],
                               src_toks=r1_tok)
            x1r_tok = []
            for mb in range(NMB):
                wait("vector", s_dve, x1_toks[mb])
                run("vector", lambda e,
                    o=x1r[:, mb * LC:(mb + 1) * LC],
                    i=x1[:, mb * LC:(mb + 1) * LC]:
                    e.tensor_copy(out=o, in_=i), inc=(s_dve, 1))
                x1r_tok.append(s_dve.n)
            def m_dma(kc, tokd):
                par = kc % 2
                dma("sync", mstg3[par][:],
                    mscr[kc * 128:(kc + 1) * 128, :], s_w4[2 + par])
                tokd[kc] = s_w4[2 + par].n

            wdma_tok = {}
            wte_tok = {}
            wait("sync", s_te)
            m_dma(0, wdma_tok)
            m_dma(1, wdma_tok)
            for kc in range(8):
                par = kc % 2
                wait("tensor", s_w4[2 + par], wdma_tok[kc])
                wait("tensor", s_dve, x1r_tok[kc])
                if kc == 0:
                    for i in range(8):
                        wait_bank("tensor", i)
                for mb in range(NMB):
                    run("tensor", lambda e, b=bank(mb),
                        lh=mstg3[par][:, mb * 128:(mb + 1) * 128],
                        rh=x1r[:, kc * LC:(kc + 1) * LC],
                        st=(kc == 0), sp=(kc == 7):
                        e.matmul(b, lh, rh, start=st, stop=sp),
                        inc=(s_te, 1) if mb == NMB - 1 else None)
                wte_tok[kc] = s_te.n
                if kc + 2 < 8:
                    wait("sync", s_te, wte_tok[kc])
                    m_dma(kc + 2, wdma_tok)
            ffn_fin = wte_tok[7]
            for mb in range(NMB):
                run("vector", lambda e, o=r1[:, mb * LC:(mb + 1) * LC],
                    i=x1[:, mb * LC:(mb + 1) * LC],
                    s1=cols["fbias_c"][:, mb:mb + 1]:
                    e.tensor_scalar(out=o, in0=i, scalar1=s1, scalar2=None,
                                    op0=ALU.add))
            run("vector", lambda e: e.drain())
            wait("vector", s_te, ffn_fin)
            r2_tok = []
            for mb in range(NMB):
                run("vector", lambda e, o=r1[:, mb * LC:(mb + 1) * LC],
                    b=bank(mb):
                    e.tensor_tensor(out=o, in0=o, in1=b, op=ALU.add),
                    inc=(s_dve, 1))
                r2_tok.append(s_dve.n)
            for i in range(8):
                bank_tok[i] = (s_dve, s_dve.n)
            ow_toks = ln_block(r1[:], xm[:], x1[:], owide[:],
                               src_toks=r2_tok)
            wait("tensor", s_dve, ow_toks[-1])
            for lt in range(4):
                if lt > 0:
                    wait("tensor", s_dve)
                for i in range(8):
                    wait_bank("tensor", i)
                for mb in range(NMB):
                    run("tensor", lambda e, o=bank(mb)[:, 0:128],
                        i=owide[:, mb * LC + lt * 128:
                                mb * LC + lt * 128 + 128]:
                        e.transpose(o, i, eye_sb[:]),
                        inc=(s_te, 1) if mb == NMB - 1 else None)
                wait("vector", s_te)
                wait("vector", s_st)
                for mb in range(NMB):
                    run("vector", lambda e,
                        o=ostg[:, mb * 128:(mb + 1) * 128],
                        i=bank(mb)[:, 0:128]:
                        e.tensor_copy(out=o, in_=i),
                        inc=(s_dve, 1) if mb == NMB - 1 else None)
                for i in range(8):
                    bank_tok[i] = (s_dve, s_dve.n)
                wait("sync", s_dve)
                dma("sync",
                    out_d[c * LC + lt * 128: c * LC + (lt + 1) * 128, :],
                    ostg[:], s_st)
    ph3.close()

    P.add("sync", lambda e, h=s_st.h, v=s_st.n: e.wait_ge(h, v))
    return nc, P, stack


def _emit(nc, P):
    with nc.Block() as block:
        for eng in ENGS:
            ops = P.ops[eng]
            if not ops:
                continue

            def body(e, ops=ops):
                for fn in ops:
                    fn(e)
            getattr(block, eng)(body)


_CACHE = {}


def _build():
    if "nc" not in _CACHE:
        nc, P, stack = build_nc()
        _emit(nc, P)
        stack.close()
        _CACHE["nc"] = nc
    return _CACHE["nc"]


def _host_inputs(inputs):
    f32 = np.float32
    import ml_dtypes

    def colpack(v, width):
        return np.ascontiguousarray(np.asarray(v, f32).reshape(width, 128).T)

    D = DM
    jk = np.outer(np.arange(D), np.arange(D)) % D
    ang = (2.0 * np.pi / D) * jk
    fr = np.cos(ang).astype(f32)
    fi = (-np.sin(ang)).astype(f32)

    esel = np.zeros((16, 16 * 128), dtype=np.float32)
    for n in range(16):
        esel[n, n * 128:(n + 1) * 128] = 1.0
    esel = esel.astype(ml_dtypes.bfloat16)
    base = {
        "gate_w": np.ascontiguousarray(np.asarray(inputs["L0_attn_gate_w"], f32)),
        "gate_bc": colpack(inputs["L0_attn_gate_b"], NMB),
        "proj_w": np.ascontiguousarray(np.asarray(inputs["L0_attn_proj_w"], f32)),
        "proj_bc": colpack(inputs["L0_attn_proj_b"], NMB),
        "wfr": np.ascontiguousarray(np.asarray(inputs["L0_ffn_w_freq_real"], f32)),
        "wfi": np.ascontiguousarray(np.asarray(inputs["L0_ffn_w_freq_imag"], f32)),
        "fbias_c": colpack(inputs["L0_ffn_bias"], NMB),
        "fr": fr, "fi": fi,
        "fr2": (fr / 1024.0).astype(f32), "fi2": (fi / 1024.0).astype(f32),
        "esel": esel,
        "eye": np.eye(128, dtype=f32),
        "ones_col": np.ones((128, 1), f32),
        "ones_row": np.ones((1, 128), f32),
    }
    for d in ("f", "b"):
        base[f"inproj_{d}"] = np.ascontiguousarray(
            np.asarray(inputs[f"L0_attn_inproj_{d}"], f32))
        base[f"xproj_{d}"] = np.ascontiguousarray(
            np.asarray(inputs[f"L0_attn_xproj_{d}"], f32))
        base[f"dtw_{d}"] = np.ascontiguousarray(
            np.asarray(inputs[f"L0_attn_dt_w_{d}"], f32))
        base[f"outproj_{d}"] = np.ascontiguousarray(
            np.asarray(inputs[f"L0_attn_outproj_{d}"], f32))
        cw = np.asarray(inputs[f"L0_attn_conv_w_{d}"], f32)
        base[f"convw_{d}"] = np.ascontiguousarray(
            cw.reshape(NB, 128, 4).transpose(1, 0, 2).reshape(128, NB * 4))
        base[f"convb_{d}"] = colpack(inputs[f"L0_attn_conv_b_{d}"], NB)
        base[f"dtb_{d}"] = colpack(inputs[f"L0_attn_dt_b_{d}"], NB)
        base[f"dpcol_{d}"] = colpack(inputs[f"L0_attn_Dp_{d}"], NB)
        al = np.asarray(inputs[f"L0_attn_Alog_{d}"], f32)
        base[f"alogc_{d}"] = np.ascontiguousarray(
            al.reshape(NB, 128, NST).transpose(1, 0, 2).reshape(128, NB * NST))
    x = np.asarray(inputs["L0_attn_x"], f32)
    in_maps = []
    for b in range(8):
        m = dict(base)
        m["xin"] = np.ascontiguousarray(x[b])
        in_maps.append(m)
    return in_maps


def kernel(**inputs):
    nc = _build()
    in_maps = _host_inputs(inputs)
    res = run_bass_kernel_spmd(nc, in_maps, core_ids=list(range(8)))
    return np.stack([np.asarray(r["out"], dtype=np.float32)
                     for r in res.results], axis=0)


# version 23
# speedup vs baseline: 1.0147x; 1.0091x over previous
import contextlib
import numpy as np

import concourse.bass as bass
import concourse.mybir as mybir
from concourse.bass_utils import run_bass_kernel_spmd

F32 = mybir.dt.float32
F32R = mybir.dt.float32r
BF16 = mybir.dt.bfloat16
AF = mybir.ActivationFunctionType
ALU = mybir.AluOpType

L = 2048
DM = 1024
DI = 2048
NST = 16
RNK = 64
LC = 512
NCH = L // LC
NB = DI // 128
NMB = DM // 128
KCD = DM // 128

ENGS = ("sync", "tensor", "scalar", "vector", "gpsimd")

STOP = None


class Prog:
    def __init__(self):
        self.ops = {e: [] for e in ENGS}

    def add(self, eng, fn):
        self.ops[eng].append(fn)


class Sem:
    def __init__(self, stack, nc, name):
        self.h = stack.enter_context(nc.semaphore(name))
        self.n = 0


def build_nc():
    nc = bass.Bass("TRN2", target_bir_lowering=False)
    P = Prog()
    stack = contextlib.ExitStack()
    dram = {}

    def din(name, shape, dt=F32):
        dram[name] = nc.dram_tensor(name, list(shape), dt,
                                    kind="ExternalInput").ap()
        return dram[name]

    def dscr(name, shape, dt):
        dram[name] = nc.dram_tensor(name, list(shape), dt, kind="Internal").ap()
        return dram[name]

    xin = din("xin", (L, DM))
    out_d = nc.dram_tensor("out", [L, DM], F32, kind="ExternalOutput").ap()

    for d in ("f", "b"):
        din(f"inproj_{d}", (DM, 2 * DI))
        din(f"xproj_{d}", (DI, RNK + 2 * NST))
        din(f"dtw_{d}", (RNK, DI))
        din(f"outproj_{d}", (DI, DM))
        din(f"convw_{d}", (128, NB * 4))
        din(f"convb_{d}", (128, NB))
        din(f"dtb_{d}", (128, NB))
        din(f"dpcol_{d}", (128, NB))
        din(f"alogc_{d}", (128, NB * NST))
    din("gate_w", (2 * DM, DM))
    din("gate_bc", (128, NMB))
    din("proj_w", (DM, DM))
    din("proj_bc", (128, NMB))
    din("wfr", (DM, DM), F32R)
    din("wfi", (DM, DM), F32R)
    din("fbias_c", (128, NMB))
    din("fr", (DM, DM), F32R)
    din("fi", (DM, DM), F32R)
    din("fr2", (DM, DM), F32R)
    din("fi2", (DM, DM), F32R)
    din("ones_col", (128, 1), F32)
    din("ones_row", (1, 128), F32)
    din("esel", (16, 16 * 128), BF16)
    din("eye", (128, 128))

    inpbf = {d: dscr(f"inpbf_{d}", (2 * NB, KCD, 128, 128), BF16)
             for d in ("f", "b")}
    outbf = {d: dscr(f"outbf_{d}", (DI, DM), BF16) for d in ("f", "b")}
    mscr = dscr("mscr", (DM, DM), F32R)
    yscr = {d: dscr(f"yscr_{d}", (NMB, 128, L), BF16) for d in ("f", "b")}

    s_in = Sem(stack, nc, "s_in")
    s_ws = Sem(stack, nc, "s_ws")
    s_st = Sem(stack, nc, "s_st")
    s_te = Sem(stack, nc, "s_te")
    s_act = Sem(stack, nc, "s_act")
    s_dve = Sem(stack, nc, "s_dve")
    s_dmp = Sem(stack, nc, "s_dmp")
    s_wsA = Sem(stack, nc, "s_wsA")
    s_wsB = Sem(stack, nc, "s_wsB")
    s_gp = Sem(stack, nc, "s_gp")
    s_w4 = [Sem(stack, nc, f"s_w4{i}") for i in range(4)]
    s_ldC = Sem(stack, nc, "s_ldC")
    s_ldD = Sem(stack, nc, "s_ldD")
    s_stC = Sem(stack, nc, "s_stC")

    def wait(eng, sem, val=None):
        v = sem.n if val is None else val
        if v > 0:
            P.add(eng, lambda e, h=sem.h, v=v: e.wait_ge(h, v))

    def run(eng, fn, inc=None):
        if inc is None:
            P.add(eng, fn)
            return None
        sem, k = inc
        sem.n += k
        P.add(eng, lambda e, fn=fn, h=sem.h, k=k: fn(e).then_inc(h, k))
        return sem.n

    def dma(eng, out, in_, sem):
        return run(eng, lambda e, o=out, i=in_: e.dma_start(out=o, in_=i),
                   inc=(sem, 16))

    _names = [0]

    def sbt(shape, dt, scope=None):
        _names[0] += 1
        return (scope or stack).enter_context(
            nc.sbuf_tensor(f"t{_names[0]}", list(shape), dt))

    pA = stack.enter_context(nc.psum_tensor("pA", [128, 2048], F32))
    pB = stack.enter_context(nc.psum_tensor("pB", [128, 2048], F32))

    def bank(i):
        p = pA if i < 4 else pB
        return p[:, (i % 4) * 512:(i % 4) * 512 + 512]

    bank_tok = [None] * 8

    def wait_bank(eng, i):
        t = bank_tok[i]
        if t is not None:
            wait(eng, t[0], t[1])

    out_wide = out_d.rearrange("(a p) m -> p a m", p=128)

    def dump(src_ap, width):
        wait("gpsimd", s_dve)
        wait("gpsimd", s_act)
        wait("gpsimd", s_te)
        for i in range(width // 1024):
            dma("gpsimd", out_wide[:, i, :],
                src_ap[:, i * 1024:(i + 1) * 1024], s_dmp)
        wait("sync", s_dmp)

    cols = {}
    for d in ("f", "b"):
        for nm, w in (("convw", NB * 4), ("convb", NB), ("dtb", NB),
                      ("dpcol", NB), ("alogc", NB * NST)):
            cols[f"{nm}_{d}"] = sbt((128, w), F32)
    for nm in ("gate_bc", "proj_bc", "fbias_c"):
        cols[nm] = sbt((128, NMB), F32)
    onec_sb = sbt((128, 1), F32)
    oner_sb = sbt((1, 128), F32)
    esel_sb = sbt((16, 16 * 128), BF16)
    eye_sb = sbt((128, 128), F32)
    acol = sbt((128, NB * NST), F32)
    carry = sbt((128, NB * NST), F32)
    stash = [sbt((128, NB * 4), BF16) for _ in range(2)]
    junk = sbt((1, 8), F32)
    epsc = sbt((128, 1), F32)

    for d in ("f", "b"):
        for nm in ("convw", "convb", "dtb", "dpcol", "alogc"):
            dma("sync", cols[f"{nm}_{d}"][:], dram[f"{nm}_{d}"][:, :], s_in)
    for nm in ("gate_bc", "proj_bc", "fbias_c"):
        dma("sync", cols[nm][:], dram[nm][:, :], s_in)
    dma("sync", onec_sb[:], dram["ones_col"][:, :], s_in)
    dma("sync", esel_sb[:], dram["esel"][:, :], s_in)
    dma("sync", oner_sb[:], dram["ones_row"][:, :], s_in)
    dma("sync", eye_sb[:], dram["eye"][:, :], s_in)
    wait("scalar", s_in)
    run("vector", lambda e: e.memset(epsc[:], 1e-5), inc=(s_dve, 1))

    ph1 = stack.enter_context(contextlib.ExitStack())
    if True:
        stgf = [sbt((128, 2048), F32, ph1) for _ in range(2)]
        stgb = [sbt((128, 2048), BF16, ph1) for _ in range(2)]
        slabs = []

        def cast_stream(src_row_fn, dst_ap_fn, nslab, colw, src3d_fn=None):
            for kc in range(nslab):
                slabs.append((src_row_fn(kc), dst_ap_fn(kc), colw,
                              src3d_fn(kc) if src3d_fn is not None else None))

        for d in ("f", "b"):
            cast_stream(
                lambda kc, d=d: dram[f"inproj_{d}"][
                    (kc // 2) * 128:(kc // 2 + 1) * 128,
                    (kc % 2) * 2048:(kc % 2 + 1) * 2048],
                lambda kc, d=d: inpbf[d][
                    (kc % 2) * 16:(kc % 2) * 16 + 16, kc // 2, :, :].rearrange(
                    "a p m -> p a m"),
                2 * KCD, 2048,
                src3d_fn=lambda kc: True)
            cast_stream(
                lambda kc, d=d: dram[f"outproj_{d}"][kc * 128:(kc + 1) * 128, :],
                lambda kc, d=d: outbf[d][kc * 128:(kc + 1) * 128, :],
                NB, DM)

        ld_tok = {}
        cast_tok = {}
        st_tok = {}
        for i, (src_ap, dst_ap, colw, src3d) in enumerate(slabs):
            par = i % 2
            if i - 2 in cast_tok:
                wait("scalar", s_act, cast_tok[i - 2])
                wait("scalar", s_stC, st_tok[i - 2])
            lsem = s_ldC if par == 0 else s_ldD
            dma("scalar", stgf[par][:, 0:colw], src_ap, lsem)
            ld_tok[i] = (lsem, lsem.n)
            if i - 1 in ld_tok and i >= 1:
                pass
            if i >= 1:
                j = i - 1
                pv = j % 2
                pcolw, psrc3d, pdst = slabs[j][2], slabs[j][3], slabs[j][1]
                wait("scalar", ld_tok[j][0], ld_tok[j][1])
                run("scalar", lambda e, cw=pcolw, b=stgb[pv], f=stgf[pv]:
                    e.activation(out=b[:, 0:cw], in_=f[:, 0:cw],
                                 func=AF.Copy), inc=(s_act, 1))
                cast_tok[j] = s_act.n
                wait("scalar", s_act, cast_tok[j])
                sap = (stgb[pv][:, 0:pcolw].rearrange(
                    "p (a m) -> p a m", m=128) if psrc3d
                    else stgb[pv][:, 0:pcolw])
                dma("scalar", pdst, sap, s_stC)
                st_tok[j] = s_stC.n
        j = len(slabs) - 1
        pv = j % 2
        pcolw, psrc3d, pdst = slabs[j][2], slabs[j][3], slabs[j][1]
        wait("scalar", ld_tok[j][0], ld_tok[j][1])
        run("scalar", lambda e, cw=pcolw, b=stgb[pv], f=stgf[pv]:
            e.activation(out=b[:, 0:cw], in_=f[:, 0:cw],
                         func=AF.Copy), inc=(s_act, 1))
        wait("scalar", s_act, s_act.n)
        sap = (stgb[pv][:, 0:pcolw].rearrange(
            "p (a m) -> p a m", m=128) if psrc3d
            else stgb[pv][:, 0:pcolw])
        dma("scalar", pdst, sap, s_stC)
        ph0_done = s_stC.n
        del slabs, ld_tok, cast_tok, st_tok

    if True:
        wr_sb = sbt((128, KCD * DM), F32R, ph1)
        wi_sb = sbt((128, KCD * DM), F32R, ph1)
        trt = sbt((128, NMB * DM), F32R, ph1)
        tit = sbt((128, NMB * DM), F32R, ph1)
        fstg = [sbt((128, 512), F32R, ph1) for _ in range(4)]
        mstg = [sbt((128, 512), F32R, ph1) for _ in range(2)]
        ntmp = sbt((128, KCD * DM), F32, ph1)

        wait("sync", s_in)
        dma("sync", wr_sb[:].rearrange("p (a o) -> p a o", o=DM),
            dram["wfr"].rearrange("(a p) o -> p a o", p=128), s_in)
        dma("sync", wi_sb[:].rearrange("p (a o) -> p a o", o=DM),
            dram["wfi"].rearrange("(a p) o -> p a o", p=128), s_in)
        w_ready = s_in.n
        wait("sync", s_in, w_ready)

        fsrc = []
        for _ in range(2):
            for dh in range(2):
                for kc in range(KCD):
                    fsrc.append((dram["fr"][kc * 128:(kc + 1) * 128,
                                            dh * 512:(dh + 1) * 512],
                                 dram["fi"][kc * 128:(kc + 1) * 128,
                                            dh * 512:(dh + 1) * 512]))
        for th in range(2):
            for ob in range(KCD):
                fsrc.append((dram["fr2"][ob * 128:(ob + 1) * 128,
                                         th * 512:(th + 1) * 512],
                             dram["fi2"][ob * 128:(ob + 1) * 128,
                                         th * 512:(th + 1) * 512]))
        fdma_tok = {}
        fte_tok = {}
        fstep = [0]

        def f_issue(i):
            if i >= len(fsrc):
                return
            par = i % 2
            if i - 2 >= 0:
                wait("sync", s_te, fte_tok[i - 2])
            dma("sync", fstg[2 * par][:], fsrc[i][0], s_w4[par])
            dma("sync", fstg[2 * par + 1][:], fsrc[i][1], s_w4[par])
            fdma_tok[i] = s_w4[par].n

        f_issue(0)
        f_issue(1)

        def t_pass(tdst, w0, w1):
            for dh in range(2):
                for kc in range(KCD):
                    i = fstep[0]
                    par = i % 2
                    wait("tensor", s_w4[par], fdma_tok[i])
                    if kc == 0:
                        wait("tensor", s_in, w_ready)
                        for ob in range(8):
                            wait_bank("tensor", ob)
                    for ob in range(KCD):
                        run("tensor", lambda e, b=bank(ob),
                            lh=w0[:, kc * DM + ob * 128:
                                  kc * DM + (ob + 1) * 128],
                            rh=fstg[2 * par][:], st=(kc == 0):
                            e.matmul(b, lh, rh, start=st, stop=False))
                        run("tensor", lambda e, b=bank(ob),
                            lh=w1[:, kc * DM + ob * 128:
                                  kc * DM + (ob + 1) * 128],
                            rh=fstg[2 * par + 1][:], sp=(kc == KCD - 1):
                            e.matmul(b, lh, rh, start=False, stop=sp),
                            inc=(s_te, 1) if ob == KCD - 1 else None)
                    fte_tok[i] = s_te.n
                    f_issue(i + 2)
                    fstep[0] += 1
                wait("vector", s_te)
                for ob in range(KCD):
                    run("vector", lambda e,
                        o=tdst[:, ob * DM + dh * 512:
                               ob * DM + (dh + 1) * 512],
                        i=bank(ob):
                        e.tensor_copy(out=o, in_=i),
                        inc=(s_dve, 1) if ob == KCD - 1 else None)
                for i in range(8):
                    bank_tok[i] = (s_dve, s_dve.n)

        t_pass(tit, wi_sb, wr_sb)
        wait("vector", s_te)
        run("vector", lambda e:
            e.tensor_scalar(out=ntmp[:], in0=wi_sb[:], scalar1=-1.0,
                            scalar2=None, op0=ALU.mult))
        run("vector", lambda e: e.drain())
        run("vector", lambda e:
            e.tensor_copy(out=wi_sb[:], in_=ntmp[:]), inc=(s_dve, 1))
        wi_neg_tok = s_dve.n
        wait("tensor", s_dve, wi_neg_tok)
        t_pass(trt, wr_sb, wi_sb)

        for th in range(2):
            for ob in range(KCD):
                i = fstep[0]
                par = i % 2
                wait("tensor", s_w4[par], fdma_tok[i])
                if ob == 0:
                    for j in range(8):
                        wait_bank("tensor", j)
                for db in range(KCD):
                    run("tensor", lambda e, b=bank(db),
                        lh=trt[:, ob * DM + db * 128: ob * DM + (db + 1) * 128],
                        rh=fstg[2 * par][:], st=(ob == 0):
                        e.matmul(b, lh, rh, start=st, stop=False))
                    run("tensor", lambda e, b=bank(db),
                        lh=tit[:, ob * DM + db * 128: ob * DM + (db + 1) * 128],
                        rh=fstg[2 * par + 1][:], sp=(ob == KCD - 1):
                        e.matmul(b, lh, rh, start=False, stop=sp),
                        inc=(s_te, 1) if db == KCD - 1 else None)
                fte_tok[i] = s_te.n
                f_issue(i + 2)
                fstep[0] += 1
            wait("vector", s_te)
            for db in range(KCD):
                wait("vector", s_st)
                run("vector", lambda e, o=mstg[db % 2][:], i=bank(db):
                    e.tensor_copy(out=o, in_=i), inc=(s_dve, 1))
                wait("sync", s_dve)
                dma("sync",
                    mscr[db * 128:(db + 1) * 128, th * 512:(th + 1) * 512],
                    mstg[db % 2][:], s_st)
            for i in range(8):
                bank_tok[i] = (s_dve, s_dve.n)
        wait("sync", s_st)
        wait("sync", s_stC, ph0_done)
        wait("sync", s_act)
        wait("sync", s_te)
        wait("sync", s_dve)
    ph1.close()

    ph2 = stack.enter_context(contextlib.ExitStack())
    ystg = sbt((128, NMB * 512), BF16, ph2)
    xTc = sbt((128, KCD * LC), BF16, ph2)
    scr2k = sbt((128, 2048), F32, ph2)
    xstg = scr2k[:, 0:DM]
    ubq = sbt((128, NB * LC), BF16, ph2)
    zbuf = sbt((128, NB * LC), BF16, ph2)
    dT = sbt((128, NB * LC), F32, ph2)
    dub = sbt((128, NB * LC), BF16, ph2)
    dtbf = sbt((64, LC), BF16, ph2)
    dblB = sbt((16, LC), BF16, ph2)
    dblC = sbt((16, LC), BF16, ph2)
    inpstg = [sbt((128, KCD * 128), BF16, ph2) for _ in range(3)]
    outstg = [sbt((128, DM), BF16, ph2) for _ in range(2)]

    def emit_xpose(c, xstg, dst):
        for lt in range(4):
            wait("sync", s_te)
            dma("sync", xstg,
                xin[c * LC + lt * 128: c * LC + (lt + 1) * 128, :], s_in)
            wait("tensor", s_in)
            wait("tensor", s_dve)
            wait("tensor", s_act)
            for mb in range(NMB):
                run("tensor", lambda e, o=bank(mb)[:, 0:128],
                    i=xstg[:, mb * 128:(mb + 1) * 128]:
                    e.transpose(o, i, eye_sb[:]),
                    inc=(s_te, 1) if mb == NMB - 1 else None)
            wait("vector", s_te)
            for mb in range(NMB):
                run("vector", lambda e,
                    o=dst[:, mb * LC + lt * 128: mb * LC + lt * 128 + 128],
                    i=bank(mb)[:, 0:128]:
                    e.tensor_copy(out=o, in_=i),
                    inc=(s_dve, 1) if mb == NMB - 1 else None)
        for i in range(8):
            bank_tok[i] = (s_dve, s_dve.n)

    last_q = [0]

    def mamba_dir(d, bufs):
        rev = (d == "b")
        xprojbf, dtwbf_, ucbuf, dscope = bufs
        with contextlib.ExitStack() as wp:
            wstg_ = sbt((128, 2048), F32, wp)
            wait("sync", s_dve)
            wait("sync", s_act)
            dma("sync", wstg_[:, 0:NB * 96].rearrange("p (a j) -> p a j", j=96),
                dram[f"xproj_{d}"].rearrange("(a p) j -> p a j", p=128), s_in)
            wait("vector", s_in)
            run("vector", lambda e:
                e.tensor_copy(out=xprojbf[:], in_=wstg_[:, 0:NB * 96]),
                inc=(s_dve, 1))
            wait("sync", s_dve)
            dma("sync", wstg_[0:64, 0:DI], dram[f"dtw_{d}"][:, :], s_in)
            wait("vector", s_in)
            run("vector", lambda e:
                e.tensor_copy(out=dtwbf_[:], in_=wstg_[0:64, 0:DI]),
                inc=(s_dve, 1))
            run("vector", lambda e: e.memset(carry[:], 0.0), inc=(s_dve, 1))
            run("vector", lambda e: e.memset(stash[0][:], 0.0),
                inc=(s_dve, 1))
            run("vector", lambda e: e.memset(stash[1][:], 0.0),
                inc=(s_dve, 1))
            wait("scalar", s_dve)
            run("scalar", lambda e, i=cols[f"alogc_{d}"][:]:
                e.activation(out=acol[:], in_=i, func=AF.Exp), inc=(s_act, 1))
            wait("vector", s_act)
            run("vector", lambda e:
                e.tensor_scalar(out=acol[:], in0=acol[:], scalar1=-1.0,
                                scalar2=None, op0=ALU.mult), inc=(s_dve, 1))
            wait("sync", s_dve)

        chunks = list(range(NCH - 1, -1, -1)) if rev else list(range(NCH))
        NSL = 32 * NCH
        inp_tok = {}
        inp_te = {}

        def inp_issue(k):
            if k >= NSL:
                return
            par = k % 3
            db = k % 32
            if k - 3 >= 0:
                wait("sync", s_te, inp_te[k - 3])
            dma("sync", inpstg[par][:].rearrange("p (a m) -> p a m", m=128),
                inpbf[d][db, :, :, :].rearrange("a p m -> p a m"),
                s_w4[par])
            inp_tok[k] = s_w4[par].n

        wait("sync", s_te)
        for k in range(3):
            inp_issue(k)

        tok = {"xp_te": 0, "xp_done": 0, "xp_in": 0, "uc_done": {},
               "scr_act": 0, "scr_dve": 0, "q_ready": last_q[0]}

        def front_bundles(ci):
            c = chunks[ci]
            bs = []
            for lt in range(4):
                def bxp(lt=lt, c=c):
                    if lt == 0:
                        wait("sync", s_act, tok["scr_act"])
                        wait("sync", s_dve, tok["scr_dve"])
                    else:
                        wait("sync", s_te, tok["xp_te"])
                    dma("sync", xstg,
                        xin[c * LC + lt * 128: c * LC + (lt + 1) * 128, :],
                        s_in)
                    xin_tok = s_in.n
                    wait("tensor", s_in, xin_tok)
                    for mb in range(NMB):
                        wait_bank("tensor", mb)
                    for mb in range(NMB):
                        run("tensor", lambda e, o=bank(mb)[:, 0:128],
                            i=xstg[:, mb * 128:(mb + 1) * 128]:
                            e.transpose(o, i, eye_sb[:]),
                            inc=(s_te, 1) if mb == NMB - 1 else None)
                    tok["xp_te"] = s_te.n
                    wait("scalar", s_te, tok["xp_te"])
                    for mb in range(NMB):
                        run("scalar", lambda e,
                            o=xTc[:, mb * LC + lt * 128:
                                  mb * LC + lt * 128 + 128],
                            i=bank(mb)[:, 0:128]:
                            e.activation(out=o, in_=i, func=AF.Copy),
                            inc=(s_act, 1) if mb == NMB - 1 else None)
                    for mb in range(NMB):
                        bank_tok[mb] = (s_act, s_act.n)
                    if lt == 3:
                        tok["xp_done"] = s_act.n
                        tok["xp_in"] = xin_tok
                bs.append(bxp)
            for db in range(NB):
                def bu(db=db, ci=ci):
                    k = ci * 32 + db
                    par = k % 3
                    bi = db % 8
                    wait("tensor", s_w4[par], inp_tok[k])
                    if db == 0:
                        wait("tensor", s_act, tok["xp_done"])
                    wait_bank("tensor", bi)
                    for kc in range(KCD):
                        run("tensor", lambda e, b=bank(bi),
                            lh=inpstg[par][:, kc * 128:(kc + 1) * 128],
                            rh=xTc[:, kc * LC:(kc + 1) * LC],
                            st=(kc == 0), sp=(kc == KCD - 1):
                            e.matmul(b, lh, rh, start=st, stop=sp),
                            inc=(s_te, 1) if kc == KCD - 1 else None)
                    inp_te[k] = s_te.n
                    inp_issue(k + 3)
                    wait("scalar", s_te, inp_te[k])
                    if db == 0:
                        wait("scalar", s_dve, tok["q_ready"])
                    run("scalar", lambda e,
                        o=ucbuf[:, db * LC:(db + 1) * LC], i=bank(bi):
                        e.activation(out=o, in_=i, func=AF.Copy),
                        inc=(s_act, 1))
                    bank_tok[bi] = (s_act, s_act.n)
                    if db == NB - 1:
                        tok["uc_done"][ci] = s_act.n
                bs.append(bu)
            return bs

        def z_quartet(ci, g):
            for db in range(NB + 4 * g, NB + 4 * g + 4):
                k = ci * 32 + db
                par = k % 3
                bi = db % 8
                wait("tensor", s_w4[par], inp_tok[k])
                wait_bank("tensor", bi)
                for kc in range(KCD):
                    run("tensor", lambda e, b=bank(bi),
                        lh=inpstg[par][:, kc * 128:(kc + 1) * 128],
                        rh=xTc[:, kc * LC:(kc + 1) * LC],
                        st=(kc == 0), sp=(kc == KCD - 1):
                        e.matmul(b, lh, rh, start=st, stop=sp),
                        inc=(s_te, 1) if kc == KCD - 1 else None)
                inp_te[k] = s_te.n
                inp_issue(k + 3)
                zdb = db - NB
                wait("scalar", s_te, inp_te[k])
                run("scalar", lambda e, o=zbuf[:, zdb * LC:(zdb + 1) * LC],
                    i=bank(bi): e.activation(out=o, in_=i, func=AF.Silu),
                    inc=(s_act, 1))
                bank_tok[bi] = (s_act, s_act.n)

        def emit_conv_z(ci):
            scur = stash[ci % 2]
            snxt = stash[(ci + 1) % 2]
            wcol = cols[f"convw_{d}"]
            bcol = cols[f"convb_{d}"]

            def conv_ops(db, j):
                u_sl = ucbuf[:, db * LC:(db + 1) * LC]
                acc = scr2k[:, j * 512:(j + 1) * 512]
                ops = [lambda e, o=acc, i=u_sl,
                       s1=wcol[:, db * 4 + 3: db * 4 + 4],
                       s2=bcol[:, db: db + 1]:
                       e.tensor_scalar(out=o, in0=i, scalar1=s1,
                                       scalar2=s2, op0=ALU.mult,
                                       op1=ALU.add)]
                for k in (2, 1, 0):
                    m = 3 - k
                    if not rev:
                        ops.append(lambda e, o=acc[:, m:512],
                                   i=u_sl[:, 0:512 - m],
                                   s1=wcol[:, db * 4 + k: db * 4 + k + 1],
                                   a=acc[:, m:512]:
                                   e.scalar_tensor_tensor(
                                       out=o, in0=i, scalar=s1, in1=a,
                                       op0=ALU.mult, op1=ALU.add))
                        ops.append(lambda e, o=acc[:, 0:m],
                                   i=scur[:, db * 4 + 3 - m: db * 4 + 3],
                                   s1=wcol[:, db * 4 + k: db * 4 + k + 1],
                                   a=acc[:, 0:m]:
                                   e.scalar_tensor_tensor(
                                       out=o, in0=i, scalar=s1, in1=a,
                                       op0=ALU.mult, op1=ALU.add))
                    else:
                        ops.append(lambda e, o=acc[:, 0:512 - m],
                                   i=u_sl[:, m:512],
                                   s1=wcol[:, db * 4 + k: db * 4 + k + 1],
                                   a=acc[:, 0:512 - m]:
                                   e.scalar_tensor_tensor(
                                       out=o, in0=i, scalar=s1, in1=a,
                                       op0=ALU.mult, op1=ALU.add))
                        ops.append(lambda e, o=acc[:, 512 - m:512],
                                   i=scur[:, db * 4: db * 4 + m],
                                   s1=wcol[:, db * 4 + k: db * 4 + k + 1],
                                   a=acc[:, 512 - m:512]:
                                   e.scalar_tensor_tensor(
                                       out=o, in0=i, scalar=s1, in1=a,
                                       op0=ALU.mult, op1=ALU.add))
                return ops

            wait("scalar", s_dve)
            wait("vector", s_act, tok["uc_done"][ci])
            wait("vector", s_te, tok["xp_te"])
            wait("vector", s_act, tok["scr_act"])
            wait("vector", s_in, tok["xp_in"])
            if not rev:
                run("vector", lambda e,
                    o=snxt[:].rearrange("p (a k) -> p a k", k=4)[:, :, 0:3],
                    i=ucbuf[:].rearrange("p (a l) -> p a l",
                                         l=LC)[:, :, 509:512]:
                    e.tensor_copy(out=o, in_=i))
            else:
                run("vector", lambda e,
                    o=snxt[:].rearrange("p (a k) -> p a k", k=4)[:, :, 0:3],
                    i=ucbuf[:].rearrange("p (a l) -> p a l",
                                         l=LC)[:, :, 0:3]:
                    e.tensor_copy(out=o, in_=i))
            silu_tok = 0
            silu_toks = []
            for grp in range(4):
                if grp > 0:
                    wait("vector", s_act, silu_tok)
                for j0 in (0, 2):
                    opsA = conv_ops(grp * 4 + j0, j0)
                    opsB = conv_ops(grp * 4 + j0 + 1, j0 + 1)
                    run("vector", opsA[0])
                    run("vector", opsB[0])
                    for lev in range(3):
                        run("vector", lambda e: e.drain())
                        run("vector", opsA[1 + 2 * lev])
                        run("vector", opsA[2 + 2 * lev])
                        run("vector", opsB[1 + 2 * lev])
                        run("vector", opsB[2 + 2 * lev])
                run("vector", lambda e: e.drain(), inc=(s_dve, 1))
                wait("scalar", s_dve)
                run("scalar", lambda e,
                    o=ucbuf[:, grp * 2048:(grp + 1) * 2048], i=scr2k[:]:
                    e.activation(out=o, in_=i, func=AF.Silu), inc=(s_act, 1))
                silu_tok = s_act.n
                silu_toks.append(s_act.n)
                z_quartet(ci, grp)
            return silu_toks

        def xproj_D(ci, silu_toks):
            for i in range(3):
                wait_bank("tensor", i)
            for grp in range(4):
                wait("tensor", s_act, silu_toks[grp])
                for kc in range(grp * 4, grp * 4 + 4):
                    for (j0, w_, dst_part) in ((0, 64, pA[0:64, 0:512]),
                                               (64, 16, pA[0:16, 512:1024]),
                                               (80, 16, pA[0:16, 1024:1536])):
                        run("tensor", lambda e, b=dst_part,
                            lh=xprojbf[:, kc * 96 + j0: kc * 96 + j0 + w_],
                            rh=ucbuf[:, kc * LC:(kc + 1) * LC],
                            st=(kc == 0), sp=(kc == NB - 1):
                            e.matmul(b, lh, rh, start=st, stop=sp),
                            inc=(s_te, 1) if kc == NB - 1 and j0 == 80
                            else None)
            wait("vector", s_te)
            run("vector", lambda e: e.tensor_copy(out=dtbf[:],
                                                  in_=pA[0:64, 0:512]))
            run("vector", lambda e: e.tensor_copy(out=dblB[:],
                                                  in_=pA[0:16, 512:1024]))
            run("vector", lambda e: e.tensor_copy(out=dblC[:],
                                                  in_=pA[0:16, 1024:1536]),
                inc=(s_dve, 1))
            for i in range(3):
                bank_tok[i] = (s_dve, s_dve.n)
            return s_dve.n

        def dtproj_E(ci):
            dw = dtwbf_
            dtb = cols[f"dtb_{d}"]
            wait("tensor", s_dve)
            for grp in range(4):
                bb = 4 - 4 * (grp % 2)
                pp = pB if grp % 2 == 0 else pA
                for j in range(4):
                    db = grp * 4 + j
                    wait_bank("tensor", bb + j)
                    run("tensor", lambda e, b=pp[:, j * 512:(j + 1) * 512],
                        lh=dw[:, db * 128:(db + 1) * 128], rh=dtbf[:]:
                        e.matmul(b, lh, rh, start=True, stop=True),
                        inc=(s_te, 1) if j == 3 else None)
                wait("scalar", s_te)
                wait("scalar", s_dve)
                run("scalar", lambda e: e.drain())
                for j in range(4):
                    db = grp * 4 + j
                    bk = pp[:, j * 512:(j + 1) * 512]
                    run("scalar", lambda e, o=scr2k[:, j * 512:(j + 1) * 512],
                        i=bk, b=dtb[:, db:db + 1]:
                        e.activation(out=o, in_=i, func=AF.Abs, bias=b))
                    run("scalar", lambda e, o=dT[:, db * LC:(db + 1) * LC],
                        i=bk, b=dtb[:, db:db + 1]:
                        e.activation(out=o, in_=i, func=AF.Relu, bias=b),
                        inc=(s_act, 1) if j == 3 else None)
                for j in range(4):
                    bank_tok[bb + j] = (s_act, s_act.n)
                run("scalar", lambda e: e.drain())
                run("scalar", lambda e: e.activation(
                    out=scr2k[:], in_=scr2k[:], func=AF.Exp, scale=-1.0))
                run("scalar", lambda e: e.drain())
                run("scalar", lambda e: e.activation(
                    out=scr2k[:], in_=scr2k[:], func=AF.Ln, bias=1.0),
                    inc=(s_act, 1))
                wait("vector", s_act)
                run("vector", lambda e,
                    o=dT[:, grp * 2048:(grp + 1) * 2048], b=scr2k[:]:
                    e.tensor_tensor(out=o, in0=o, in1=b, op=ALU.add),
                    inc=(s_dve, 1))
            tok["scr_act"] = s_act.n
            tok["scr_dve"] = s_dve.n

        def qinit_early(ci):
            run("vector", lambda e: e.drain())
            wait("vector", s_act)
            dpc = cols[f"dpcol_{d}"]
            for db in range(NB):
                run("vector", lambda e, o=ubq[:, db * LC:(db + 1) * LC],
                    i=ucbuf[:, db * LC:(db + 1) * LC], s1=dpc[:, db:db + 1]:
                    e.tensor_scalar(out=o, in0=i, scalar1=s1, scalar2=None,
                                    op0=ALU.mult))

        def qinit_F(ci):
            run("vector", lambda e: e.drain())
            run("vector", lambda e: e.tensor_tensor(
                out=dub[:], in0=dT[:], in1=ucbuf[:], op=ALU.mult),
                inc=(s_dve, 1))
            tok["q_ready"] = s_dve.n
            last_q[0] = s_dve.n
            return s_dve.n

        def y_evac(c):
            wait("scalar", s_te)
            wait("scalar", s_st)
            for mb in range(NMB):
                run("scalar", lambda e,
                    o=ystg[:, mb * 512:(mb + 1) * 512],
                    i=bank(mb):
                    e.activation(out=o, in_=i, func=AF.Copy),
                    inc=(s_act, 1) if mb == NMB - 1 else None)
            for i in range(8):
                bank_tok[i] = (s_act, s_act.n)
            wait("sync", s_act)
            dma("sync",
                yscr[d].rearrange("a p l -> p a l")[:, :, c * LC:(c + 1) * LC],
                ystg[:].rearrange("p (a l) -> p a l", l=512), s_st)

        for b in front_bundles(0):
            b()
        prev_c = None
        for ci, c in enumerate(chunks):
            if prev_c is not None:
                y_evac(prev_c)
            silu_toks = emit_conv_z(ci)
            if STOP == "uc":
                dump(ucbuf[:], NB * LC)
                return True
            qinit_early(ci)
            dbl_tok = xproj_D(ci, silu_toks)
            dtproj_E(ci)
            if STOP == "dT":
                dump(dT[:], NB * LC)
                return True
            q_ready = qinit_F(ci)
            fillers = front_bundles(ci + 1) if ci + 1 < NCH else []
            yield_scan(d, rev, c, q_ready, dbl_tok, fillers)
            if STOP == "hq":
                dump(ubq[:], NB * LC)
                return True
            prev_c = c
        y_evac(prev_c)
        return False

    def yield_scan(d, rev, c, q_ready, dbl_tok, fillers):
        ac = acol
        scan_tok = {0: 0, 1: 0}
        act_tok = {0: 0, 1: 0}
        sm_tok = {0: 0}
        te_tok = {}
        run("scalar", lambda e: e.drain())
        grp_dve_end = {}

        def emit_pats(g):
            gp = g % 2
            if g - 2 in grp_dve_end:
                wait("tensor", s_dve, grp_dve_end[g - 2])
            else:
                wait("tensor", s_dve)
            wait("tensor", s_act)
            for i in range(8):
                wait_bank("tensor", i)
            for j in range(4):
                n = g * 4 + j
                run("tensor", lambda e, b=pA[:, j * 512:(j + 1) * 512],
                    lh=esel_sb[:, n * 128:(n + 1) * 128], rh=dblB[:]:
                    e.matmul(b, lh, rh, start=True, stop=True))
                run("tensor", lambda e, b=pB[:, j * 512:(j + 1) * 512],
                    lh=esel_sb[:, n * 128:(n + 1) * 128], rh=dblC[:]:
                    e.matmul(b, lh, rh, start=True, stop=True),
                    inc=(s_te, 1) if j == 3 else None)
            wait("scalar", s_te)
            run("scalar", lambda e, o=bpat[gp][:]: e.activation(
                out=o, in_=pA[:], func=AF.Copy))
            run("scalar", lambda e, o=cpat[gp][:]: e.activation(
                out=o, in_=pB[:], func=AF.Copy), inc=(s_act, 1))
            for i in range(8):
                bank_tok[i] = (s_act, s_act.n)
            return ("act", s_act.n)

        pat_tok_next = emit_pats(0)
        for g in range(4):
            gp = g % 2
            pat_tok = pat_tok_next
            bpat_g = bpat[gp]
            cpat_g = cpat[gp]
            def emit_A(db):
                h = hg4[db % 2]
                sw = swide[db % 2]
                items = []
                smfn = (lambda e,
                        i0=dub[:, db * LC:(db + 1) * LC]
                        .rearrange("p (o l) -> p o l", o=1)
                        .broadcast_to((128, 4, LC)),
                        i1=bpat_g[:].rearrange("p (o l) -> p o l", l=LC),
                        o=sw[:].rearrange("p (o l) -> p o l", l=LC):
                        e.tensor_tensor(out=o, in0=i0, in1=i1, op=ALU.mult))
                items.append((None, smfn,
                              ("sm", db % 2, pat_tok if db == 0 else 0,
                               q_ready if (g == 0 and db == 0) else 0)))
                for j in range(4):
                    n = g * 4 + j
                    buf = j % 2
                    actfn = (lambda e, o=dab[buf][:],
                             i=dT[:, db * LC:(db + 1) * LC],
                             s=ac[:, db * NST + n: db * NST + n + 1]:
                             e.activation(out=o, in_=i, func=AF.Exp, scale=s))
                    items.append((None, actfn, ("act", buf)))
                    if not rev:
                        scfn = (lambda e, o=h[:, j * 512:(j + 1) * 512],
                                d0=dab[buf][:],
                                d1=sw[:, j * 512:(j + 1) * 512],
                                i0=carry[:, db * NST + n: db * NST + n + 1]:
                                e.tensor_tensor_scan(out=o, data0=d0, data1=d1,
                                                     initial=i0, op0=ALU.mult,
                                                     op1=ALU.add))
                    else:
                        scfn = (lambda e,
                                o=h[:, j * 512:(j + 1) * 512][:, ::-1],
                                d0=dab[buf][:, ::-1],
                                d1=sw[:, j * 512:(j + 1) * 512][:, ::-1],
                                i0=carry[:, db * NST + n: db * NST + n + 1]:
                                e.tensor_tensor_scan(out=o, data0=d0, data1=d1,
                                                     initial=i0, op0=ALU.mult,
                                                     op1=ALU.add))
                    items.append((None, scfn, ("scan", buf, db % 2, j == 0)))
                return items

            def emit_T(db):
                h = hg4[db % 2]
                cstart = 0 if rev else LC - 1
                ops = []
                ops.append((("dve", lambda: max(scan_tok[1],
                                                tloc.get("qa", 0))),
                            lambda e, hh=h, cp=cpat_g: e.tensor_tensor(
                                out=tmp2k[:], in0=hh[:], in1=cp[:],
                                op=ALU.mult), ("tok", "cm")))
                ops.append((None, lambda e,
                            o=carry[:, db * NST + g * 4: db * NST + g * 4 + 4],
                            i=h[:].rearrange("p (a l) -> p a l", l=LC)
                            [:, :, cstart:cstart + 1]
                            .rearrange("p a k -> p (a k)"):
                            e.tensor_copy(out=o, in_=i), None))
                ops.append((("dve", lambda: tloc["cm"]),
                            lambda e: e.tensor_tensor(
                                out=tmp2k[:, 0:1024], in0=tmp2k[:, 0:1024],
                                in1=tmp2k[:, 1024:2048], op=ALU.add),
                            ("tok", "a1")))
                ops.append((("dve", lambda: tloc["a1"]),
                            lambda e: e.tensor_tensor(
                                out=tmp2k[:, 0:512], in0=tmp2k[:, 0:512],
                                in1=tmp2k[:, 512:1024], op=ALU.add),
                            ("tok", "a2")))
                ops.append((("dve", lambda: tloc["a2"]),
                            lambda e, o=ubq[:, db * LC:(db + 1) * LC]:
                            e.tensor_tensor(out=o, in0=o, in1=tmp2k[:, 0:512],
                                            op=ALU.add), ("tok", "qa")))
                if g == 3:
                    par = db % 2
                    ops.append((("te2", db - 2 if db >= 2 else None,
                                 "dve", lambda: tloc["qa"]),
                                lambda e, o=y2b[par][:],
                                i0=ubq[:, db * LC:(db + 1) * LC],
                                i1=zbuf[:, db * LC:(db + 1) * LC]:
                                e.tensor_tensor(out=o, in0=i0, in1=i1,
                                                op=ALU.mult), ("y2", db)))
                return ops

            tloc = {}

            def flush(items):
                for item in items:
                    pre, fn, tag = item
                    if tag is not None and tag[0] == "act":
                        buf = tag[1]
                        wait("scalar", s_dve, scan_tok[buf])
                        run("scalar", fn, inc=(s_act, 1))
                        act_tok[buf] = s_act.n
                        continue
                    if tag is not None and tag[0] == "sm":
                        par, ptok, qtok = tag[1], tag[2], tag[3]
                        if ptok:
                            wait("vector", s_act, ptok[1])
                        run("vector", fn, inc=(s_dve, 1))
                        sm_tok[0] = s_dve.n
                        continue
                    if pre is not None:
                        if pre[0] == "dve":
                            wait("vector", s_dve, pre[1]())
                        elif pre[0] == "te2":
                            if pre[1] is not None:
                                wait("vector", s_te, te_tok[pre[1]])
                            wait("vector", s_dve, pre[3]())
                    if tag is not None and tag[0] == "scan":
                        buf, par, first = tag[1], tag[2], tag[3]
                        wait("vector", s_act, act_tok[buf])
                        if first:
                            wait("vector", s_dve, sm_tok[0])
                        run("vector", fn, inc=(s_dve, 1))
                        scan_tok[buf] = s_dve.n
                    elif tag is not None and tag[0] == "tok":
                        run("vector", fn, inc=(s_dve, 1))
                        tloc[tag[1]] = s_dve.n
                    elif tag is not None and tag[0] == "y2":
                        run("vector", fn, inc=(s_dve, 1))
                        emit_outproj(tag[1], s_dve.n)
                    else:
                        run("vector", fn)

            def interleave(a_items, t_items):
                out = []
                na, nt = len(a_items), len(t_items)
                ai = ti = 0
                while ai < na or ti < nt:
                    if ai < na:
                        out.append(a_items[ai]); ai += 1
                        if ai < na:
                            out.append(a_items[ai]); ai += 1
                    if ti < nt:
                        out.append(t_items[ti]); ti += 1
                return out

            def emit_outproj(db, y2_tok):
                par = db % 2
                psem = s_wsA if par == 0 else s_wsB
                if db == 0:
                    wait("sync", s_te)
                    dma("sync", outstg[0][:], outbf[d][0:128, :], s_wsA)
                    for i in range(8):
                        wait_bank("tensor", i)
                wait("tensor", psem)
                wait("tensor", s_dve, y2_tok)
                for mb in range(NMB):
                    run("tensor", lambda e, b=bank(mb),
                        lh=outstg[par][:, mb * 128:(mb + 1) * 128],
                        rh=y2b[par][:], st=(db == 0), sp=(db == NB - 1):
                        e.matmul(b, lh, rh, start=st, stop=sp),
                        inc=(s_te, 1) if mb == NMB - 1 else None)
                te_tok[db] = s_te.n
                if db < NB - 1:
                    if db >= 1:
                        wait("sync", s_te, te_tok[db - 1])
                    dma("sync", outstg[(db + 1) % 2][:],
                        outbf[d][(db + 1) * 128:(db + 2) * 128, :],
                        s_wsB if (db + 1) % 2 else s_wsA)

            prevT = None
            for db in range(NB):
                a_items = emit_A(db)
                if prevT is None:
                    flush(a_items)
                else:
                    flush(interleave(a_items, prevT))
                prevT = emit_T(db)
                if db == 1 and g < 3:
                    pat_tok_next = emit_pats(g + 1)
                if g < 3 and fillers:
                    fillers.pop(0)()
            flush(prevT)
            grp_dve_end[g] = s_dve.n

    bpat = [sbt((128, 4 * LC), BF16, ph2) for _ in range(2)]
    cpat = [sbt((128, 4 * LC), BF16, ph2) for _ in range(2)]
    hg4 = [sbt((128, 4 * LC), BF16, ph2) for _ in range(2)]
    dab = [sbt((128, LC), F32, ph2) for _ in range(2)]
    swide = [sbt((128, 4 * LC), BF16, ph2) for _ in range(2)]
    tmp2k = sbt((128, 2048), BF16, ph2)
    y2b = [sbt((128, LC), BF16, ph2) for _ in range(2)]

    stopped = False
    for d in ("f", "b"):
        with contextlib.ExitStack() as dscope:
            xprojbf_ = sbt((128, NB * 96), BF16, dscope)
            dtwbf_ = sbt((64, NB * 128), BF16, dscope)
            ucbuf = sbt((128, NB * LC), BF16, dscope)
            stopped = mamba_dir(d, (xprojbf_, dtwbf_, ucbuf, dscope))
            if stopped:
                return nc, P, stack
        if STOP == f"y_{d}":
            dump(ystg[:, 0:4096], 4096)
            return nc, P, stack
    ph2.close()

    ph3 = stack.enter_context(contextlib.ExitStack())
    ystg3 = {d: sbt((128, NMB * 512), BF16, ph3) for d in ("f", "b")}
    xstg3 = sbt((128, DM), F32, ph3)
    gbuf = sbt((128, NMB * LC), BF16, ph3)
    comb = sbt((128, NMB * LC), BF16, ph3)
    tbuf = sbt((128, NMB * LC), BF16, ph3)
    xm = sbt((128, NMB * LC), F32, ph3)
    x1 = sbt((128, NMB * LC), F32, ph3)
    x1r = sbt((128, NMB * LC), F32R, ph3)
    owide = sbt((128, NMB * LC), F32, ph3)
    mstg3 = [sbt((128, DM), F32R, ph3) for _ in range(2)]
    xTf = sbt((128, NMB * LC), F32, ph3)
    r1 = sbt((128, NMB * LC), F32, ph3)
    rows = sbt((1, 1024), F32, ph3)
    ostg = sbt((128, DM), F32, ph3)
    wstg = [sbt((128, DM), BF16, ph3) for _ in range(3)]
    wstgf = [sbt((128, DM), F32, ph3) for _ in range(3)]
    wsem = [s_w4[0], s_w4[1], s_ldC]

    def ln_block(src, xm, sqw, dst, src_toks=None):
        if src_toks is None:
            wait("tensor", s_dve)
        wait_bank("tensor", 0)
        for mb in range(NMB):
            if src_toks is not None:
                wait("tensor", s_dve, src_toks[mb])
            run("tensor", lambda e, b=pA[0:1, 0:512],
                lh=onec_sb[:], rh=src[:, mb * LC:(mb + 1) * LC],
                st=(mb == 0), sp=(mb == NMB - 1):
                e.matmul(b, lh, rh, start=st, stop=sp),
                inc=(s_te, 1) if mb == NMB - 1 else None)
        wait("vector", s_te)
        run("vector", lambda e: e.tensor_scalar(
            out=rows[0:1, 0:512], in0=pA[0:1, 0:512], scalar1=1.0 / DM,
            scalar2=None, op0=ALU.mult), inc=(s_dve, 1))
        bank_tok[0] = (s_dve, s_dve.n)
        wait("tensor", s_dve)
        wait_bank("tensor", 4)
        run("tensor", lambda e: e.matmul(pB[:, 0:512], oner_sb[:],
                                         rows[0:1, 0:512], start=True,
                                         stop=True), inc=(s_te, 1))
        wait("vector", s_te)
        for mb in range(NMB):
            run("vector", lambda e, o=xm[:, mb * LC:(mb + 1) * LC],
                a=src[:, mb * LC:(mb + 1) * LC], b=pB[:, 0:512]:
                e.tensor_tensor(out=o, in0=a, in1=b, op=ALU.subtract))
        run("vector", lambda e: e.drain())
        for mb in range(NMB):
            run("vector", lambda e, o=sqw[:, mb * LC:(mb + 1) * LC],
                a=xm[:, mb * LC:(mb + 1) * LC]:
                e.tensor_tensor(out=o, in0=a, in1=a, op=ALU.mult),
                inc=(s_dve, 1) if mb == NMB - 1 else None)
        bank_tok[4] = (s_dve, s_dve.n)
        wait("tensor", s_dve)
        wait_bank("tensor", 1)
        for mb in range(NMB):
            run("tensor", lambda e, b=pA[0:1, 512:1024],
                lh=onec_sb[:], rh=sqw[:, mb * LC:(mb + 1) * LC],
                st=(mb == 0), sp=(mb == NMB - 1):
                e.matmul(b, lh, rh, start=st, stop=sp),
                inc=(s_te, 1) if mb == NMB - 1 else None)
        wait("vector", s_te)
        run("vector", lambda e: e.tensor_scalar(
            out=rows[0:1, 512:1024], in0=pA[0:1, 512:1024], scalar1=1.0 / DM,
            scalar2=None, op0=ALU.mult), inc=(s_dve, 1))
        bank_tok[1] = (s_dve, s_dve.n)
        wait("scalar", s_dve)
        run("scalar", lambda e: e.activation(out=rows[0:1, 512:1024],
                                             in_=rows[0:1, 512:1024],
                                             func=AF.Ln, bias=epsc[0:1, 0:1]))
        run("scalar", lambda e: e.drain())
        run("scalar", lambda e: e.activation(out=rows[0:1, 512:1024],
                                             in_=rows[0:1, 512:1024],
                                             func=AF.Exp, scale=-0.5),
            inc=(s_act, 1))
        wait("tensor", s_act)
        wait_bank("tensor", 5)
        run("tensor", lambda e: e.matmul(pB[:, 512:1024], oner_sb[:],
                                         rows[0:1, 512:1024], start=True,
                                         stop=True), inc=(s_te, 1))
        wait("vector", s_te)
        dst_toks = []
        for mb in range(NMB):
            run("vector", lambda e, o=dst[:, mb * LC:(mb + 1) * LC],
                a=xm[:, mb * LC:(mb + 1) * LC], b=pB[:, 512:1024]:
                e.tensor_tensor(out=o, in0=a, in1=b, op=ALU.mult),
                inc=(s_dve, 1))
            dst_toks.append(s_dve.n)
        bank_tok[5] = (s_dve, s_dve.n)
        return dst_toks

    wait("sync", s_st)
    def issue_ystg(c, dve_tok, te_tok):
        if dve_tok:
            wait("sync", s_dve, dve_tok)
        if te_tok:
            wait("sync", s_te, te_tok)
        for d in ("f", "b"):
            dma("sync", ystg3[d][:].rearrange("p (a l) -> p a l", l=512),
                yscr[d].rearrange("a p l -> p a l")[:, :, c * LC:(c + 1) * LC],
                s_ldD)
        return s_ldD.n

    ystg_tok = {}
    wait("sync", s_te)
    wait("sync", s_dve)
    ystg_tok[0] = issue_ystg(0, 0, 0)
    for c in range(NCH):
        wait("sync", s_te)
        wait("sync", s_dve)
        emit_xpose(c, xstg3[:], xTf)
        if True:
            wdma_tok = {}
            wcast_tok = {}
            wte_tok = {}

            def w_dma(kc, src_d):
                par = kc % 3
                if kc - 3 in wcast_tok:
                    wait("sync", s_act, wcast_tok[kc - 3])
                dma("sync", wstgf[par][:],
                    src_d[kc * 128:(kc + 1) * 128, :], wsem[par])
                wdma_tok[kc] = wsem[par].n

            def w_cast(kc):
                par = kc % 3
                wait("scalar", wsem[par], wdma_tok[kc])
                if kc - 3 in wte_tok:
                    wait("scalar", s_te, wte_tok[kc - 3])
                run("scalar", lambda e, o=wstg[par][:], i=wstgf[par][:]:
                    e.activation(out=o, in_=i, func=AF.Copy),
                    inc=(s_act, 1))
                wcast_tok[kc] = s_act.n

            wait("sync", s_te)
            for kc in range(3):
                w_dma(kc, dram["gate_w"])
                w_cast(kc)
            for kc in range(16):
                par = kc % 3
                wait("tensor", s_act, wcast_tok[kc])
                if kc == 0:
                    for i in range(8):
                        wait_bank("tensor", i)
                    wait("tensor", s_ldD, ystg_tok[c])
                ysrc = ystg3["f"] if kc < 8 else ystg3["b"]
                ycol = (kc % 8) * 512
                for mb in range(NMB):
                    run("tensor", lambda e, b=bank(mb),
                        lh=wstg[par][:, mb * 128:(mb + 1) * 128],
                        rh=ysrc[:, ycol:ycol + LC],
                        st=(kc == 0), sp=(kc == 15):
                        e.matmul(b, lh, rh, start=st, stop=sp),
                        inc=(s_te, 1) if mb == NMB - 1 else None)
                wte_tok[kc] = s_te.n
                if kc + 3 < 16:
                    w_dma(kc + 3, dram["gate_w"])
                    w_cast(kc + 3)
            gate_fin = wte_tok[15]
            wait("scalar", s_te)
            for mb in range(NMB):
                run("scalar", lambda e, o=gbuf[:, mb * LC:(mb + 1) * LC],
                    i=bank(mb), b=cols["gate_bc"][:, mb:mb + 1]:
                    e.activation(out=o, in_=i, func=AF.Sigmoid, bias=b),
                    inc=(s_act, 1) if mb == NMB - 1 else None)
            for i in range(8):
                bank_tok[i] = (s_act, s_act.n)
            wait("vector", s_act)
            for mb in range(NMB):
                yf = ystg3["f"][:, mb * 512:(mb + 1) * 512]
                yb = ystg3["b"][:, mb * 512:(mb + 1) * 512]
                run("vector", lambda e, a=yf, b=yb,
                    o=tbuf[:, mb * LC:(mb + 1) * LC]:
                    e.tensor_tensor(out=o, in0=a, in1=b, op=ALU.subtract))
            run("vector", lambda e: e.drain())
            for mb in range(NMB):
                run("vector", lambda e, g=gbuf[:, mb * LC:(mb + 1) * LC],
                    o=tbuf[:, mb * LC:(mb + 1) * LC]:
                    e.tensor_tensor(out=o, in0=g, in1=o, op=ALU.mult))
            run("vector", lambda e: e.drain())
            comb_tok = []
            for mb in range(NMB):
                yb = ystg3["b"][:, mb * 512:(mb + 1) * 512]
                run("vector", lambda e, o=comb[:, mb * LC:(mb + 1) * LC],
                    b=yb, t=tbuf[:, mb * LC:(mb + 1) * LC]:
                    e.tensor_tensor(out=o, in0=b, in1=t, op=ALU.add),
                    inc=(s_dve, 1))
                comb_tok.append(s_dve.n)
            for mb in range(NMB):
                run("vector", lambda e, o=r1[:, mb * LC:(mb + 1) * LC],
                    i=xTf[:, mb * LC:(mb + 1) * LC],
                    s1=cols["proj_bc"][:, mb:mb + 1]:
                    e.tensor_scalar(out=o, in0=i, scalar1=s1, scalar2=None,
                                    op0=ALU.add))
            run("vector", lambda e: e.drain(), inc=(s_dve, 1))
            r1h_tok = s_dve.n
            wdma_tok = {}
            wcast_tok = {}
            wte_tok = {}
            wait("sync", s_te)
            for kc in range(3):
                w_dma(kc, dram["proj_w"])
                w_cast(kc)
            for kc in range(8):
                par = kc % 3
                wait("tensor", s_act, wcast_tok[kc])
                wait("tensor", s_dve, comb_tok[kc])
                if kc == 0:
                    for i in range(8):
                        wait_bank("tensor", i)
                for mb in range(NMB):
                    run("tensor", lambda e, b=bank(mb),
                        lh=wstg[par][:, mb * 128:(mb + 1) * 128],
                        rh=comb[:, kc * LC:(kc + 1) * LC],
                        st=(kc == 0), sp=(kc == 7):
                        e.matmul(b, lh, rh, start=st, stop=sp),
                        inc=(s_te, 1) if mb == NMB - 1 else None)
                wte_tok[kc] = s_te.n
                if kc + 3 < 8:
                    w_dma(kc + 3, dram["proj_w"])
                    w_cast(kc + 3)
            proj_fin = wte_tok[7]
            wait("vector", s_te, proj_fin)
            r1_tok = []
            for mb in range(NMB):
                run("vector", lambda e, o=r1[:, mb * LC:(mb + 1) * LC],
                    b=bank(mb):
                    e.tensor_tensor(out=o, in0=o, in1=b, op=ALU.add),
                    inc=(s_dve, 1))
                r1_tok.append(s_dve.n)
            for i in range(8):
                bank_tok[i] = (s_dve, s_dve.n)
        if STOP == "r1":
            dump(r1[:], NMB * LC)
            return nc, P, stack
        if True:
            x1_toks = ln_block(r1[:], xm[:], owide[:], x1[:],
                               src_toks=r1_tok)
            x1r_tok = []
            for mb in range(NMB):
                wait("vector", s_dve, x1_toks[mb])
                run("vector", lambda e,
                    o=x1r[:, mb * LC:(mb + 1) * LC],
                    i=x1[:, mb * LC:(mb + 1) * LC]:
                    e.tensor_copy(out=o, in_=i), inc=(s_dve, 1))
                x1r_tok.append(s_dve.n)
            def m_dma(kc, tokd):
                par = kc % 2
                dma("sync", mstg3[par][:],
                    mscr[kc * 128:(kc + 1) * 128, :], s_w4[2 + par])
                tokd[kc] = s_w4[2 + par].n

            wdma_tok = {}
            wte_tok = {}
            wait("sync", s_te)
            m_dma(0, wdma_tok)
            m_dma(1, wdma_tok)
            for kc in range(8):
                par = kc % 2
                wait("tensor", s_w4[2 + par], wdma_tok[kc])
                wait("tensor", s_dve, x1r_tok[kc])
                if kc == 0:
                    for i in range(8):
                        wait_bank("tensor", i)
                for mb in range(NMB):
                    run("tensor", lambda e, b=bank(mb),
                        lh=mstg3[par][:, mb * 128:(mb + 1) * 128],
                        rh=x1r[:, kc * LC:(kc + 1) * LC],
                        st=(kc == 0), sp=(kc == 7):
                        e.matmul(b, lh, rh, start=st, stop=sp),
                        inc=(s_te, 1) if mb == NMB - 1 else None)
                wte_tok[kc] = s_te.n
                if kc + 2 < 8:
                    wait("sync", s_te, wte_tok[kc])
                    m_dma(kc + 2, wdma_tok)
            ffn_fin = wte_tok[7]
            if c + 1 < NCH:
                ystg_tok[c + 1] = issue_ystg(c + 1, comb_tok[-1], gate_fin)
            for mb in range(NMB):
                run("vector", lambda e, o=r1[:, mb * LC:(mb + 1) * LC],
                    i=x1[:, mb * LC:(mb + 1) * LC],
                    s1=cols["fbias_c"][:, mb:mb + 1]:
                    e.tensor_scalar(out=o, in0=i, scalar1=s1, scalar2=None,
                                    op0=ALU.add))
            run("vector", lambda e: e.drain())
            wait("vector", s_te, ffn_fin)
            r2_tok = []
            for mb in range(NMB):
                run("vector", lambda e, o=r1[:, mb * LC:(mb + 1) * LC],
                    b=bank(mb):
                    e.tensor_tensor(out=o, in0=o, in1=b, op=ALU.add),
                    inc=(s_dve, 1))
                r2_tok.append(s_dve.n)
            for i in range(8):
                bank_tok[i] = (s_dve, s_dve.n)
            ow_toks = ln_block(r1[:], xm[:], x1[:], owide[:],
                               src_toks=r2_tok)
            wait("tensor", s_dve, ow_toks[-1])
            for lt in range(4):
                if lt > 0:
                    wait("tensor", s_dve)
                for i in range(8):
                    wait_bank("tensor", i)
                for mb in range(NMB):
                    run("tensor", lambda e, o=bank(mb)[:, 0:128],
                        i=owide[:, mb * LC + lt * 128:
                                mb * LC + lt * 128 + 128]:
                        e.transpose(o, i, eye_sb[:]),
                        inc=(s_te, 1) if mb == NMB - 1 else None)
                wait("vector", s_te)
                wait("vector", s_st)
                for mb in range(NMB):
                    run("vector", lambda e,
                        o=ostg[:, mb * 128:(mb + 1) * 128],
                        i=bank(mb)[:, 0:128]:
                        e.tensor_copy(out=o, in_=i),
                        inc=(s_dve, 1) if mb == NMB - 1 else None)
                for i in range(8):
                    bank_tok[i] = (s_dve, s_dve.n)
                wait("sync", s_dve)
                dma("sync",
                    out_d[c * LC + lt * 128: c * LC + (lt + 1) * 128, :],
                    ostg[:], s_st)
    ph3.close()

    P.add("sync", lambda e, h=s_st.h, v=s_st.n: e.wait_ge(h, v))
    return nc, P, stack


def _emit(nc, P):
    with nc.Block() as block:
        for eng in ENGS:
            ops = P.ops[eng]
            if not ops:
                continue

            def body(e, ops=ops):
                for fn in ops:
                    fn(e)
            getattr(block, eng)(body)


_CACHE = {}


def _build():
    if "nc" not in _CACHE:
        nc, P, stack = build_nc()
        _emit(nc, P)
        stack.close()
        _CACHE["nc"] = nc
    return _CACHE["nc"]


def _host_inputs(inputs):
    f32 = np.float32
    import ml_dtypes

    def colpack(v, width):
        return np.ascontiguousarray(np.asarray(v, f32).reshape(width, 128).T)

    D = DM
    jk = np.outer(np.arange(D), np.arange(D)) % D
    ang = (2.0 * np.pi / D) * jk
    fr = np.cos(ang).astype(f32)
    fi = (-np.sin(ang)).astype(f32)

    esel = np.zeros((16, 16 * 128), dtype=np.float32)
    for n in range(16):
        esel[n, n * 128:(n + 1) * 128] = 1.0
    esel = esel.astype(ml_dtypes.bfloat16)
    base = {
        "gate_w": np.ascontiguousarray(np.asarray(inputs["L0_attn_gate_w"], f32)),
        "gate_bc": colpack(inputs["L0_attn_gate_b"], NMB),
        "proj_w": np.ascontiguousarray(np.asarray(inputs["L0_attn_proj_w"], f32)),
        "proj_bc": colpack(inputs["L0_attn_proj_b"], NMB),
        "wfr": np.ascontiguousarray(np.asarray(inputs["L0_ffn_w_freq_real"], f32)),
        "wfi": np.ascontiguousarray(np.asarray(inputs["L0_ffn_w_freq_imag"], f32)),
        "fbias_c": colpack(inputs["L0_ffn_bias"], NMB),
        "fr": fr, "fi": fi,
        "fr2": (fr / 1024.0).astype(f32), "fi2": (fi / 1024.0).astype(f32),
        "esel": esel,
        "eye": np.eye(128, dtype=f32),
        "ones_col": np.ones((128, 1), f32),
        "ones_row": np.ones((1, 128), f32),
    }
    for d in ("f", "b"):
        base[f"inproj_{d}"] = np.ascontiguousarray(
            np.asarray(inputs[f"L0_attn_inproj_{d}"], f32))
        base[f"xproj_{d}"] = np.ascontiguousarray(
            np.asarray(inputs[f"L0_attn_xproj_{d}"], f32))
        base[f"dtw_{d}"] = np.ascontiguousarray(
            np.asarray(inputs[f"L0_attn_dt_w_{d}"], f32))
        base[f"outproj_{d}"] = np.ascontiguousarray(
            np.asarray(inputs[f"L0_attn_outproj_{d}"], f32))
        cw = np.asarray(inputs[f"L0_attn_conv_w_{d}"], f32)
        base[f"convw_{d}"] = np.ascontiguousarray(
            cw.reshape(NB, 128, 4).transpose(1, 0, 2).reshape(128, NB * 4))
        base[f"convb_{d}"] = colpack(inputs[f"L0_attn_conv_b_{d}"], NB)
        base[f"dtb_{d}"] = colpack(inputs[f"L0_attn_dt_b_{d}"], NB)
        base[f"dpcol_{d}"] = colpack(inputs[f"L0_attn_Dp_{d}"], NB)
        al = np.asarray(inputs[f"L0_attn_Alog_{d}"], f32)
        base[f"alogc_{d}"] = np.ascontiguousarray(
            al.reshape(NB, 128, NST).transpose(1, 0, 2).reshape(128, NB * NST))
    x = np.asarray(inputs["L0_attn_x"], f32)
    in_maps = []
    for b in range(8):
        m = dict(base)
        m["xin"] = np.ascontiguousarray(x[b])
        in_maps.append(m)
    return in_maps


def kernel(**inputs):
    nc = _build()
    in_maps = _host_inputs(inputs)
    res = run_bass_kernel_spmd(nc, in_maps, core_ids=list(range(8)))
    return np.stack([np.asarray(r["out"], dtype=np.float32)
                     for r in res.results], axis=0)


# version 24
# speedup vs baseline: 1.0149x; 1.0001x over previous
import contextlib
import numpy as np

import concourse.bass as bass
import concourse.mybir as mybir
from concourse.bass_utils import run_bass_kernel_spmd

F32 = mybir.dt.float32
F32R = mybir.dt.float32r
BF16 = mybir.dt.bfloat16
AF = mybir.ActivationFunctionType
ALU = mybir.AluOpType

L = 2048
DM = 1024
DI = 2048
NST = 16
RNK = 64
LC = 512
NCH = L // LC
NB = DI // 128
NMB = DM // 128
KCD = DM // 128

ENGS = ("sync", "tensor", "scalar", "vector", "gpsimd")

STOP = None


class Prog:
    def __init__(self):
        self.ops = {e: [] for e in ENGS}

    def add(self, eng, fn):
        self.ops[eng].append(fn)


class Sem:
    def __init__(self, stack, nc, name):
        self.h = stack.enter_context(nc.semaphore(name))
        self.n = 0


def build_nc():
    nc = bass.Bass("TRN2", target_bir_lowering=False)
    P = Prog()
    stack = contextlib.ExitStack()
    dram = {}

    def din(name, shape, dt=F32):
        dram[name] = nc.dram_tensor(name, list(shape), dt,
                                    kind="ExternalInput").ap()
        return dram[name]

    def dscr(name, shape, dt):
        dram[name] = nc.dram_tensor(name, list(shape), dt, kind="Internal").ap()
        return dram[name]

    xin = din("xin", (L, DM))
    out_d = nc.dram_tensor("out", [L, DM], F32, kind="ExternalOutput").ap()

    for d in ("f", "b"):
        din(f"inproj_{d}", (DM, 2 * DI))
        din(f"xproj_{d}", (DI, RNK + 2 * NST))
        din(f"dtw_{d}", (RNK, DI))
        din(f"outproj_{d}", (DI, DM))
        din(f"convw_{d}", (128, NB * 4))
        din(f"convb_{d}", (128, NB))
        din(f"dtb_{d}", (128, NB))
        din(f"dpcol_{d}", (128, NB))
        din(f"alogc_{d}", (128, NB * NST))
    din("gate_w", (2 * DM, DM))
    din("gate_bc", (128, NMB))
    din("proj_w", (DM, DM))
    din("proj_bc", (128, NMB))
    din("wfr", (DM, DM), F32R)
    din("wfi", (DM, DM), F32R)
    din("fbias_c", (128, NMB))
    din("fr", (DM, DM), F32R)
    din("fi", (DM, DM), F32R)
    din("fr2", (DM, DM), F32R)
    din("fi2", (DM, DM), F32R)
    din("ones_col", (128, 1), F32)
    din("ones_row", (1, 128), F32)
    din("esel", (16, 16 * 128), BF16)
    din("eye", (128, 128))

    inpbf = {d: dscr(f"inpbf_{d}", (2 * NB, KCD, 128, 128), BF16)
             for d in ("f", "b")}
    outbf = {d: dscr(f"outbf_{d}", (DI, DM), BF16) for d in ("f", "b")}
    mscr = dscr("mscr", (DM, DM), F32R)
    yscr = {d: dscr(f"yscr_{d}", (NMB, 128, L), BF16) for d in ("f", "b")}

    s_in = Sem(stack, nc, "s_in")
    s_ws = Sem(stack, nc, "s_ws")
    s_st = Sem(stack, nc, "s_st")
    s_te = Sem(stack, nc, "s_te")
    s_act = Sem(stack, nc, "s_act")
    s_dve = Sem(stack, nc, "s_dve")
    s_dmp = Sem(stack, nc, "s_dmp")
    s_wsA = Sem(stack, nc, "s_wsA")
    s_wsB = Sem(stack, nc, "s_wsB")
    s_gp = Sem(stack, nc, "s_gp")
    s_w4 = [Sem(stack, nc, f"s_w4{i}") for i in range(4)]
    s_ldC = Sem(stack, nc, "s_ldC")
    s_ldD = Sem(stack, nc, "s_ldD")
    s_stC = Sem(stack, nc, "s_stC")

    def wait(eng, sem, val=None):
        v = sem.n if val is None else val
        if v > 0:
            P.add(eng, lambda e, h=sem.h, v=v: e.wait_ge(h, v))

    def run(eng, fn, inc=None):
        if inc is None:
            P.add(eng, fn)
            return None
        sem, k = inc
        sem.n += k
        P.add(eng, lambda e, fn=fn, h=sem.h, k=k: fn(e).then_inc(h, k))
        return sem.n

    def dma(eng, out, in_, sem):
        return run(eng, lambda e, o=out, i=in_: e.dma_start(out=o, in_=i),
                   inc=(sem, 16))

    _names = [0]

    def sbt(shape, dt, scope=None):
        _names[0] += 1
        return (scope or stack).enter_context(
            nc.sbuf_tensor(f"t{_names[0]}", list(shape), dt))

    pA = stack.enter_context(nc.psum_tensor("pA", [128, 2048], F32))
    pB = stack.enter_context(nc.psum_tensor("pB", [128, 2048], F32))

    def bank(i):
        p = pA if i < 4 else pB
        return p[:, (i % 4) * 512:(i % 4) * 512 + 512]

    bank_tok = [None] * 8

    def wait_bank(eng, i):
        t = bank_tok[i]
        if t is not None:
            wait(eng, t[0], t[1])

    out_wide = out_d.rearrange("(a p) m -> p a m", p=128)

    def dump(src_ap, width):
        wait("gpsimd", s_dve)
        wait("gpsimd", s_act)
        wait("gpsimd", s_te)
        for i in range(width // 1024):
            dma("gpsimd", out_wide[:, i, :],
                src_ap[:, i * 1024:(i + 1) * 1024], s_dmp)
        wait("sync", s_dmp)

    cols = {}
    for d in ("f", "b"):
        for nm, w in (("convw", NB * 4), ("convb", NB), ("dtb", NB),
                      ("dpcol", NB), ("alogc", NB * NST)):
            cols[f"{nm}_{d}"] = sbt((128, w), F32)
    for nm in ("gate_bc", "proj_bc", "fbias_c"):
        cols[nm] = sbt((128, NMB), F32)
    onec_sb = sbt((128, 1), F32)
    oner_sb = sbt((1, 128), F32)
    esel_sb = sbt((16, 16 * 128), BF16)
    eye_sb = sbt((128, 128), F32)
    acol = sbt((128, NB * NST), F32)
    carry = sbt((128, NB * NST), F32)
    stash = [sbt((128, NB * 4), BF16) for _ in range(2)]
    junk = sbt((1, 8), F32)
    epsc = sbt((128, 1), F32)

    for d in ("f", "b"):
        for nm in ("convw", "convb", "dtb", "dpcol", "alogc"):
            dma("sync", cols[f"{nm}_{d}"][:], dram[f"{nm}_{d}"][:, :], s_in)
    for nm in ("gate_bc", "proj_bc", "fbias_c"):
        dma("sync", cols[nm][:], dram[nm][:, :], s_in)
    dma("sync", onec_sb[:], dram["ones_col"][:, :], s_in)
    dma("sync", esel_sb[:], dram["esel"][:, :], s_in)
    dma("sync", oner_sb[:], dram["ones_row"][:, :], s_in)
    dma("sync", eye_sb[:], dram["eye"][:, :], s_in)
    wait("scalar", s_in)
    run("vector", lambda e: e.memset(epsc[:], 1e-5), inc=(s_dve, 1))

    ph1 = stack.enter_context(contextlib.ExitStack())
    if True:
        stgf = [sbt((128, 2048), F32, ph1) for _ in range(2)]
        stgb = [sbt((128, 2048), BF16, ph1) for _ in range(2)]
        slabs = []

        def cast_stream(src_row_fn, dst_ap_fn, nslab, colw, src3d_fn=None):
            for kc in range(nslab):
                slabs.append((src_row_fn(kc), dst_ap_fn(kc), colw,
                              src3d_fn(kc) if src3d_fn is not None else None))

        for d in ("f", "b"):
            cast_stream(
                lambda kc, d=d: dram[f"inproj_{d}"][
                    (kc // 2) * 128:(kc // 2 + 1) * 128,
                    (kc % 2) * 2048:(kc % 2 + 1) * 2048],
                lambda kc, d=d: inpbf[d][
                    (kc % 2) * 16:(kc % 2) * 16 + 16, kc // 2, :, :].rearrange(
                    "a p m -> p a m"),
                2 * KCD, 2048,
                src3d_fn=lambda kc: True)
            cast_stream(
                lambda kc, d=d: dram[f"outproj_{d}"][kc * 128:(kc + 1) * 128, :],
                lambda kc, d=d: outbf[d][kc * 128:(kc + 1) * 128, :],
                NB, DM)

        ld_tok = {}
        cast_tok = {}
        st_tok = {}
        for i, (src_ap, dst_ap, colw, src3d) in enumerate(slabs):
            par = i % 2
            if i - 2 in cast_tok:
                wait("scalar", s_act, cast_tok[i - 2])
                wait("scalar", s_stC, st_tok[i - 2])
            lsem = s_ldC if par == 0 else s_ldD
            dma("scalar", stgf[par][:, 0:colw], src_ap, lsem)
            ld_tok[i] = (lsem, lsem.n)
            if i - 1 in ld_tok and i >= 1:
                pass
            if i >= 1:
                j = i - 1
                pv = j % 2
                pcolw, psrc3d, pdst = slabs[j][2], slabs[j][3], slabs[j][1]
                wait("scalar", ld_tok[j][0], ld_tok[j][1])
                run("scalar", lambda e, cw=pcolw, b=stgb[pv], f=stgf[pv]:
                    e.activation(out=b[:, 0:cw], in_=f[:, 0:cw],
                                 func=AF.Copy), inc=(s_act, 1))
                cast_tok[j] = s_act.n
                wait("scalar", s_act, cast_tok[j])
                sap = (stgb[pv][:, 0:pcolw].rearrange(
                    "p (a m) -> p a m", m=128) if psrc3d
                    else stgb[pv][:, 0:pcolw])
                dma("scalar", pdst, sap, s_stC)
                st_tok[j] = s_stC.n
        j = len(slabs) - 1
        pv = j % 2
        pcolw, psrc3d, pdst = slabs[j][2], slabs[j][3], slabs[j][1]
        wait("scalar", ld_tok[j][0], ld_tok[j][1])
        run("scalar", lambda e, cw=pcolw, b=stgb[pv], f=stgf[pv]:
            e.activation(out=b[:, 0:cw], in_=f[:, 0:cw],
                         func=AF.Copy), inc=(s_act, 1))
        wait("scalar", s_act, s_act.n)
        sap = (stgb[pv][:, 0:pcolw].rearrange(
            "p (a m) -> p a m", m=128) if psrc3d
            else stgb[pv][:, 0:pcolw])
        dma("scalar", pdst, sap, s_stC)
        ph0_done = s_stC.n
        del slabs, ld_tok, cast_tok, st_tok

    if True:
        wr_sb = sbt((128, KCD * DM), F32R, ph1)
        wi_sb = sbt((128, KCD * DM), F32R, ph1)
        trt = sbt((128, NMB * DM), F32R, ph1)
        tit = sbt((128, NMB * DM), F32R, ph1)
        fstg = [sbt((128, 512), F32R, ph1) for _ in range(4)]
        mstg = [sbt((128, 512), F32R, ph1) for _ in range(2)]
        ntmp = sbt((128, KCD * DM), F32, ph1)

        wait("sync", s_in)
        dma("sync", wr_sb[:].rearrange("p (a o) -> p a o", o=DM),
            dram["wfr"].rearrange("(a p) o -> p a o", p=128), s_in)
        dma("sync", wi_sb[:].rearrange("p (a o) -> p a o", o=DM),
            dram["wfi"].rearrange("(a p) o -> p a o", p=128), s_in)
        w_ready = s_in.n
        wait("sync", s_in, w_ready)

        fsrc = []
        for _ in range(2):
            for dh in range(2):
                for kc in range(KCD):
                    fsrc.append((dram["fr"][kc * 128:(kc + 1) * 128,
                                            dh * 512:(dh + 1) * 512],
                                 dram["fi"][kc * 128:(kc + 1) * 128,
                                            dh * 512:(dh + 1) * 512]))
        for th in range(2):
            for ob in range(KCD):
                fsrc.append((dram["fr2"][ob * 128:(ob + 1) * 128,
                                         th * 512:(th + 1) * 512],
                             dram["fi2"][ob * 128:(ob + 1) * 128,
                                         th * 512:(th + 1) * 512]))
        fdma_tok = {}
        fte_tok = {}
        fstep = [0]

        def f_issue(i):
            if i >= len(fsrc):
                return
            par = i % 2
            if i - 2 >= 0:
                wait("sync", s_te, fte_tok[i - 2])
            dma("sync", fstg[2 * par][:], fsrc[i][0], s_w4[par])
            dma("sync", fstg[2 * par + 1][:], fsrc[i][1], s_w4[par])
            fdma_tok[i] = s_w4[par].n

        f_issue(0)
        f_issue(1)

        def t_pass(tdst, w0, w1):
            for dh in range(2):
                for kc in range(KCD):
                    i = fstep[0]
                    par = i % 2
                    wait("tensor", s_w4[par], fdma_tok[i])
                    if kc == 0:
                        wait("tensor", s_in, w_ready)
                        for ob in range(8):
                            wait_bank("tensor", ob)
                    for ob in range(KCD):
                        run("tensor", lambda e, b=bank(ob),
                            lh=w0[:, kc * DM + ob * 128:
                                  kc * DM + (ob + 1) * 128],
                            rh=fstg[2 * par][:], st=(kc == 0):
                            e.matmul(b, lh, rh, start=st, stop=False))
                        run("tensor", lambda e, b=bank(ob),
                            lh=w1[:, kc * DM + ob * 128:
                                  kc * DM + (ob + 1) * 128],
                            rh=fstg[2 * par + 1][:], sp=(kc == KCD - 1):
                            e.matmul(b, lh, rh, start=False, stop=sp),
                            inc=(s_te, 1) if ob == KCD - 1 else None)
                    fte_tok[i] = s_te.n
                    f_issue(i + 2)
                    fstep[0] += 1
                wait("vector", s_te)
                for ob in range(KCD):
                    run("vector", lambda e,
                        o=tdst[:, ob * DM + dh * 512:
                               ob * DM + (dh + 1) * 512],
                        i=bank(ob):
                        e.tensor_copy(out=o, in_=i),
                        inc=(s_dve, 1) if ob == KCD - 1 else None)
                for i in range(8):
                    bank_tok[i] = (s_dve, s_dve.n)

        t_pass(tit, wi_sb, wr_sb)
        wait("vector", s_te)
        run("vector", lambda e:
            e.tensor_scalar(out=ntmp[:], in0=wi_sb[:], scalar1=-1.0,
                            scalar2=None, op0=ALU.mult))
        run("vector", lambda e: e.drain())
        run("vector", lambda e:
            e.tensor_copy(out=wi_sb[:], in_=ntmp[:]), inc=(s_dve, 1))
        wi_neg_tok = s_dve.n
        wait("tensor", s_dve, wi_neg_tok)
        t_pass(trt, wr_sb, wi_sb)

        for th in range(2):
            for ob in range(KCD):
                i = fstep[0]
                par = i % 2
                wait("tensor", s_w4[par], fdma_tok[i])
                if ob == 0:
                    for j in range(8):
                        wait_bank("tensor", j)
                for db in range(KCD):
                    run("tensor", lambda e, b=bank(db),
                        lh=trt[:, ob * DM + db * 128: ob * DM + (db + 1) * 128],
                        rh=fstg[2 * par][:], st=(ob == 0):
                        e.matmul(b, lh, rh, start=st, stop=False))
                    run("tensor", lambda e, b=bank(db),
                        lh=tit[:, ob * DM + db * 128: ob * DM + (db + 1) * 128],
                        rh=fstg[2 * par + 1][:], sp=(ob == KCD - 1):
                        e.matmul(b, lh, rh, start=False, stop=sp),
                        inc=(s_te, 1) if db == KCD - 1 else None)
                fte_tok[i] = s_te.n
                f_issue(i + 2)
                fstep[0] += 1
            wait("vector", s_te)
            for db in range(KCD):
                wait("vector", s_st)
                run("vector", lambda e, o=mstg[db % 2][:], i=bank(db):
                    e.tensor_copy(out=o, in_=i), inc=(s_dve, 1))
                wait("sync", s_dve)
                dma("sync",
                    mscr[db * 128:(db + 1) * 128, th * 512:(th + 1) * 512],
                    mstg[db % 2][:], s_st)
            for i in range(8):
                bank_tok[i] = (s_dve, s_dve.n)
        wait("sync", s_st)
        wait("sync", s_stC, ph0_done)
        wait("sync", s_act)
        wait("sync", s_te)
        wait("sync", s_dve)
    ph1.close()

    ph2 = stack.enter_context(contextlib.ExitStack())
    ystg = sbt((128, NMB * 512), BF16, ph2)
    xTc = sbt((128, KCD * LC), BF16, ph2)
    scr2k = sbt((128, 2048), F32, ph2)
    xstg = scr2k[:, 0:DM]
    ubq = sbt((128, NB * LC), BF16, ph2)
    zbuf = sbt((128, NB * LC), BF16, ph2)
    dT = sbt((128, NB * LC), F32, ph2)
    dub = sbt((128, NB * LC), BF16, ph2)
    dtbf = sbt((64, LC), BF16, ph2)
    dblB = sbt((16, LC), BF16, ph2)
    dblC = sbt((16, LC), BF16, ph2)
    inpstg = [sbt((128, KCD * 128), BF16, ph2) for _ in range(3)]
    outstg = [sbt((128, DM), BF16, ph2) for _ in range(2)]

    def emit_xpose(c, xstg, dst):
        for lt in range(4):
            wait("sync", s_te)
            dma("sync", xstg,
                xin[c * LC + lt * 128: c * LC + (lt + 1) * 128, :], s_in)
            wait("tensor", s_in)
            wait("tensor", s_dve)
            wait("tensor", s_act)
            for mb in range(NMB):
                run("tensor", lambda e, o=bank(mb)[:, 0:128],
                    i=xstg[:, mb * 128:(mb + 1) * 128]:
                    e.transpose(o, i, eye_sb[:]),
                    inc=(s_te, 1) if mb == NMB - 1 else None)
            wait("vector", s_te)
            for mb in range(NMB):
                run("vector", lambda e,
                    o=dst[:, mb * LC + lt * 128: mb * LC + lt * 128 + 128],
                    i=bank(mb)[:, 0:128]:
                    e.tensor_copy(out=o, in_=i),
                    inc=(s_dve, 1) if mb == NMB - 1 else None)
        for i in range(8):
            bank_tok[i] = (s_dve, s_dve.n)

    last_q = [0]

    def mamba_dir(d, bufs):
        rev = (d == "b")
        xprojbf, dtwbf_, ucbuf, dscope = bufs
        with contextlib.ExitStack() as wp:
            wstg_ = sbt((128, 2048), F32, wp)
            wait("sync", s_dve)
            wait("sync", s_act)
            dma("sync", wstg_[:, 0:NB * 96].rearrange("p (a j) -> p a j", j=96),
                dram[f"xproj_{d}"].rearrange("(a p) j -> p a j", p=128), s_in)
            wait("vector", s_in)
            run("vector", lambda e:
                e.tensor_copy(out=xprojbf[:], in_=wstg_[:, 0:NB * 96]),
                inc=(s_dve, 1))
            wait("sync", s_dve)
            dma("sync", wstg_[0:64, 0:DI], dram[f"dtw_{d}"][:, :], s_in)
            wait("vector", s_in)
            run("vector", lambda e:
                e.tensor_copy(out=dtwbf_[:], in_=wstg_[0:64, 0:DI]),
                inc=(s_dve, 1))
            run("vector", lambda e: e.memset(carry[:], 0.0), inc=(s_dve, 1))
            run("vector", lambda e: e.memset(stash[0][:], 0.0),
                inc=(s_dve, 1))
            run("vector", lambda e: e.memset(stash[1][:], 0.0),
                inc=(s_dve, 1))
            wait("scalar", s_dve)
            run("scalar", lambda e, i=cols[f"alogc_{d}"][:]:
                e.activation(out=acol[:], in_=i, func=AF.Exp), inc=(s_act, 1))
            wait("vector", s_act)
            run("vector", lambda e:
                e.tensor_scalar(out=acol[:], in0=acol[:], scalar1=-1.0,
                                scalar2=None, op0=ALU.mult), inc=(s_dve, 1))
            wait("sync", s_dve)

        chunks = list(range(NCH - 1, -1, -1)) if rev else list(range(NCH))
        NSL = 32 * NCH
        inp_tok = {}
        inp_te = {}

        def inp_issue(k):
            if k >= NSL:
                return
            par = k % 3
            db = k % 32
            if k - 3 >= 0:
                wait("sync", s_te, inp_te[k - 3])
            dma("sync", inpstg[par][:].rearrange("p (a m) -> p a m", m=128),
                inpbf[d][db, :, :, :].rearrange("a p m -> p a m"),
                s_w4[par])
            inp_tok[k] = s_w4[par].n

        wait("sync", s_te)
        for k in range(3):
            inp_issue(k)

        tok = {"xp_te": 0, "xp_done": 0, "xp_in": 0, "uc_done": {},
               "scr_act": 0, "scr_dve": 0, "q_ready": last_q[0]}

        def front_bundles(ci):
            c = chunks[ci]
            bs = []
            for lt in range(4):
                def bxp(lt=lt, c=c):
                    if lt == 0:
                        wait("sync", s_act, tok["scr_act"])
                        wait("sync", s_dve, tok["scr_dve"])
                    else:
                        wait("sync", s_te, tok["xp_te"])
                    dma("sync", xstg,
                        xin[c * LC + lt * 128: c * LC + (lt + 1) * 128, :],
                        s_in)
                    xin_tok = s_in.n
                    wait("tensor", s_in, xin_tok)
                    for mb in range(NMB):
                        wait_bank("tensor", mb)
                    for mb in range(NMB):
                        run("tensor", lambda e, o=bank(mb)[:, 0:128],
                            i=xstg[:, mb * 128:(mb + 1) * 128]:
                            e.transpose(o, i, eye_sb[:]),
                            inc=(s_te, 1) if mb == NMB - 1 else None)
                    tok["xp_te"] = s_te.n
                    wait("scalar", s_te, tok["xp_te"])
                    for mb in range(NMB):
                        run("scalar", lambda e,
                            o=xTc[:, mb * LC + lt * 128:
                                  mb * LC + lt * 128 + 128],
                            i=bank(mb)[:, 0:128]:
                            e.activation(out=o, in_=i, func=AF.Copy),
                            inc=(s_act, 1) if mb == NMB - 1 else None)
                    for mb in range(NMB):
                        bank_tok[mb] = (s_act, s_act.n)
                    if lt == 3:
                        tok["xp_done"] = s_act.n
                        tok["xp_in"] = xin_tok
                bs.append(bxp)
            for db in range(NB):
                def bu(db=db, ci=ci):
                    k = ci * 32 + db
                    par = k % 3
                    bi = db % 8
                    wait("tensor", s_w4[par], inp_tok[k])
                    if db == 0:
                        wait("tensor", s_act, tok["xp_done"])
                    wait_bank("tensor", bi)
                    for kc in range(KCD):
                        run("tensor", lambda e, b=bank(bi),
                            lh=inpstg[par][:, kc * 128:(kc + 1) * 128],
                            rh=xTc[:, kc * LC:(kc + 1) * LC],
                            st=(kc == 0), sp=(kc == KCD - 1):
                            e.matmul(b, lh, rh, start=st, stop=sp),
                            inc=(s_te, 1) if kc == KCD - 1 else None)
                    inp_te[k] = s_te.n
                    inp_issue(k + 3)
                    wait("scalar", s_te, inp_te[k])
                    if db == 0:
                        wait("scalar", s_dve, tok["q_ready"])
                    run("scalar", lambda e,
                        o=ucbuf[:, db * LC:(db + 1) * LC], i=bank(bi):
                        e.activation(out=o, in_=i, func=AF.Copy),
                        inc=(s_act, 1))
                    bank_tok[bi] = (s_act, s_act.n)
                    if db == NB - 1:
                        tok["uc_done"][ci] = s_act.n
                bs.append(bu)
            return bs

        def z_quartet(ci, g):
            for db in range(NB + 4 * g, NB + 4 * g + 4):
                k = ci * 32 + db
                par = k % 3
                bi = db % 8
                wait("tensor", s_w4[par], inp_tok[k])
                wait_bank("tensor", bi)
                for kc in range(KCD):
                    run("tensor", lambda e, b=bank(bi),
                        lh=inpstg[par][:, kc * 128:(kc + 1) * 128],
                        rh=xTc[:, kc * LC:(kc + 1) * LC],
                        st=(kc == 0), sp=(kc == KCD - 1):
                        e.matmul(b, lh, rh, start=st, stop=sp),
                        inc=(s_te, 1) if kc == KCD - 1 else None)
                inp_te[k] = s_te.n
                inp_issue(k + 3)
                zdb = db - NB
                wait("scalar", s_te, inp_te[k])
                run("scalar", lambda e, o=zbuf[:, zdb * LC:(zdb + 1) * LC],
                    i=bank(bi): e.activation(out=o, in_=i, func=AF.Silu),
                    inc=(s_act, 1))
                bank_tok[bi] = (s_act, s_act.n)

        def emit_conv_z(ci):
            scur = stash[ci % 2]
            snxt = stash[(ci + 1) % 2]
            wcol = cols[f"convw_{d}"]
            bcol = cols[f"convb_{d}"]

            def conv_ops(db, j):
                u_sl = ucbuf[:, db * LC:(db + 1) * LC]
                acc = scr2k[:, j * 512:(j + 1) * 512]
                ops = [lambda e, o=acc, i=u_sl,
                       s1=wcol[:, db * 4 + 3: db * 4 + 4],
                       s2=bcol[:, db: db + 1]:
                       e.tensor_scalar(out=o, in0=i, scalar1=s1,
                                       scalar2=s2, op0=ALU.mult,
                                       op1=ALU.add)]
                for k in (2, 1, 0):
                    m = 3 - k
                    if not rev:
                        ops.append(lambda e, o=acc[:, m:512],
                                   i=u_sl[:, 0:512 - m],
                                   s1=wcol[:, db * 4 + k: db * 4 + k + 1],
                                   a=acc[:, m:512]:
                                   e.scalar_tensor_tensor(
                                       out=o, in0=i, scalar=s1, in1=a,
                                       op0=ALU.mult, op1=ALU.add))
                        ops.append(lambda e, o=acc[:, 0:m],
                                   i=scur[:, db * 4 + 3 - m: db * 4 + 3],
                                   s1=wcol[:, db * 4 + k: db * 4 + k + 1],
                                   a=acc[:, 0:m]:
                                   e.scalar_tensor_tensor(
                                       out=o, in0=i, scalar=s1, in1=a,
                                       op0=ALU.mult, op1=ALU.add))
                    else:
                        ops.append(lambda e, o=acc[:, 0:512 - m],
                                   i=u_sl[:, m:512],
                                   s1=wcol[:, db * 4 + k: db * 4 + k + 1],
                                   a=acc[:, 0:512 - m]:
                                   e.scalar_tensor_tensor(
                                       out=o, in0=i, scalar=s1, in1=a,
                                       op0=ALU.mult, op1=ALU.add))
                        ops.append(lambda e, o=acc[:, 512 - m:512],
                                   i=scur[:, db * 4: db * 4 + m],
                                   s1=wcol[:, db * 4 + k: db * 4 + k + 1],
                                   a=acc[:, 512 - m:512]:
                                   e.scalar_tensor_tensor(
                                       out=o, in0=i, scalar=s1, in1=a,
                                       op0=ALU.mult, op1=ALU.add))
                return ops

            wait("scalar", s_dve)
            wait("vector", s_act, tok["uc_done"][ci])
            wait("vector", s_te, tok["xp_te"])
            wait("vector", s_act, tok["scr_act"])
            wait("vector", s_in, tok["xp_in"])
            if not rev:
                run("vector", lambda e,
                    o=snxt[:].rearrange("p (a k) -> p a k", k=4)[:, :, 0:3],
                    i=ucbuf[:].rearrange("p (a l) -> p a l",
                                         l=LC)[:, :, 509:512]:
                    e.tensor_copy(out=o, in_=i))
            else:
                run("vector", lambda e,
                    o=snxt[:].rearrange("p (a k) -> p a k", k=4)[:, :, 0:3],
                    i=ucbuf[:].rearrange("p (a l) -> p a l",
                                         l=LC)[:, :, 0:3]:
                    e.tensor_copy(out=o, in_=i))
            silu_tok = 0
            silu_toks = []
            for grp in range(4):
                if grp > 0:
                    wait("vector", s_act, silu_tok)
                for j0 in (0, 2):
                    opsA = conv_ops(grp * 4 + j0, j0)
                    opsB = conv_ops(grp * 4 + j0 + 1, j0 + 1)
                    run("vector", opsA[0])
                    run("vector", opsB[0])
                    for lev in range(3):
                        run("vector", lambda e: e.drain())
                        run("vector", opsA[1 + 2 * lev])
                        run("vector", opsA[2 + 2 * lev])
                        run("vector", opsB[1 + 2 * lev])
                        run("vector", opsB[2 + 2 * lev])
                run("vector", lambda e: e.drain(), inc=(s_dve, 1))
                wait("scalar", s_dve)
                run("scalar", lambda e,
                    o=ucbuf[:, grp * 2048:(grp + 1) * 2048], i=scr2k[:]:
                    e.activation(out=o, in_=i, func=AF.Silu), inc=(s_act, 1))
                silu_tok = s_act.n
                silu_toks.append(s_act.n)
                z_quartet(ci, grp)
            return silu_toks

        def xproj_D(ci, silu_toks):
            for i in range(3):
                wait_bank("tensor", i)
            for grp in range(4):
                wait("tensor", s_act, silu_toks[grp])
                for kc in range(grp * 4, grp * 4 + 4):
                    for (j0, w_, dst_part) in ((0, 64, pA[0:64, 0:512]),
                                               (64, 16, pA[0:16, 512:1024]),
                                               (80, 16, pA[0:16, 1024:1536])):
                        run("tensor", lambda e, b=dst_part,
                            lh=xprojbf[:, kc * 96 + j0: kc * 96 + j0 + w_],
                            rh=ucbuf[:, kc * LC:(kc + 1) * LC],
                            st=(kc == 0), sp=(kc == NB - 1):
                            e.matmul(b, lh, rh, start=st, stop=sp),
                            inc=(s_te, 1) if kc == NB - 1 and j0 == 80
                            else None)
            wait("vector", s_te)
            run("vector", lambda e: e.tensor_copy(out=dtbf[:],
                                                  in_=pA[0:64, 0:512]))
            run("vector", lambda e: e.tensor_copy(out=dblB[:],
                                                  in_=pA[0:16, 512:1024]))
            run("vector", lambda e: e.tensor_copy(out=dblC[:],
                                                  in_=pA[0:16, 1024:1536]),
                inc=(s_dve, 1))
            for i in range(3):
                bank_tok[i] = (s_dve, s_dve.n)
            return s_dve.n

        def dtproj_E(ci):
            dw = dtwbf_
            dtb = cols[f"dtb_{d}"]
            wait("tensor", s_dve)
            for grp in range(4):
                bb = 4 - 4 * (grp % 2)
                pp = pB if grp % 2 == 0 else pA
                for j in range(4):
                    db = grp * 4 + j
                    wait_bank("tensor", bb + j)
                    run("tensor", lambda e, b=pp[:, j * 512:(j + 1) * 512],
                        lh=dw[:, db * 128:(db + 1) * 128], rh=dtbf[:]:
                        e.matmul(b, lh, rh, start=True, stop=True),
                        inc=(s_te, 1) if j == 3 else None)
                wait("scalar", s_te)
                wait("scalar", s_dve)
                run("scalar", lambda e: e.drain())
                for j in range(4):
                    db = grp * 4 + j
                    bk = pp[:, j * 512:(j + 1) * 512]
                    run("scalar", lambda e, o=scr2k[:, j * 512:(j + 1) * 512],
                        i=bk, b=dtb[:, db:db + 1]:
                        e.activation(out=o, in_=i, func=AF.Abs, bias=b))
                    run("scalar", lambda e, o=dT[:, db * LC:(db + 1) * LC],
                        i=bk, b=dtb[:, db:db + 1]:
                        e.activation(out=o, in_=i, func=AF.Relu, bias=b),
                        inc=(s_act, 1) if j == 3 else None)
                for j in range(4):
                    bank_tok[bb + j] = (s_act, s_act.n)
                run("scalar", lambda e: e.drain())
                run("scalar", lambda e: e.activation(
                    out=scr2k[:], in_=scr2k[:], func=AF.Exp, scale=-1.0))
                run("scalar", lambda e: e.drain())
                run("scalar", lambda e: e.activation(
                    out=scr2k[:], in_=scr2k[:], func=AF.Ln, bias=1.0),
                    inc=(s_act, 1))
                wait("vector", s_act)
                run("vector", lambda e,
                    o=dT[:, grp * 2048:(grp + 1) * 2048], b=scr2k[:]:
                    e.tensor_tensor(out=o, in0=o, in1=b, op=ALU.add),
                    inc=(s_dve, 1))
            tok["scr_act"] = s_act.n
            tok["scr_dve"] = s_dve.n

        def qinit_F(ci):
            run("vector", lambda e: e.drain())
            dpc = cols[f"dpcol_{d}"]
            for db in range(NB):
                run("vector", lambda e, o=ubq[:, db * LC:(db + 1) * LC],
                    i=ucbuf[:, db * LC:(db + 1) * LC], s1=dpc[:, db:db + 1]:
                    e.tensor_scalar(out=o, in0=i, scalar1=s1, scalar2=None,
                                    op0=ALU.mult))
            run("vector", lambda e: e.tensor_tensor(
                out=dub[:], in0=dT[:], in1=ucbuf[:], op=ALU.mult),
                inc=(s_dve, 1))
            tok["q_ready"] = s_dve.n
            last_q[0] = s_dve.n
            return s_dve.n

        def y_evac(c):
            wait("scalar", s_te)
            wait("scalar", s_st)
            for mb in range(NMB):
                run("scalar", lambda e,
                    o=ystg[:, mb * 512:(mb + 1) * 512],
                    i=bank(mb):
                    e.activation(out=o, in_=i, func=AF.Copy),
                    inc=(s_act, 1) if mb == NMB - 1 else None)
            for i in range(8):
                bank_tok[i] = (s_act, s_act.n)
            wait("sync", s_act)
            dma("sync",
                yscr[d].rearrange("a p l -> p a l")[:, :, c * LC:(c + 1) * LC],
                ystg[:].rearrange("p (a l) -> p a l", l=512), s_st)

        for b in front_bundles(0):
            b()
        prev_c = None
        for ci, c in enumerate(chunks):
            if prev_c is not None:
                y_evac(prev_c)
            silu_toks = emit_conv_z(ci)
            if STOP == "uc":
                dump(ucbuf[:], NB * LC)
                return True
            dbl_tok = xproj_D(ci, silu_toks)
            dtproj_E(ci)
            if STOP == "dT":
                dump(dT[:], NB * LC)
                return True
            q_ready = qinit_F(ci)
            fillers = front_bundles(ci + 1) if ci + 1 < NCH else []
            yield_scan(d, rev, c, q_ready, dbl_tok, fillers)
            if STOP == "hq":
                dump(ubq[:], NB * LC)
                return True
            prev_c = c
        y_evac(prev_c)
        return False

    def yield_scan(d, rev, c, q_ready, dbl_tok, fillers):
        ac = acol
        scan_tok = {0: 0, 1: 0}
        act_tok = {0: 0, 1: 0}
        sm_tok = {0: 0}
        te_tok = {}
        run("scalar", lambda e: e.drain())
        grp_dve_end = {}

        def emit_pats(g):
            gp = g % 2
            if g - 2 in grp_dve_end:
                wait("tensor", s_dve, grp_dve_end[g - 2])
            else:
                wait("tensor", s_dve)
            wait("tensor", s_act)
            for i in range(8):
                wait_bank("tensor", i)
            for j in range(4):
                n = g * 4 + j
                run("tensor", lambda e, b=pA[:, j * 512:(j + 1) * 512],
                    lh=esel_sb[:, n * 128:(n + 1) * 128], rh=dblB[:]:
                    e.matmul(b, lh, rh, start=True, stop=True))
                run("tensor", lambda e, b=pB[:, j * 512:(j + 1) * 512],
                    lh=esel_sb[:, n * 128:(n + 1) * 128], rh=dblC[:]:
                    e.matmul(b, lh, rh, start=True, stop=True),
                    inc=(s_te, 1) if j == 3 else None)
            wait("scalar", s_te)
            run("scalar", lambda e, o=bpat[gp][:]: e.activation(
                out=o, in_=pA[:], func=AF.Copy))
            run("scalar", lambda e, o=cpat[gp][:]: e.activation(
                out=o, in_=pB[:], func=AF.Copy), inc=(s_act, 1))
            for i in range(8):
                bank_tok[i] = (s_act, s_act.n)
            return ("act", s_act.n)

        pat_tok_next = emit_pats(0)
        for g in range(4):
            gp = g % 2
            pat_tok = pat_tok_next
            bpat_g = bpat[gp]
            cpat_g = cpat[gp]
            def emit_A(db):
                h = hg4[db % 2]
                sw = swide[db % 2]
                items = []
                smfn = (lambda e,
                        i0=dub[:, db * LC:(db + 1) * LC]
                        .rearrange("p (o l) -> p o l", o=1)
                        .broadcast_to((128, 4, LC)),
                        i1=bpat_g[:].rearrange("p (o l) -> p o l", l=LC),
                        o=sw[:].rearrange("p (o l) -> p o l", l=LC):
                        e.tensor_tensor(out=o, in0=i0, in1=i1, op=ALU.mult))
                items.append((None, smfn,
                              ("sm", db % 2, pat_tok if db == 0 else 0,
                               q_ready if (g == 0 and db == 0) else 0)))
                for j in range(4):
                    n = g * 4 + j
                    buf = j % 2
                    actfn = (lambda e, o=dab[buf][:],
                             i=dT[:, db * LC:(db + 1) * LC],
                             s=ac[:, db * NST + n: db * NST + n + 1]:
                             e.activation(out=o, in_=i, func=AF.Exp, scale=s))
                    items.append((None, actfn, ("act", buf)))
                    if not rev:
                        scfn = (lambda e, o=h[:, j * 512:(j + 1) * 512],
                                d0=dab[buf][:],
                                d1=sw[:, j * 512:(j + 1) * 512],
                                i0=carry[:, db * NST + n: db * NST + n + 1]:
                                e.tensor_tensor_scan(out=o, data0=d0, data1=d1,
                                                     initial=i0, op0=ALU.mult,
                                                     op1=ALU.add))
                    else:
                        scfn = (lambda e,
                                o=h[:, j * 512:(j + 1) * 512][:, ::-1],
                                d0=dab[buf][:, ::-1],
                                d1=sw[:, j * 512:(j + 1) * 512][:, ::-1],
                                i0=carry[:, db * NST + n: db * NST + n + 1]:
                                e.tensor_tensor_scan(out=o, data0=d0, data1=d1,
                                                     initial=i0, op0=ALU.mult,
                                                     op1=ALU.add))
                    items.append((None, scfn, ("scan", buf, db % 2, j == 0)))
                return items

            def emit_T(db):
                h = hg4[db % 2]
                cstart = 0 if rev else LC - 1
                ops = []
                ops.append((("dve", lambda: max(scan_tok[1],
                                                tloc.get("qa", 0))),
                            lambda e, hh=h, cp=cpat_g: e.tensor_tensor(
                                out=tmp2k[:], in0=hh[:], in1=cp[:],
                                op=ALU.mult), ("tok", "cm")))
                ops.append((None, lambda e,
                            o=carry[:, db * NST + g * 4: db * NST + g * 4 + 4],
                            i=h[:].rearrange("p (a l) -> p a l", l=LC)
                            [:, :, cstart:cstart + 1]
                            .rearrange("p a k -> p (a k)"):
                            e.tensor_copy(out=o, in_=i), None))
                ops.append((("dve", lambda: tloc["cm"]),
                            lambda e: e.tensor_tensor(
                                out=tmp2k[:, 0:1024], in0=tmp2k[:, 0:1024],
                                in1=tmp2k[:, 1024:2048], op=ALU.add),
                            ("tok", "a1")))
                ops.append((("dve", lambda: tloc["a1"]),
                            lambda e: e.tensor_tensor(
                                out=tmp2k[:, 0:512], in0=tmp2k[:, 0:512],
                                in1=tmp2k[:, 512:1024], op=ALU.add),
                            ("tok", "a2")))
                ops.append((("dve", lambda: tloc["a2"]),
                            lambda e, o=ubq[:, db * LC:(db + 1) * LC]:
                            e.tensor_tensor(out=o, in0=o, in1=tmp2k[:, 0:512],
                                            op=ALU.add), ("tok", "qa")))
                if g == 3:
                    par = db % 2
                    ops.append((("te2", db - 2 if db >= 2 else None,
                                 "dve", lambda: tloc["qa"]),
                                lambda e, o=y2b[par][:],
                                i0=ubq[:, db * LC:(db + 1) * LC],
                                i1=zbuf[:, db * LC:(db + 1) * LC]:
                                e.tensor_tensor(out=o, in0=i0, in1=i1,
                                                op=ALU.mult), ("y2", db)))
                return ops

            tloc = {}

            def flush(items):
                for item in items:
                    pre, fn, tag = item
                    if tag is not None and tag[0] == "act":
                        buf = tag[1]
                        wait("scalar", s_dve, scan_tok[buf])
                        run("scalar", fn, inc=(s_act, 1))
                        act_tok[buf] = s_act.n
                        continue
                    if tag is not None and tag[0] == "sm":
                        par, ptok, qtok = tag[1], tag[2], tag[3]
                        if ptok:
                            wait("vector", s_act, ptok[1])
                        run("vector", fn, inc=(s_dve, 1))
                        sm_tok[0] = s_dve.n
                        continue
                    if pre is not None:
                        if pre[0] == "dve":
                            wait("vector", s_dve, pre[1]())
                        elif pre[0] == "te2":
                            if pre[1] is not None:
                                wait("vector", s_te, te_tok[pre[1]])
                            wait("vector", s_dve, pre[3]())
                    if tag is not None and tag[0] == "scan":
                        buf, par, first = tag[1], tag[2], tag[3]
                        wait("vector", s_act, act_tok[buf])
                        if first:
                            wait("vector", s_dve, sm_tok[0])
                        run("vector", fn, inc=(s_dve, 1))
                        scan_tok[buf] = s_dve.n
                    elif tag is not None and tag[0] == "tok":
                        run("vector", fn, inc=(s_dve, 1))
                        tloc[tag[1]] = s_dve.n
                    elif tag is not None and tag[0] == "y2":
                        run("vector", fn, inc=(s_dve, 1))
                        emit_outproj(tag[1], s_dve.n)
                    else:
                        run("vector", fn)

            def interleave(a_items, t_items):
                out = []
                na, nt = len(a_items), len(t_items)
                ai = ti = 0
                while ai < na or ti < nt:
                    if ai < na:
                        out.append(a_items[ai]); ai += 1
                        if ai < na:
                            out.append(a_items[ai]); ai += 1
                    if ti < nt:
                        out.append(t_items[ti]); ti += 1
                return out

            def emit_outproj(db, y2_tok):
                par = db % 2
                psem = s_wsA if par == 0 else s_wsB
                if db == 0:
                    wait("sync", s_te)
                    dma("sync", outstg[0][:], outbf[d][0:128, :], s_wsA)
                    for i in range(8):
                        wait_bank("tensor", i)
                wait("tensor", psem)
                wait("tensor", s_dve, y2_tok)
                for mb in range(NMB):
                    run("tensor", lambda e, b=bank(mb),
                        lh=outstg[par][:, mb * 128:(mb + 1) * 128],
                        rh=y2b[par][:], st=(db == 0), sp=(db == NB - 1):
                        e.matmul(b, lh, rh, start=st, stop=sp),
                        inc=(s_te, 1) if mb == NMB - 1 else None)
                te_tok[db] = s_te.n
                if db < NB - 1:
                    if db >= 1:
                        wait("sync", s_te, te_tok[db - 1])
                    dma("sync", outstg[(db + 1) % 2][:],
                        outbf[d][(db + 1) * 128:(db + 2) * 128, :],
                        s_wsB if (db + 1) % 2 else s_wsA)

            prevT = None
            for db in range(NB):
                a_items = emit_A(db)
                if prevT is None:
                    flush(a_items)
                else:
                    flush(interleave(a_items, prevT))
                prevT = emit_T(db)
                if db == 1 and g < 3:
                    pat_tok_next = emit_pats(g + 1)
                if g < 3 and fillers:
                    fillers.pop(0)()
            flush(prevT)
            grp_dve_end[g] = s_dve.n

    bpat = [sbt((128, 4 * LC), BF16, ph2) for _ in range(2)]
    cpat = [sbt((128, 4 * LC), BF16, ph2) for _ in range(2)]
    hg4 = [sbt((128, 4 * LC), BF16, ph2) for _ in range(2)]
    dab = [sbt((128, LC), F32, ph2) for _ in range(2)]
    swide = [sbt((128, 4 * LC), BF16, ph2) for _ in range(2)]
    tmp2k = sbt((128, 2048), BF16, ph2)
    y2b = [sbt((128, LC), BF16, ph2) for _ in range(2)]

    stopped = False
    for d in ("f", "b"):
        with contextlib.ExitStack() as dscope:
            xprojbf_ = sbt((128, NB * 96), BF16, dscope)
            dtwbf_ = sbt((64, NB * 128), BF16, dscope)
            ucbuf = sbt((128, NB * LC), BF16, dscope)
            stopped = mamba_dir(d, (xprojbf_, dtwbf_, ucbuf, dscope))
            if stopped:
                return nc, P, stack
        if STOP == f"y_{d}":
            dump(ystg[:, 0:4096], 4096)
            return nc, P, stack
    ph2.close()

    ph3 = stack.enter_context(contextlib.ExitStack())
    ystg3 = {d: sbt((128, NMB * 512), BF16, ph3) for d in ("f", "b")}
    xstg3 = sbt((128, DM), F32, ph3)
    gbuf = sbt((128, NMB * LC), BF16, ph3)
    comb = sbt((128, NMB * LC), BF16, ph3)
    tbuf = sbt((128, NMB * LC), BF16, ph3)
    xm = sbt((128, NMB * LC), F32, ph3)
    x1 = sbt((128, NMB * LC), F32, ph3)
    x1r = sbt((128, NMB * LC), F32R, ph3)
    owide = sbt((128, NMB * LC), F32, ph3)
    mstg3 = [sbt((128, DM), F32R, ph3) for _ in range(2)]
    xTf = sbt((128, NMB * LC), F32, ph3)
    r1 = sbt((128, NMB * LC), F32, ph3)
    rows = sbt((1, 1024), F32, ph3)
    ostg = sbt((128, DM), F32, ph3)
    wstg = [sbt((128, DM), BF16, ph3) for _ in range(3)]
    wstgf = [sbt((128, DM), F32, ph3) for _ in range(3)]
    wsem = [s_w4[0], s_w4[1], s_ldC]

    def ln_block(src, xm, sqw, dst, src_toks=None):
        if src_toks is None:
            wait("tensor", s_dve)
        wait_bank("tensor", 0)
        for mb in range(NMB):
            if src_toks is not None:
                wait("tensor", s_dve, src_toks[mb])
            run("tensor", lambda e, b=pA[0:1, 0:512],
                lh=onec_sb[:], rh=src[:, mb * LC:(mb + 1) * LC],
                st=(mb == 0), sp=(mb == NMB - 1):
                e.matmul(b, lh, rh, start=st, stop=sp),
                inc=(s_te, 1) if mb == NMB - 1 else None)
        wait("vector", s_te)
        run("vector", lambda e: e.tensor_scalar(
            out=rows[0:1, 0:512], in0=pA[0:1, 0:512], scalar1=1.0 / DM,
            scalar2=None, op0=ALU.mult), inc=(s_dve, 1))
        bank_tok[0] = (s_dve, s_dve.n)
        wait("tensor", s_dve)
        wait_bank("tensor", 4)
        run("tensor", lambda e: e.matmul(pB[:, 0:512], oner_sb[:],
                                         rows[0:1, 0:512], start=True,
                                         stop=True), inc=(s_te, 1))
        wait("vector", s_te)
        for mb in range(NMB):
            run("vector", lambda e, o=xm[:, mb * LC:(mb + 1) * LC],
                a=src[:, mb * LC:(mb + 1) * LC], b=pB[:, 0:512]:
                e.tensor_tensor(out=o, in0=a, in1=b, op=ALU.subtract))
        run("vector", lambda e: e.drain())
        for mb in range(NMB):
            run("vector", lambda e, o=sqw[:, mb * LC:(mb + 1) * LC],
                a=xm[:, mb * LC:(mb + 1) * LC]:
                e.tensor_tensor(out=o, in0=a, in1=a, op=ALU.mult),
                inc=(s_dve, 1) if mb == NMB - 1 else None)
        bank_tok[4] = (s_dve, s_dve.n)
        wait("tensor", s_dve)
        wait_bank("tensor", 1)
        for mb in range(NMB):
            run("tensor", lambda e, b=pA[0:1, 512:1024],
                lh=onec_sb[:], rh=sqw[:, mb * LC:(mb + 1) * LC],
                st=(mb == 0), sp=(mb == NMB - 1):
                e.matmul(b, lh, rh, start=st, stop=sp),
                inc=(s_te, 1) if mb == NMB - 1 else None)
        wait("vector", s_te)
        run("vector", lambda e: e.tensor_scalar(
            out=rows[0:1, 512:1024], in0=pA[0:1, 512:1024], scalar1=1.0 / DM,
            scalar2=None, op0=ALU.mult), inc=(s_dve, 1))
        bank_tok[1] = (s_dve, s_dve.n)
        wait("scalar", s_dve)
        run("scalar", lambda e: e.activation(out=rows[0:1, 512:1024],
                                             in_=rows[0:1, 512:1024],
                                             func=AF.Ln, bias=epsc[0:1, 0:1]))
        run("scalar", lambda e: e.drain())
        run("scalar", lambda e: e.activation(out=rows[0:1, 512:1024],
                                             in_=rows[0:1, 512:1024],
                                             func=AF.Exp, scale=-0.5),
            inc=(s_act, 1))
        wait("tensor", s_act)
        wait_bank("tensor", 5)
        run("tensor", lambda e: e.matmul(pB[:, 512:1024], oner_sb[:],
                                         rows[0:1, 512:1024], start=True,
                                         stop=True), inc=(s_te, 1))
        wait("vector", s_te)
        dst_toks = []
        for mb in range(NMB):
            run("vector", lambda e, o=dst[:, mb * LC:(mb + 1) * LC],
                a=xm[:, mb * LC:(mb + 1) * LC], b=pB[:, 512:1024]:
                e.tensor_tensor(out=o, in0=a, in1=b, op=ALU.mult),
                inc=(s_dve, 1))
            dst_toks.append(s_dve.n)
        bank_tok[5] = (s_dve, s_dve.n)
        return dst_toks

    wait("sync", s_st)
    def issue_ystg(c, dve_tok, te_tok):
        if dve_tok:
            wait("sync", s_dve, dve_tok)
        if te_tok:
            wait("sync", s_te, te_tok)
        for d in ("f", "b"):
            dma("sync", ystg3[d][:].rearrange("p (a l) -> p a l", l=512),
                yscr[d].rearrange("a p l -> p a l")[:, :, c * LC:(c + 1) * LC],
                s_ldD)
        return s_ldD.n

    ystg_tok = {}
    wait("sync", s_te)
    wait("sync", s_dve)
    ystg_tok[0] = issue_ystg(0, 0, 0)
    for c in range(NCH):
        wait("sync", s_te)
        wait("sync", s_dve)
        emit_xpose(c, xstg3[:], xTf)
        if True:
            wdma_tok = {}
            wcast_tok = {}
            wte_tok = {}

            def w_dma(kc, src_d):
                par = kc % 3
                if kc - 3 in wcast_tok:
                    wait("sync", s_act, wcast_tok[kc - 3])
                dma("sync", wstgf[par][:],
                    src_d[kc * 128:(kc + 1) * 128, :], wsem[par])
                wdma_tok[kc] = wsem[par].n

            def w_cast(kc):
                par = kc % 3
                wait("scalar", wsem[par], wdma_tok[kc])
                if kc - 3 in wte_tok:
                    wait("scalar", s_te, wte_tok[kc - 3])
                run("scalar", lambda e, o=wstg[par][:], i=wstgf[par][:]:
                    e.activation(out=o, in_=i, func=AF.Copy),
                    inc=(s_act, 1))
                wcast_tok[kc] = s_act.n

            wait("sync", s_te)
            for kc in range(3):
                w_dma(kc, dram["gate_w"])
                w_cast(kc)
            for kc in range(16):
                par = kc % 3
                wait("tensor", s_act, wcast_tok[kc])
                if kc == 0:
                    for i in range(8):
                        wait_bank("tensor", i)
                    wait("tensor", s_ldD, ystg_tok[c])
                ysrc = ystg3["f"] if kc < 8 else ystg3["b"]
                ycol = (kc % 8) * 512
                for mb in range(NMB):
                    run("tensor", lambda e, b=bank(mb),
                        lh=wstg[par][:, mb * 128:(mb + 1) * 128],
                        rh=ysrc[:, ycol:ycol + LC],
                        st=(kc == 0), sp=(kc == 15):
                        e.matmul(b, lh, rh, start=st, stop=sp),
                        inc=(s_te, 1) if mb == NMB - 1 else None)
                wte_tok[kc] = s_te.n
                if kc + 3 < 16:
                    w_dma(kc + 3, dram["gate_w"])
                    w_cast(kc + 3)
            gate_fin = wte_tok[15]
            wait("scalar", s_te)
            for mb in range(NMB):
                run("scalar", lambda e, o=gbuf[:, mb * LC:(mb + 1) * LC],
                    i=bank(mb), b=cols["gate_bc"][:, mb:mb + 1]:
                    e.activation(out=o, in_=i, func=AF.Sigmoid, bias=b),
                    inc=(s_act, 1) if mb == NMB - 1 else None)
            for i in range(8):
                bank_tok[i] = (s_act, s_act.n)
            wait("vector", s_act)
            for mb in range(NMB):
                yf = ystg3["f"][:, mb * 512:(mb + 1) * 512]
                yb = ystg3["b"][:, mb * 512:(mb + 1) * 512]
                run("vector", lambda e, a=yf, b=yb,
                    o=tbuf[:, mb * LC:(mb + 1) * LC]:
                    e.tensor_tensor(out=o, in0=a, in1=b, op=ALU.subtract))
            run("vector", lambda e: e.drain())
            for mb in range(NMB):
                run("vector", lambda e, g=gbuf[:, mb * LC:(mb + 1) * LC],
                    o=tbuf[:, mb * LC:(mb + 1) * LC]:
                    e.tensor_tensor(out=o, in0=g, in1=o, op=ALU.mult))
            run("vector", lambda e: e.drain())
            comb_tok = []
            for mb in range(NMB):
                yb = ystg3["b"][:, mb * 512:(mb + 1) * 512]
                run("vector", lambda e, o=comb[:, mb * LC:(mb + 1) * LC],
                    b=yb, t=tbuf[:, mb * LC:(mb + 1) * LC]:
                    e.tensor_tensor(out=o, in0=b, in1=t, op=ALU.add),
                    inc=(s_dve, 1))
                comb_tok.append(s_dve.n)
            for mb in range(NMB):
                run("vector", lambda e, o=r1[:, mb * LC:(mb + 1) * LC],
                    i=xTf[:, mb * LC:(mb + 1) * LC],
                    s1=cols["proj_bc"][:, mb:mb + 1]:
                    e.tensor_scalar(out=o, in0=i, scalar1=s1, scalar2=None,
                                    op0=ALU.add))
            run("vector", lambda e: e.drain(), inc=(s_dve, 1))
            r1h_tok = s_dve.n
            wdma_tok = {}
            wcast_tok = {}
            wte_tok = {}
            wait("sync", s_te)
            for kc in range(3):
                w_dma(kc, dram["proj_w"])
                w_cast(kc)
            for kc in range(8):
                par = kc % 3
                wait("tensor", s_act, wcast_tok[kc])
                wait("tensor", s_dve, comb_tok[kc])
                if kc == 0:
                    for i in range(8):
                        wait_bank("tensor", i)
                for mb in range(NMB):
                    run("tensor", lambda e, b=bank(mb),
                        lh=wstg[par][:, mb * 128:(mb + 1) * 128],
                        rh=comb[:, kc * LC:(kc + 1) * LC],
                        st=(kc == 0), sp=(kc == 7):
                        e.matmul(b, lh, rh, start=st, stop=sp),
                        inc=(s_te, 1) if mb == NMB - 1 else None)
                wte_tok[kc] = s_te.n
                if kc + 3 < 8:
                    w_dma(kc + 3, dram["proj_w"])
                    w_cast(kc + 3)
            proj_fin = wte_tok[7]
            wait("vector", s_te, proj_fin)
            r1_tok = []
            for mb in range(NMB):
                run("vector", lambda e, o=r1[:, mb * LC:(mb + 1) * LC],
                    b=bank(mb):
                    e.tensor_tensor(out=o, in0=o, in1=b, op=ALU.add),
                    inc=(s_dve, 1))
                r1_tok.append(s_dve.n)
            for i in range(8):
                bank_tok[i] = (s_dve, s_dve.n)
        if STOP == "r1":
            dump(r1[:], NMB * LC)
            return nc, P, stack
        if True:
            x1_toks = ln_block(r1[:], xm[:], owide[:], x1[:],
                               src_toks=r1_tok)
            x1r_tok = []
            for mb in range(NMB):
                wait("vector", s_dve, x1_toks[mb])
                run("vector", lambda e,
                    o=x1r[:, mb * LC:(mb + 1) * LC],
                    i=x1[:, mb * LC:(mb + 1) * LC]:
                    e.tensor_copy(out=o, in_=i), inc=(s_dve, 1))
                x1r_tok.append(s_dve.n)
            def m_dma(kc, tokd):
                par = kc % 2
                dma("sync", mstg3[par][:],
                    mscr[kc * 128:(kc + 1) * 128, :], s_w4[2 + par])
                tokd[kc] = s_w4[2 + par].n

            wdma_tok = {}
            wte_tok = {}
            wait("sync", s_te)
            m_dma(0, wdma_tok)
            m_dma(1, wdma_tok)
            for kc in range(8):
                par = kc % 2
                wait("tensor", s_w4[2 + par], wdma_tok[kc])
                wait("tensor", s_dve, x1r_tok[kc])
                if kc == 0:
                    for i in range(8):
                        wait_bank("tensor", i)
                for mb in range(NMB):
                    run("tensor", lambda e, b=bank(mb),
                        lh=mstg3[par][:, mb * 128:(mb + 1) * 128],
                        rh=x1r[:, kc * LC:(kc + 1) * LC],
                        st=(kc == 0), sp=(kc == 7):
                        e.matmul(b, lh, rh, start=st, stop=sp),
                        inc=(s_te, 1) if mb == NMB - 1 else None)
                wte_tok[kc] = s_te.n
                if kc + 2 < 8:
                    wait("sync", s_te, wte_tok[kc])
                    m_dma(kc + 2, wdma_tok)
            ffn_fin = wte_tok[7]
            if c + 1 < NCH:
                ystg_tok[c + 1] = issue_ystg(c + 1, comb_tok[-1], gate_fin)
            for mb in range(NMB):
                run("vector", lambda e, o=r1[:, mb * LC:(mb + 1) * LC],
                    i=x1[:, mb * LC:(mb + 1) * LC],
                    s1=cols["fbias_c"][:, mb:mb + 1]:
                    e.tensor_scalar(out=o, in0=i, scalar1=s1, scalar2=None,
                                    op0=ALU.add))
            run("vector", lambda e: e.drain())
            wait("vector", s_te, ffn_fin)
            r2_tok = []
            for mb in range(NMB):
                run("vector", lambda e, o=r1[:, mb * LC:(mb + 1) * LC],
                    b=bank(mb):
                    e.tensor_tensor(out=o, in0=o, in1=b, op=ALU.add),
                    inc=(s_dve, 1))
                r2_tok.append(s_dve.n)
            for i in range(8):
                bank_tok[i] = (s_dve, s_dve.n)
            ow_toks = ln_block(r1[:], xm[:], x1[:], owide[:],
                               src_toks=r2_tok)
            wait("tensor", s_dve, ow_toks[-1])
            for lt in range(4):
                if lt > 0:
                    wait("tensor", s_dve)
                for i in range(8):
                    wait_bank("tensor", i)
                for mb in range(NMB):
                    run("tensor", lambda e, o=bank(mb)[:, 0:128],
                        i=owide[:, mb * LC + lt * 128:
                                mb * LC + lt * 128 + 128]:
                        e.transpose(o, i, eye_sb[:]),
                        inc=(s_te, 1) if mb == NMB - 1 else None)
                wait("vector", s_te)
                wait("vector", s_st)
                for mb in range(NMB):
                    run("vector", lambda e,
                        o=ostg[:, mb * 128:(mb + 1) * 128],
                        i=bank(mb)[:, 0:128]:
                        e.tensor_copy(out=o, in_=i),
                        inc=(s_dve, 1) if mb == NMB - 1 else None)
                for i in range(8):
                    bank_tok[i] = (s_dve, s_dve.n)
                wait("sync", s_dve)
                dma("sync",
                    out_d[c * LC + lt * 128: c * LC + (lt + 1) * 128, :],
                    ostg[:], s_st)
    ph3.close()

    P.add("sync", lambda e, h=s_st.h, v=s_st.n: e.wait_ge(h, v))
    return nc, P, stack


def _emit(nc, P):
    with nc.Block() as block:
        for eng in ENGS:
            ops = P.ops[eng]
            if not ops:
                continue

            def body(e, ops=ops):
                for fn in ops:
                    fn(e)
            getattr(block, eng)(body)


_CACHE = {}


def _build():
    if "nc" not in _CACHE:
        nc, P, stack = build_nc()
        _emit(nc, P)
        stack.close()
        _CACHE["nc"] = nc
    return _CACHE["nc"]


def _host_inputs(inputs):
    f32 = np.float32
    import ml_dtypes

    def colpack(v, width):
        return np.ascontiguousarray(np.asarray(v, f32).reshape(width, 128).T)

    D = DM
    jk = np.outer(np.arange(D), np.arange(D)) % D
    ang = (2.0 * np.pi / D) * jk
    fr = np.cos(ang).astype(f32)
    fi = (-np.sin(ang)).astype(f32)

    esel = np.zeros((16, 16 * 128), dtype=np.float32)
    for n in range(16):
        esel[n, n * 128:(n + 1) * 128] = 1.0
    esel = esel.astype(ml_dtypes.bfloat16)
    base = {
        "gate_w": np.ascontiguousarray(np.asarray(inputs["L0_attn_gate_w"], f32)),
        "gate_bc": colpack(inputs["L0_attn_gate_b"], NMB),
        "proj_w": np.ascontiguousarray(np.asarray(inputs["L0_attn_proj_w"], f32)),
        "proj_bc": colpack(inputs["L0_attn_proj_b"], NMB),
        "wfr": np.ascontiguousarray(np.asarray(inputs["L0_ffn_w_freq_real"], f32)),
        "wfi": np.ascontiguousarray(np.asarray(inputs["L0_ffn_w_freq_imag"], f32)),
        "fbias_c": colpack(inputs["L0_ffn_bias"], NMB),
        "fr": fr, "fi": fi,
        "fr2": (fr / 1024.0).astype(f32), "fi2": (fi / 1024.0).astype(f32),
        "esel": esel,
        "eye": np.eye(128, dtype=f32),
        "ones_col": np.ones((128, 1), f32),
        "ones_row": np.ones((1, 128), f32),
    }
    for d in ("f", "b"):
        base[f"inproj_{d}"] = np.ascontiguousarray(
            np.asarray(inputs[f"L0_attn_inproj_{d}"], f32))
        base[f"xproj_{d}"] = np.ascontiguousarray(
            np.asarray(inputs[f"L0_attn_xproj_{d}"], f32))
        base[f"dtw_{d}"] = np.ascontiguousarray(
            np.asarray(inputs[f"L0_attn_dt_w_{d}"], f32))
        base[f"outproj_{d}"] = np.ascontiguousarray(
            np.asarray(inputs[f"L0_attn_outproj_{d}"], f32))
        cw = np.asarray(inputs[f"L0_attn_conv_w_{d}"], f32)
        base[f"convw_{d}"] = np.ascontiguousarray(
            cw.reshape(NB, 128, 4).transpose(1, 0, 2).reshape(128, NB * 4))
        base[f"convb_{d}"] = colpack(inputs[f"L0_attn_conv_b_{d}"], NB)
        base[f"dtb_{d}"] = colpack(inputs[f"L0_attn_dt_b_{d}"], NB)
        base[f"dpcol_{d}"] = colpack(inputs[f"L0_attn_Dp_{d}"], NB)
        al = np.asarray(inputs[f"L0_attn_Alog_{d}"], f32)
        base[f"alogc_{d}"] = np.ascontiguousarray(
            al.reshape(NB, 128, NST).transpose(1, 0, 2).reshape(128, NB * NST))
    x = np.asarray(inputs["L0_attn_x"], f32)
    in_maps = []
    for b in range(8):
        m = dict(base)
        m["xin"] = np.ascontiguousarray(x[b])
        in_maps.append(m)
    return in_maps


def kernel(**inputs):
    nc = _build()
    in_maps = _host_inputs(inputs)
    res = run_bass_kernel_spmd(nc, in_maps, core_ids=list(range(8)))
    return np.stack([np.asarray(r["out"], dtype=np.float32)
                     for r in res.results], axis=0)
